# version 02
# baseline (speedup 1.0000x reference)
import math
import numpy as np
import ml_dtypes
import concourse.bass as bass
import concourse.mybir as mybir
from concourse.bass_utils import run_bass_kernel_spmd

F32 = mybir.dt.float32
F32R = mybir.dt.float32r
BF16 = mybir.dt.bfloat16
I32 = mybir.dt.int32
U8 = mybir.dt.uint8
AF = mybir.ActivationFunctionType
AX = mybir.AxisListType
ALU = mybir.AluOpType

NCORE = 8
S = 4096
D = 1024
DC = D // 128
H = 16
HD = 64
TC = 512
NT = S // TC
MOE_I = 4096
NEXP = 9
HH = 2048
FF1 = 3072
KVL = 128
EXP_SHIFT = 3.0

ENGINES = ("sync", "scalar", "vector", "tensor", "gpsimd")


def _tile_vec(v, p=128):
    n = v.shape[0]
    return np.ascontiguousarray(v.reshape(n // p, p).T)


def _host_constants():
    half = HD // 2
    inv_freq = (1.0 / (10000.0 ** (np.arange(half, dtype=np.float32) / half)))
    t = np.arange(S, dtype=np.float32)
    freqs = t[:, None] * inv_freq[None, :]
    cosT = np.cos(freqs).T.astype(np.float32)
    sinT = np.sin(freqs).T.astype(np.float32)
    CC4 = np.ascontiguousarray(np.concatenate([cosT] * 4, axis=0))
    SS4 = np.ascontiguousarray(
        np.concatenate([-sinT, sinT, -sinT, sinT], axis=0))
    i = np.arange(128)[:, None]
    j = np.arange(TC)[None, :]
    masks = np.stack([(i <= j - 128 * r).astype(np.float32) for r in range(4)])
    ident32 = np.eye(128, dtype=np.float32)
    ident16 = np.eye(128, dtype=ml_dtypes.bfloat16)
    return {
        "cc4": CC4, "ss4": SS4, "masks": np.ascontiguousarray(masks),
        "ident32": ident32, "ident16": ident16,
    }


def _host_weights(inp):
    f = np.float32
    w = {}
    w["wqkvT"] = np.ascontiguousarray(inp["L0_attn_w_qkv"].T)
    w["woutT"] = np.ascontiguousarray(inp["L0_attn_w_out"].T)
    w["wr1"] = np.ascontiguousarray(inp["L0_ffn_Wr1"])
    w["br1t"] = _tile_vec(inp["L0_ffn_br1"])
    w["wr2"] = np.ascontiguousarray(inp["L0_ffn_Wr2"])
    w["br2"] = np.ascontiguousarray(inp["L0_ffn_br2"].reshape(8, 1))
    w["ws1"] = np.ascontiguousarray(inp["L0_ffn_Ws1"])
    w["bs1t"] = _tile_vec(inp["L0_ffn_bs1"])
    w["ws2"] = np.ascontiguousarray(inp["L0_ffn_Ws2"])
    w["bs2t"] = _tile_vec(inp["L0_ffn_bs2"])
    w["we1"] = np.ascontiguousarray(inp["L0_ffn_We1"])
    w["be1t"] = np.ascontiguousarray(
        np.stack([_tile_vec(inp["L0_ffn_be1"][e]) for e in range(8)]))
    w["we2"] = np.ascontiguousarray(inp["L0_ffn_We2"])
    w["be2t"] = np.ascontiguousarray(
        np.stack([_tile_vec(inp["L0_ffn_be2"][e]) for e in range(8)]))
    w["wdqT"] = np.ascontiguousarray(inp["L1_attn_W_dq"].T)
    w["wdq"] = np.ascontiguousarray(inp["L1_attn_W_dq"])
    w["wuq"] = np.ascontiguousarray(inp["L1_attn_W_uq"])
    w["wuqr"] = np.ascontiguousarray(
        inp["L1_attn_W_uq"].reshape(16, 64, 128).transpose(1, 0, 2))
    w["wdkvT"] = np.ascontiguousarray(inp["L1_attn_W_dkv"].T)
    w["wuk"] = np.ascontiguousarray(inp["L1_attn_W_uk"])
    w["wuv"] = np.ascontiguousarray(inp["L1_attn_W_uv"])
    w["woT"] = np.ascontiguousarray(inp["L1_attn_W_o"].T)
    w["wgT"] = np.ascontiguousarray(inp["L1_ffn_w_gate"].T)
    w["wuT"] = np.ascontiguousarray(inp["L1_ffn_w_up"].T)
    w["wdT"] = np.ascontiguousarray(inp["L1_ffn_w_down"].T)
    return w


def _host_x(inp, b):
    xb = inp["L0_attn_x"][b]
    return np.ascontiguousarray(xb.T.reshape(DC, 128, S).transpose(1, 0, 2))


INTER = {
    "xn0": ([128, DC, S], BF16),
    "xn0f": ([128, DC, S], F32),
    "ctx0": ([128, DC, S], BF16),
    "s0": ([128, DC, S], BF16),
    "w9": ([NEXP, S], F32),
    "acc": ([128, DC, S], F32),
    "x1": ([128, DC, S], F32),
    "xn1": ([128, DC, S], BF16),
    "s1": ([128, DC, S], BF16),
}

WEIGHT_SPECS = {
    "wqkvT": ([D, 3 * D], F32), "woutT": ([D, D], F32),
    "wr1": ([D, 256], F32), "br1t": ([128, 2], F32),
    "wr2": ([256, 8], F32), "br2": ([8, 1], F32),
    "ws1": ([D, MOE_I], F32), "bs1t": ([128, 32], F32),
    "ws2": ([MOE_I, D], F32), "bs2t": ([128, 8], F32),
    "we1": ([8, D, MOE_I], F32), "be1t": ([8, 128, 32], F32),
    "we2": ([8, MOE_I, D], F32), "be2t": ([8, 128, 8], F32),
    "wdqT": ([D, KVL], F32), "wdq": ([KVL, D], F32),
    "wuq": ([D, KVL], F32), "wuqr": ([64, 16, KVL], F32),
    "wdkvT": ([D, KVL], F32),
    "wuk": ([D, KVL], F32), "wuv": ([D, KVL], F32), "woT": ([D, D], F32),
    "wgT": ([D, FF1], I32), "wuT": ([D, FF1], I32), "wdT": ([FF1, D], I32),
    "cc4": ([128, S], F32), "ss4": ([128, S], F32),
    "masks": ([4, 128, TC], F32),
    "ident32": ([128, 128], F32), "ident16": ([128, 128], BF16),
}

PHASES = {
    "ln0": ([], ["xn0", "xn0f"]),
    "attn0": (["xn0"], ["ctx0", "s0"]),
    "router": (["xn0f"], ["w9"]),
    "moe": (["xn0", "w9"], ["acc"]),
    "comb0": (["s0", "acc"], ["x1"]),
    "ln1": (["x1"], ["xn1"]),
    "attn1": (["xn1"], ["s1"]),
    "ffn1": (["xn1", "x1", "s1"], []),
}
PHASE_ORDER = ["ln0", "attn0", "router", "moe", "comb0", "ln1", "attn1", "ffn1"]


class CSem:
    def __init__(self, nc, name):
        self.h = nc.alloc_semaphore(name)
        self.n = 0


class Prog:
    def __init__(self, nc):
        self.nc = nc
        self.q = {e: [] for e in ENGINES}
        self.sems = {}

    def sem(self, name):
        if name not in self.sems:
            self.sems[name] = CSem(self.nc, name)
        return self.sems[name]

    def emit(self, eng, fn):
        self.q[eng].append(fn)

    def op(self, eng, call, waits=(), inc=None, inc_k=1):
        tgt = None
        if inc is not None:
            inc.n += inc_k
            tgt = inc.n
        wl = [(cs.h, v) for (cs, v) in waits]
        if inc is not None:
            ih, iv = inc.h, inc_k

            def fn(e, call=call, wl=wl, ih=ih, iv=iv):
                for h, v in wl:
                    e.wait_ge(h, v)
                call(e).then_inc(ih, iv)
        else:
            def fn(e, call=call, wl=wl):
                for h, v in wl:
                    e.wait_ge(h, v)
                call(e)
        self.emit(eng, fn)
        return tgt

    def wait(self, eng, waits):
        wl = [(cs.h, v) for (cs, v) in waits]

        def fn(e, wl=wl):
            for h, v in wl:
                e.wait_ge(h, v)
        self.emit(eng, fn)

    def barrier(self):
        bar = self.sem("BAR")
        bar.n += len(ENGINES)
        tgt = bar.n
        for eng in ENGINES:
            def fn(e, h=bar.h, t=tgt):
                e.drain().then_inc(h, 1)
                e.wait_ge(h, t)
            self.emit(eng, fn)

    def realize(self):
        with self.nc.Block() as blk:
            blk.sync(lambda e: [f(e) for f in self.q["sync"]])
            blk.scalar(lambda e: [f(e) for f in self.q["scalar"]])
            blk.vector(lambda e: [f(e) for f in self.q["vector"]])
            blk.tensor(lambda e: [f(e) for f in self.q["tensor"]])
            blk.gpsimd(lambda e: [f(e) for f in self.q["gpsimd"]])


class Arena:
    def __init__(self, nc, name, nbytes):
        self.nc = nc
        self.name = name
        self.t = nc.alloc_sbuf_tensor(name, [128, nbytes], U8)
        self.nbytes = nbytes
        self.addr = nc.lookup_mloc(self.t).addr
        self._n_alias = 0

    def alias(self, dtype, off, shape):
        esz = {F32: 4, F32R: 4, BF16: 2, I32: 4, U8: 1}[dtype]
        cols = 1
        for s_ in shape[1:]:
            cols *= s_
        assert off + cols * esz <= self.nbytes
        self._n_alias += 1
        t = self.nc.alloc_sbuf_tensor_at(
            f"{self.name}_al{self._n_alias}", [shape[0], cols], dtype,
            offset=self.addr + off)
        ap = t[:, :]
        if len(shape) == 3:
            ap = ap.rearrange("p (a b) -> p a b", a=shape[1], b=shape[2])
        elif len(shape) == 4:
            ap = ap.rearrange("p (a b c) -> p a b c",
                              a=shape[1], b=shape[2], c=shape[3])
        return ap

    def view(self, dtype, off, shape):
        esz = {F32: 4, F32R: 4, BF16: 2, I32: 4, U8: 1}[dtype]
        assert off % 64 == 0, off
        cols = 1
        for s_ in shape[1:]:
            cols *= s_
        assert off + cols * esz <= self.nbytes, (off, cols * esz, self.nbytes)
        ap = self.t.bitcast(dtype)[0:shape[0], off // esz: off // esz + cols]
        if len(shape) == 2:
            return ap
        if len(shape) == 3:
            return ap.rearrange("p (a b) -> p a b", a=shape[1], b=shape[2])
        if len(shape) == 4:
            return ap.rearrange("p (a b c) -> p a b c",
                                a=shape[1], b=shape[2], c=shape[3])
        raise ValueError(shape)


def emit_ln(p, T, AR, PS, src_name, dst_name, f32_dst=None):
    nc = p.nc
    LD = p.sem("LDS")
    ST = p.sem("STS")
    LG = p.sem("LDG")
    MM = p.sem("TEN")
    SC = p.sem("SCA")
    VE = p.sem("VEC")
    src, dst = T[src_name], T[dst_name]
    dstf = T[f32_dst] if f32_dst else None

    xt = [AR.view(F32, 0, [128, DC, TC]), AR.view(F32, 16384, [128, DC, TC])]
    x2 = AR.view(F32, 32768, [128, DC, TC])
    xo = [AR.view(F32, 49152, [128, DC, TC]), AR.view(F32, 81920, [128, DC, TC])]
    srow = AR.view(F32, 114688, [1, 6, TC])
    ones8 = AR.view(F32, 127232, [128, 8])
    ones_row = AR.view(F32, 127296, [1, 128])

    ps_sum = PS[0][0:8, :]
    ps_sq = PS[1][0:8, :]
    ps_nm = PS[2]
    ps_rs = PS[3]

    p.op("vector", lambda e: e.memset(ones8[:], 1.0), inc=VE)
    p.op("vector", lambda e: e.memset(ones_row[:], 1.0), inc=VE)
    ones_ready = VE.n

    st_base = ST.n
    ln_done = {}
    ln_store_lg = {}
    ln_store_st = {}
    prev_t1 = prev_t2 = prev_n2 = None
    for it in range(NT):
        buf = xt[it % 2]
        w = []
        if it >= 2:
            w.append((VE, ln_done[it - 2]))
        ld_t = p.op("sync", lambda e, b=buf, i=it: e.dma_start(
            b[:], src[:, :, i * TC:(i + 1) * TC]), waits=w, inc=LD, inc_k=16)
        wsq = [(LD, ld_t)]
        if prev_n2 is not None:
            wsq.append((VE, prev_n2))
        sq_t = p.op("vector", lambda e, b=buf: e.tensor_tensor(
            x2[:], b[:], b[:], ALU.mult), waits=wsq, inc=VE)
        for c in range(DC):
            w0 = [(LD, ld_t), (VE, ones_ready)]
            if c == 0 and prev_t1 is not None:
                w0.append((VE, prev_t1))
            p.op("tensor", lambda e, b=buf, c=c: e.matmul(
                ps_sum[:], ones8[:], b[:, c, :], start=(c == 0), stop=(c == DC - 1)),
                waits=w0 if c == 0 else [])
        for c in range(DC):
            last = c == DC - 1
            w0 = [(VE, sq_t)]
            if c == 0 and prev_t2 is not None:
                w0.append((VE, prev_t2))
            t = p.op("tensor", lambda e, c=c: e.matmul(
                ps_sq[:], ones8[:], x2[:, c, :], start=(c == 0), stop=(c == DC - 1)),
                waits=w0 if c == 0 else [],
                inc=MM if last else None)
        mm_stats = MM.n
        def rc(call, waits=()):
            return p.op("vector", call, waits=waits, inc=VE)
        t1 = rc(lambda e: e.tensor_scalar_mul(srow[:, 0, :], ps_sum[0:1, :], 1.0 / D),
                waits=[(MM, mm_stats)])
        t2 = rc(lambda e: e.tensor_scalar(srow[:, 1, :], ps_sq[0:1, :], 1.0 / D, None,
                                          ALU.mult), waits=[(VE, t1)])
        t3 = rc(lambda e: e.tensor_tensor(srow[:, 4, :], srow[:, 0, :], srow[:, 0, :],
                                          ALU.mult), waits=[(VE, t2)])
        t4 = rc(lambda e: e.tensor_tensor(srow[:, 1, :], srow[:, 1, :], srow[:, 4, :],
                                          ALU.subtract), waits=[(VE, t3)])
        t4b = rc(lambda e: e.tensor_scalar_add(srow[:, 1, :], srow[:, 1, :], 1e-5),
                 waits=[(VE, t4)])
        t5 = p.op("scalar", lambda e: e.activation(
            srow[:, 2, :], srow[:, 1, :], AF.Sqrt), waits=[(VE, t4b)], inc=SC)
        t6 = rc(lambda e: e.reciprocal(srow[:, 2, :], srow[:, 2, :]), waits=[(SC, t5)])
        t7 = rc(lambda e: e.tensor_scalar_mul(srow[:, 3, :], srow[:, 0, :], -1.0),
                waits=[(VE, t6)])
        wbc = [(VE, t7)]
        if prev_n2 is not None:
            wbc.append((VE, prev_n2))
        p.op("tensor", lambda e: e.matmul(
            ps_nm[:], ones_row[:], srow[:, 3, :], start=True, stop=True),
            waits=wbc)
        bc_t = p.op("tensor", lambda e: e.matmul(
            ps_rs[:], ones_row[:], srow[:, 2, :], start=True, stop=True),
            inc=MM)
        ob = xo[it % 2]
        wst = []
        if it >= 2:
            wst.append((LG, ln_store_lg[it - 2]))
            if dstf is not None:
                wst.append((ST, ln_store_st[it - 2]))
        n1 = p.op("vector", lambda e, b=buf, ob=ob: e.tensor_tensor(
            x2[:], b[:], ps_nm[:, None, :].broadcast_to([128, DC, TC]), ALU.add),
            waits=[(MM, bc_t)] + wst, inc=VE)
        n2 = p.op("vector", lambda e, ob=ob: e.tensor_tensor(
            ob[:], x2[:], ps_rs[:, None, :].broadcast_to([128, DC, TC]), ALU.mult),
            waits=[(VE, n1)], inc=VE)
        ln_done[it] = n2
        prev_t1, prev_t2, prev_n2 = t1, t2, n2
        ln_store_lg[it] = p.op("gpsimd", lambda e, ob=ob, i=it: e.dma_start(
            dst[:, :, i * TC:(i + 1) * TC], ob[:]), waits=[(VE, n2)], inc=LG, inc_k=16)
        if dstf is not None:
            ln_store_st[it] = p.op("sync", lambda e, ob=ob, i=it: e.dma_start(
                dstf[:, :, i * TC:(i + 1) * TC], ob[:]), waits=[(VE, n2)],
                inc=ST, inc_k=16)
    p.wait("sync", [(ST, ST.n)])
    p.wait("gpsimd", [(LG, LG.n)])


def emit_attn0(p, T, AR, SM, PS):
    nc = p.nc
    LD = p.sem("LDS")
    LG = p.sem("LDG")
    ST = p.sem("STS")
    MM = p.sem("TEN")
    SC = p.sem("SCA")
    VE = p.sem("VEC")

    o = [0]

    def al(nbytes, align=64):
        o[0] = (o[0] + align - 1) // align * align
        r = o[0]
        o[0] += nbytes
        return r

    xn = AR.view(BF16, al(DC * S * 2), [128, DC, S])
    pairctx = AR.view(BF16, al(S * 2), [128, S])
    RING = 4
    expT = AR.view(BF16, al(RING * TC * 2), [128, RING, TC])
    qpr = AR.alias(F32R, al(S * 4), [128, S])
    kpr = AR.alias(F32R, al(S * 4), [128, S])
    vt = [AR.view(BF16, al(32 * 65 * 2), [128, 32, 65]) for _ in range(2)]
    qtmp = AR.view(F32, al(TC * 4), [128, TC])
    qswap = AR.view(F32, al(TC * 4), [128, TC])
    cc4 = AR.view(F32, al(S * 4), [128, S])
    ss4 = AR.view(F32, al(S * 4), [128, S])
    msk = AR.view(BF16, al(4 * TC * 2), [128, 4, TC])
    bcb = AR.view(F32, al(TC * 4), [128, TC])
    wqb = [AR.view(BF16, al(DC * 128 * 2), [128, DC, 128]) for _ in range(2)]
    wkb = [AR.view(BF16, al(DC * 128 * 2), [128, DC, 128]) for _ in range(2)]
    wvb = [AR.view(BF16, al(DC * 128 * 2), [128, DC, 128]) for _ in range(2)]
    invr = AR.view(F32, al(TC * 4), [1, TC])
    onesr = AR.view(F32, al(512), [1, 128])
    shift = AR.view(F32, al(64), [128, 1])
    head_base = o[0]
    assert o[0] <= AR.nbytes, o[0]

    ps_q = PS[0]
    ps_k = PS[1]
    ps_v = PS[2]
    ps_qk = [PS[3], PS[4]]
    ps_pv = PS[5]
    ps_bc = PS[6]

    p.op("vector", lambda e: e.memset(shift[:], -EXP_SHIFT), inc=VE)
    p.op("vector", lambda e: e.memset(onesr[:], 1.0), inc=VE)
    p.op("vector", lambda e: e.memset(vt[0][:, :, 64:65], 1.0), inc=VE)
    p.op("vector", lambda e: e.memset(vt[1][:, :, 64:65], 1.0), inc=VE)
    const_v = VE.n
    ld0 = p.op("sync", lambda e: e.dma_start(xn[:], T["xn0"][:]), inc=LD, inc_k=16)
    p.op("sync", lambda e: e.dma_start(cc4[:], T["cc4"][:]), inc=LD, inc_k=16)
    ld1 = p.op("sync", lambda e: e.dma_start(ss4[:], T["ss4"][:]), inc=LD, inc_k=16)
    ldm = p.op("gpsimd", lambda e: e.dma_start(
        msk[:], T["masks"].rearrange("r p t -> p r t")), inc=LG, inc_k=16)

    wqkvT = T["wqkvT"].rearrange("(c p) n -> p c n", p=128)

    pair_mm_done = {}
    prev_ctx_tt = None
    prev_pair_st = None
    prev_exp = {}
    prev_pv = {}
    prev_cp = None
    evac_done = {"q": None, "k": None}
    prev_v3 = None
    vev = None

    for g in range(H // 2):
        wb = g % 2
        wld_guard = [(MM, pair_mm_done[g - 2])] if g >= 2 else []
        wready = None
        for dst, col0 in ((wqb[wb], g * 128), (wkb[wb], D + g * 128),
                          (wvb[wb], 2 * D + g * 128)):
            wready = p.op("gpsimd", lambda e, d=dst, c=col0: e.dma_start(
                d[:], wqkvT[:, :, c:c + 128]), waits=wld_guard, inc=LG, inc_k=16)
            wld_guard = []

        for which, psb, wsl, dstp in (("q", ps_q, wqb[wb], qpr),
                                      ("k", ps_k, wkb[wb], kpr)):
            for ch in range(NT):
                w0 = [(LD, ld1), (LG, wready), (VE, const_v)]
                if evac_done[which] is not None:
                    w0.append((SC, evac_done[which]))
                for c in range(DC):
                    mmt = p.op("tensor",
                               lambda e, c=c, ch=ch, psb=psb, wsl=wsl: e.matmul(
                                   psb[:], wsl[:, c, :],
                                   xn[:, c, ch * TC:(ch + 1) * TC],
                                   start=(c == 0), stop=(c == DC - 1)),
                               waits=w0 if c == 0 else [],
                               inc=MM if c == DC - 1 else None)
                wev = [(MM, mmt)]
                if prev_v3 is not None:
                    wev.append((VE, prev_v3))
                p.op("scalar", lambda e, psb=psb: e.copy(qtmp[:], psb[:]),
                     waits=wev, inc=SC)
                for q_ in range(4):
                    src = (q_ ^ 1) * 32
                    p.op("scalar", lambda e, psb=psb, q_=q_, src=src: e.copy(
                        qswap[q_ * 32:(q_ + 1) * 32, :], psb[src:src + 32, :]),
                        inc=SC)
                ev_t = SC.n
                evac_done[which] = ev_t
                sl = slice(ch * TC, (ch + 1) * TC)
                p.op("vector", lambda e, dstp=dstp, sl=sl: e.tensor_tensor(
                    dstp[:, sl], qtmp[:], cc4[:, sl], ALU.mult),
                    waits=[(SC, ev_t)], inc=VE)
                v2 = p.op("vector", lambda e, sl=sl: e.tensor_tensor(
                    qswap[:], qswap[:], ss4[:, sl], ALU.mult), inc=VE)
                v3 = p.op("vector", lambda e, dstp=dstp, sl=sl: e.tensor_tensor(
                    dstp[:, sl], dstp[:, sl], qswap[:], ALU.add),
                    waits=[(VE, v2)], inc=VE)
                prev_v3 = v3

        for tq in range(8):
            w0 = [(LD, ld0), (LG, wready)]
            if vev is not None:
                w0.append((VE, vev))
            for sub in range(4):
                t128 = tq * 512 + sub * 128
                for c in range(DC):
                    mmt = p.op("tensor",
                               lambda e, c=c, sub=sub, t128=t128, wb=wb: e.matmul(
                                   ps_v[:, sub * 128:(sub + 1) * 128],
                                   xn[:, c, t128:t128 + 128], wvb[wb][:, c, :],
                                   start=(c == 0), stop=(c == DC - 1)),
                               waits=w0 if (sub == 0 and c == 0) else [],
                               inc=MM if (sub == 3 and c == DC - 1) else None)
            pv4 = ps_v.rearrange("p (s n) -> p s n", s=4)
            p.op("vector", lambda e, tq=tq: e.tensor_copy(
                vt[0][:, tq * 4:(tq + 1) * 4, 0:64], pv4[:, :, 0:64]),
                waits=[(MM, mmt)], inc=VE)
            vev = p.op("vector", lambda e, tq=tq: e.tensor_copy(
                vt[1][:, tq * 4:(tq + 1) * 4, 0:64], pv4[:, :, 64:128]),
                inc=VE)
        vproj_done = vev

        for hh in range(2):
            base = hh * 64
            myvt = vt[hh]
            for qc in range(NT):
                nkt = 4 * (qc + 1)
                qsl = slice(qc * TC, (qc + 1) * TC)
                for ktc in range(nkt):
                    bank = ktc % 2
                    slot = ktc % RING
                    wqk = []
                    if prev_exp.get(bank) is not None:
                        wqk.append((SC, prev_exp[bank]))
                    if ktc == 0:
                        if prev_ctx_tt is not None:
                            wqk.append((VE, prev_ctx_tt))
                        if hh == 0 and qc == 0:
                            wqk.append((VE, vproj_done))
                            wqk.append((LG, ldm))
                    qk_t = p.op("tensor",
                                lambda e, b=bank, ktc=ktc, base=base, qsl=qsl:
                                e.matmul(ps_qk[b][:],
                                         kpr[base:base + 64,
                                             ktc * 128:(ktc + 1) * 128],
                                         qpr[base:base + 64, qsl],
                                         start=True, stop=True),
                                waits=wqk, inc=MM)
                    wex = [(MM, qk_t)]
                    if prev_pv.get(slot) is not None:
                        wex.append((MM, prev_pv[slot]))
                    ex_t = p.op("scalar", lambda e, b=bank, slot=slot: e.activation(
                        expT[:, slot, :], ps_qk[b][:], AF.Exp,
                        bias=shift[:, 0:1], scale=0.125),
                        waits=wex, inc=SC)
                    prev_exp[bank] = ex_t
                    r = ktc - 4 * qc
                    dep = (SC, ex_t)
                    if 0 <= r <= 3:
                        mk_t = p.op("vector", lambda e, slot=slot, r=r:
                                    e.tensor_tensor(expT[:, slot, :],
                                                    expT[:, slot, :],
                                                    msk[:, r, :], ALU.mult),
                                    waits=[(SC, ex_t)], inc=VE)
                        dep = (VE, mk_t)
                    pv_t = p.op("tensor",
                                lambda e, slot=slot, ktc=ktc, nkt=nkt, myvt=myvt:
                                e.matmul(ps_pv[0:65, :], myvt[:, ktc, 0:65],
                                         expT[:, slot, :],
                                         start=(ktc == 0), stop=(ktc == nkt - 1)),
                                waits=[dep], inc=MM)
                    prev_pv[slot] = pv_t
                iv_t = p.op("vector", lambda e: e.reciprocal(
                    invr[:], ps_pv[64:65, :]), waits=[(MM, pv_t)], inc=VE)
                wbc = [(VE, iv_t)]
                if prev_cp is not None:
                    wbc.append((SC, prev_cp))
                bc_t = p.op("tensor", lambda e: e.matmul(
                    ps_bc[0:64, :], onesr[0:1, 0:64], invr[:],
                    start=True, stop=True), waits=wbc, inc=MM)
                cp_t = p.op("scalar", lambda e: e.copy(bcb[0:64, :], ps_bc[0:64, :]),
                            waits=[(MM, bc_t)], inc=SC)
                prev_cp = cp_t
                wctx = [(SC, cp_t)]
                if hh == 0 and qc == 0 and prev_pair_st is not None:
                    wctx.append((ST, prev_pair_st))
                ctx_t = p.op("vector", lambda e, base=base, qsl=qsl: e.tensor_tensor(
                    pairctx[base:base + 64, qsl], ps_pv[0:64, :], bcb[0:64, :],
                    ALU.mult), waits=wctx, inc=VE)
                prev_ctx_tt = ctx_t
        prev_pair_st = p.op("sync", lambda e, g=g: e.dma_start(
            T["ctx0"][:, g, :], pairctx[:]), waits=[(VE, prev_ctx_tt)],
            inc=ST, inc_k=16)
        pair_mm_done[g] = MM.n

    p.barrier()
    wo = AR.view(BF16, 0, [128, DC, DC, 128])
    ctxt = [AR.view(BF16, 16384 + i * DC * TC * 2, [128, DC, TC]) for i in range(2)]
    s0st = [AR.view(BF16, 32768 + i * DC * TC * 2, [128, DC, TC]) for i in range(2)]

    lw = p.op("gpsimd", lambda e: e.dma_start(
        wo[:], T["woutT"].rearrange("(fc p) (dc m) -> p fc dc m", p=128, m=128)),
        inc=LG, inc_k=16)
    mm_done = {}
    ev_done = {}
    st_done = {}
    for it in range(NT):
        bi = it % 2
        w0 = [(LG, lw)]
        if it >= 2:
            w0.append((MM, mm_done[it - 2]))
        ld_t = p.op("sync", lambda e, it=it, bi=bi: e.dma_start(
            ctxt[bi][:], T["ctx0"][:, :, it * TC:(it + 1) * TC]),
            waits=w0, inc=LD, inc_k=16)
        for dc in range(DC):
            bank = dc % 2
            wmm = [(LD, ld_t)]
            prev = it * DC + dc - 2
            if prev >= 0:
                wmm.append((SC, ev_done[prev]))
            for fc in range(DC):
                mmt = p.op("tensor",
                           lambda e, fc=fc, dc=dc, bi=bi, bank=bank: e.matmul(
                               ps_qk[bank][:], wo[:, fc, dc, :], ctxt[bi][:, fc, :],
                               start=(fc == 0), stop=(fc == DC - 1)),
                           waits=wmm if fc == 0 else [],
                           inc=MM if fc == DC - 1 else None)
            wev = [(MM, mmt)]
            if it >= 2 and dc == 0:
                wev.append((ST, st_done[it - 2]))
            ev_done[it * DC + dc] = p.op(
                "scalar", lambda e, dc=dc, bi=bi, bank=bank: e.copy(
                    s0st[bi][:, dc, :], ps_qk[bank][:]), waits=wev, inc=SC)
        mm_done[it] = MM.n
        st_done[it] = p.op("sync", lambda e, it=it, bi=bi: e.dma_start(
            T["s0"][:, :, it * TC:(it + 1) * TC], s0st[bi][:]),
            waits=[(SC, ev_done[it * DC + DC - 1])], inc=ST, inc_k=16)
    p.wait("sync", [(ST, ST.n)])


def emit_router(p, T, AR, SM, PS):
    nc = p.nc
    LD = p.sem("LDS")
    LG = p.sem("LDG")
    ST = p.sem("STS")
    MM = p.sem("TEN")
    SC = p.sem("SCA")
    VE = p.sem("VEC")

    o = [0]

    def al(n, a=64):
        o[0] = (o[0] + a - 1) // a * a
        r = o[0]
        o[0] += n
        return r

    xnt = [AR.alias(F32R, al(DC * TC * 4), [128, DC, TC]) for _ in range(2)]
    r1 = AR.alias(F32R, al(2 * S * 4), [128, 2, S])
    wr1b = AR.alias(F32R, al(DC * 256 * 4), [128, DC, 256])
    wr2b = AR.alias(F32R, al(2 * 8 * 4), [128, 2, 8])
    br1b = AR.view(F32, al(2 * 4), [128, 2])
    br2b = AR.view(F32, al(4), [8, 1])
    lgrow = AR.view(F32, al(S * 4), [8, S])
    idf = AR.view(F32, al(128 * 4), [128, 128])
    gTM = AR.view(F32, al(32 * 8 * 4), [128, 32, 8])
    e8 = AR.view(F32, al(32 * 8 * 4), [128, 32, 8])
    m1 = AR.view(F32, al(32 * 8 * 4), [128, 32, 8])
    m2 = AR.view(F32, al(32 * 8 * 4), [128, 32, 8])
    w9tm = AR.view(F32, al(32 * 9 * 4), [128, 32, 9])
    cols = AR.view(F32, al(32 * 8 * 4), [128, 8, 32])
    w9sb = AR.view(F32, al(S * 4), [9, S])
    assert o[0] <= AR.nbytes

    ps_r = [PS[0], PS[1]]
    ps_l = PS[2][0:8, :]
    ps_t = PS[3]
    ps_t9 = PS[4][0:9, :]

    lw = p.op("gpsimd", lambda e: e.dma_start(
        wr1b[:], T["wr1"].rearrange("(c p) m -> p c m", p=128)), inc=LG, inc_k=16)
    lw = p.op("gpsimd", lambda e: e.dma_start(
        wr2b[:], T["wr2"].rearrange("(c p) m -> p c m", p=128)), inc=LG, inc_k=16)
    p.op("sync", lambda e: e.dma_start(br1b[:], T["br1t"][:]), inc=LD, inc_k=16)
    p.op("sync", lambda e: e.dma_start(br2b[:], T["br2"][:]), inc=LD, inc_k=16)
    ld_c = p.op("sync", lambda e: e.dma_start(idf[:], T["ident32"][:]),
                inc=LD, inc_k=16)

    ev_prev = {0: None, 1: None}
    ev_prev_tile = {}
    lg_prev = None
    for it in range(NT):
        bi = it % 2
        w0 = [(LG, lw), (LD, ld_c)]
        if it >= 2:
            w0.append((SC, ev_prev_tile[it - 2]))
        ld_t = p.op("gpsimd", lambda e, it=it, bi=bi: e.dma_start(
            xnt[bi][:], T["xn0f"][:, :, it * TC:(it + 1) * TC]),
            waits=w0, inc=LG, inc_k=16)
        for m_ in range(2):
            wmm = [(LD, ld_t)]
            if ev_prev[m_] is not None:
                wmm.append((SC, ev_prev[m_]))
            wmm = [(LG, ld_t)] + wmm[1:]
            for c in range(DC):
                mmt = p.op("tensor", lambda e, c=c, m_=m_, bi=bi: e.matmul(
                    ps_r[m_][:], wr1b[:, c, m_ * 128:(m_ + 1) * 128],
                    xnt[bi][:, c, :], start=(c == 0), stop=(c == DC - 1)),
                    waits=wmm if c == 0 else [], inc=MM if c == DC - 1 else None)
            ev_prev[m_] = p.op("scalar", lambda e, m_=m_, it=it: e.activation(
                r1[:, m_, it * TC:(it + 1) * TC], ps_r[m_][:], AF.Relu,
                bias=br1b[:, m_:m_ + 1]), waits=[(MM, mmt)], inc=SC)
        ev_prev_tile[it] = ev_prev[1]
        wl = [(SC, ev_prev[1])]
        if lg_prev is not None:
            wl.append((SC, lg_prev))
        for c2 in range(2):
            mmt = p.op("tensor", lambda e, c2=c2, it=it: e.matmul(
                ps_l[:], wr2b[:, c2, :], r1[:, c2, it * TC:(it + 1) * TC],
                start=(c2 == 0), stop=(c2 == 1)),
                waits=wl if c2 == 0 else [], inc=MM if c2 == 1 else None)
        lg_prev = p.op("scalar", lambda e, it=it: e.activation(
            lgrow[:, it * TC:(it + 1) * TC], ps_l[:], AF.Identity,
            bias=br2b[:, 0:1]), waits=[(MM, mmt)], inc=SC)

    tp_prev = None
    for grp in range(2):
        wtp = [(SC, lg_prev)]
        if tp_prev is not None:
            wtp.append((VE, tp_prev))
        for i in range(16):
            tt = grp * 16 + i
            mmt = p.op("tensor", lambda e, i=i, tt=tt: e.transpose(
                ps_t[:, i * 8:(i + 1) * 8], lgrow[:, tt * 128:(tt + 1) * 128],
                idf[0:8, 0:8]), waits=wtp if i == 0 else [],
                inc=MM if i == 15 else None)
        pt3 = ps_t.rearrange("p (i v) -> p i v", v=8)
        tp_prev = p.op("vector", lambda e, grp=grp: e.tensor_copy(
            gTM[:, grp * 16:(grp + 1) * 16, :], pt3[:, 0:16, :]),
            waits=[(MM, mmt)], inc=VE)

    def vop(call, waits=()):
        return p.op("vector", call, waits=waits, inc=VE)
    mx = cols[:, 0, :]
    e1c = cols[:, 1, :]
    e2c = cols[:, 2, :]
    Sc_ = cols[:, 3, :]
    rS = cols[:, 4, :]
    w1 = cols[:, 5, :]
    w2 = cols[:, 6, :]
    tc_ = cols[:, 7, :]
    t0 = vop(lambda e: e.reduce_max(mx, gTM[:], axis=AX.X), waits=[(VE, tp_prev)])
    t1 = vop(lambda e: e.tensor_tensor(
        e8[:], gTM[:], mx[:, :, None].broadcast_to([128, 32, 8]), ALU.subtract),
        waits=[(VE, t0)])
    t2 = p.op("scalar", lambda e: e.activation(
        e8.rearrange("p a b -> p (a b)"), e8.rearrange("p a b -> p (a b)"),
        AF.Exp), waits=[(VE, t1)], inc=SC)
    t3 = vop(lambda e: e.reduce_sum(Sc_, e8[:], axis=AX.X), waits=[(SC, t2)])
    t4 = vop(lambda e: e.tensor_tensor(
        m1[:], gTM[:], mx[:, :, None].broadcast_to([128, 32, 8]), ALU.is_ge),
        waits=[(SC, t2)])
    t5 = vop(lambda e: e.reduce_max(e1c, e8[:], axis=AX.X), waits=[(VE, t4)])
    t6 = vop(lambda e: e.tensor_scalar(m2[:], m1[:], -1.0, 1.0, ALU.mult, ALU.add),
             waits=[(VE, t4)])
    t7 = vop(lambda e: e.tensor_tensor(gTM[:], e8[:], m2[:], ALU.mult),
             waits=[(VE, t6)])
    t8 = vop(lambda e: e.reduce_max(e2c, gTM[:], axis=AX.X), waits=[(VE, t7)])
    t9 = vop(lambda e: e.tensor_tensor(
        m2[:], gTM[:], e2c[:, :, None].broadcast_to([128, 32, 8]), ALU.is_ge),
        waits=[(VE, t8)])
    t10 = vop(lambda e: e.tensor_tensor(tc_, e1c, e2c, ALU.subtract),
              waits=[(VE, t9), (VE, t5), (VE, t3)])
    t11 = vop(lambda e: e.reciprocal(rS, Sc_), waits=[(VE, t10)])
    t12 = vop(lambda e: e.tensor_tensor(tc_, tc_, rS, ALU.mult), waits=[(VE, t11)])
    t13 = vop(lambda e: e.tensor_scalar_mul(tc_, tc_, 0.5), waits=[(VE, t12)])
    t14 = p.op("scalar", lambda e: e.activation(w1, tc_, AF.Sigmoid),
               waits=[(VE, t13)], inc=SC)
    t15 = vop(lambda e: e.tensor_scalar(w2, w1, -1.0, 1.0, ALU.mult, ALU.add),
              waits=[(SC, t14)])
    t16 = vop(lambda e: e.memset(w9tm[:, :, 0:1], 1.0))
    t17 = vop(lambda e: e.tensor_tensor(
        m1[:], m1[:], w1[:, :, None].broadcast_to([128, 32, 8]), ALU.mult),
        waits=[(VE, t15)])
    t18 = vop(lambda e: e.tensor_tensor(
        m2[:], m2[:], w2[:, :, None].broadcast_to([128, 32, 8]), ALU.mult),
        waits=[(VE, t17)])
    t19 = vop(lambda e: e.tensor_tensor(w9tm[:, :, 1:9], m1[:], m2[:], ALU.add),
              waits=[(VE, t18)])

    tp_prev = None
    for grp in range(8):
        wtp = [(VE, t19)]
        if tp_prev is not None:
            wtp.append((SC, tp_prev))
        for i in range(4):
            tt = grp * 4 + i
            mmt = p.op("tensor", lambda e, i=i, tt=tt: e.transpose(
                ps_t9[:, i * 128:(i + 1) * 128],
                w9tm.rearrange("p a b -> p (a b)")[:, tt * 9:(tt + 1) * 9],
                idf[:, :]), waits=wtp if i == 0 else [],
                inc=MM if i == 3 else None)
        pt4 = ps_t9.rearrange("p (i v) -> p i v", v=128)
        tp_prev = p.op("scalar", lambda e, grp=grp: e.copy(
            w9sb.rearrange("p (i v) -> p i v", v=128)[:, grp * 4:(grp + 1) * 4, :],
            pt4[:, 0:4, :]), waits=[(MM, mmt)], inc=SC)
    p.op("sync", lambda e: e.dma_start(T["w9"][:], w9sb[:]),
         waits=[(SC, tp_prev)], inc=ST, inc_k=16)
    p.wait("sync", [(ST, ST.n)])


def emit_moe(p, T, AR, SM, PS):
    nc = p.nc
    LD = p.sem("LDS")
    LG = p.sem("LDG")
    MM = p.sem("TEN")
    SC = p.sem("SCA")
    VE = p.sem("VEC")

    o = [0]

    def al(n, a=64):
        o[0] = (o[0] + a - 1) // a * a
        r = o[0]
        o[0] += n
        return r

    we1 = [AR.view(BF16, al(DC * 16 * 128 * 2), [128, DC, 16, 128]) for _ in range(2)]
    we2 = [AR.view(BF16, al(16 * DC * 128 * 2), [128, 16, DC, 128]) for _ in range(2)]
    ab = AR.view(BF16, al(16 * TC * 2), [128, 16, TC])
    xnt = [AR.view(BF16, al(DC * TC * 2), [128, DC, TC]) for _ in range(2)]
    accst = [AR.view(F32, al(DC * TC * 4), [128, DC, TC]) for _ in range(2)]
    assert o[0] <= AR.nbytes, o[0]
    wbc = [SM.view(F32, i * 2048, [128, TC]) for i in range(2)]
    be1b = SM.view(F32, 4096, [128, 32])
    be2b = SM.view(F32, 4096 + 128, [128, 8])

    ps_a = [PS[0], PS[1]]
    ps_h = [PS[2], PS[3]]

    we1_r = T["we1"].rearrange("e (c p) (m q) -> e p c m q", p=128, q=128)
    we2_r = T["we2"].rearrange("e (k p) (d q) -> e p k d q", p=128, q=128)
    ws1_r = T["ws1"].rearrange("(c p) (m q) -> p c m q", p=128, q=128)
    ws2_r = T["ws2"].rearrange("(k p) (d q) -> p k d q", p=128, q=128)

    def w_srcs(ei, hh):
        if ei == 0:
            return (ws1_r[:, :, hh * 16:(hh + 1) * 16, :],
                    ws2_r[:, hh * 16:(hh + 1) * 16, :, :])
        return (we1_r[ei - 1, :, :, hh * 16:(hh + 1) * 16, :],
                we2_r[ei - 1, :, hh * 16:(hh + 1) * 16, :, :])

    steps = [(ei, hh) for ei in range(NEXP) for hh in range(2)]
    ld_w = {}
    mm_step = {}
    mm_tile = {}
    ac_t = {}
    sil_last = {}
    stt_last_h0 = {}
    prev_sil = {}
    prev_stt = {}
    prev_stt_tile_map = {}

    for si in range(2):
        ei, hh = steps[si]
        s1, s2 = w_srcs(ei, hh)
        p.op("gpsimd", lambda e, s1=s1, si=si: e.dma_start(we1[si % 2][:], s1),
             inc=LG, inc_k=16)
        ld_w[si] = p.op("gpsimd", lambda e, s2=s2, si=si: e.dma_start(
            we2[si % 2][:], s2), inc=LG, inc_k=16)

    bias_ld = {}
    p.op("sync", lambda e: e.dma_start(be1b[:], T["bs1t"][:]), inc=LD, inc_k=16)
    bias_ld[0] = p.op("sync", lambda e: e.dma_start(be2b[:], T["bs2t"][:]),
                      inc=LD, inc_k=16)

    for si, (ei, hh) in enumerate(steps):
        wb = si % 2
        if hh == 0 and ei > 0:
            wbias = [(SC, sil_last[si - 1]), (VE, stt_last_h0[ei - 1])]
            p.op("sync", lambda e, ei=ei: e.dma_start(
                be1b[:], T["be1t"][ei - 1]), waits=wbias, inc=LD, inc_k=16)
            bias_ld[ei] = p.op("sync", lambda e, ei=ei: e.dma_start(
                be2b[:], T["be2t"][ei - 1]), inc=LD, inc_k=16)
        sec_prev = None
        for tg in range(NT):
            tb = tg % 2
            sl = slice(tg * TC, (tg + 1) * TC)
            idx = si * NT + tg
            wxn = []
            pu = (si, tg - 2) if tg >= 2 else (si - 1, NT - 2 + tg)
            if pu in mm_tile:
                wxn.append((MM, mm_tile[pu]))
            ld_x = p.op("sync", lambda e, tb=tb, sl=sl: e.dma_start(
                xnt[tb][:], T["xn0"][:, :, sl]), waits=wxn, inc=LD, inc_k=16)
            wwb = []
            if pu in prev_stt_tile_map:
                wwb.append((VE, prev_stt_tile_map[pu]))
            ld_wb = p.op("gpsimd", lambda e, tb=tb, ei=ei, sl=sl: e.dma_start(
                wbc[tb][:], T["w9"][ei:ei + 1, sl].partition_broadcast(128)),
                waits=wwb, inc=LG, inc_k=16)
            for m_ in range(16):
                bank = m_ % 2
                wmm = [(LD, ld_x), (LG, ld_w[si])]
                if prev_sil.get(bank) is not None:
                    wmm.append((SC, prev_sil[bank]))
                for c in range(DC):
                    mmt = p.op("tensor",
                               lambda e, c=c, m_=m_, wb=wb, tb=tb, bank=bank:
                               e.matmul(ps_a[bank][:], we1[wb][:, c, m_, :],
                                        xnt[tb][:, c, :],
                                        start=(c == 0), stop=(c == DC - 1)),
                               waits=wmm if c == 0 else [],
                               inc=MM if c == DC - 1 else None)
                wsl = [(MM, mmt)]
                if m_ == 0:
                    wsl.append((LD, bias_ld[ei]))
                    if sec_prev is not None:
                        wsl.append((MM, sec_prev))
                prev_sil[bank] = p.op(
                    "scalar", lambda e, m_=m_, bank=bank, hh=hh: e.activation(
                        ab[:, m_, :], ps_a[bank][:], AF.Silu,
                        bias=be1b[:, hh * 16 + m_:hh * 16 + m_ + 1]),
                    waits=wsl, inc=SC)
            sil_all = SC.n
            stt_t = None
            for d in range(DC):
                bank = d % 2
                wmm = [(SC, sil_all)] if d == 0 else []
                if prev_stt.get(bank) is not None:
                    wmm.append((VE, prev_stt[bank]))
                for k in range(16):
                    mmt = p.op("tensor",
                               lambda e, k=k, d=d, wb=wb, bank=bank:
                               e.matmul(ps_h[bank][:], we2[wb][:, k, d, :],
                                        ab[:, k, :],
                                        start=(k == 0), stop=(k == 15)),
                               waits=wmm if k == 0 else [],
                               inc=MM if k == 15 else None)
                wev = [(MM, mmt), (LG, ld_wb)]
                if d == 0:
                    if (idx - 2) >= 0 and (idx - 2) in ac_t:
                        wev.append((LG, ac_t[idx - 2]))
                if hh == 0:
                    stt_t = p.op(
                        "vector",
                        lambda e, d=d, tb=tb, bank=bank, idx=idx:
                        e.scalar_tensor_tensor(
                            accst[idx % 2][:, d, :], ps_h[bank][:],
                            be2b[:, d:d + 1], wbc[tb][:], ALU.add, ALU.mult),
                        waits=wev, inc=VE)
                else:
                    stt_t = p.op(
                        "vector",
                        lambda e, d=d, tb=tb, bank=bank, idx=idx: e.tensor_tensor(
                            accst[idx % 2][:, d, :], ps_h[bank][:], wbc[tb][:],
                            ALU.mult), waits=wev, inc=VE)
                prev_stt[bank] = stt_t
            prev_stt_tile_map[(si, tg)] = stt_t
            op_ = ALU.bypass if si == 0 else ALU.add
            ac_t[idx] = p.op("gpsimd", lambda e, sl=sl, op_=op_, idx=idx:
                             e.dma_start(T["acc"][:, :, sl], accst[idx % 2][:],
                                         accum_op=op_),
                             waits=[(VE, stt_t)], inc=LG, inc_k=16)
            mm_tile[(si, tg)] = MM.n
            sec_prev = MM.n
        mm_step[si] = MM.n
        sil_last[si] = SC.n
        if hh == 0:
            stt_last_h0[ei] = VE.n
        if si + 2 < len(steps):
            nei, nhh = steps[si + 2]
            s1, s2 = w_srcs(nei, nhh)
            p.op("gpsimd", lambda e, s1=s1, wb=wb: e.dma_start(we1[wb][:], s1),
                 waits=[(MM, mm_step[si])], inc=LG, inc_k=16)
            ld_w[si + 2] = p.op("gpsimd", lambda e, s2=s2, wb=wb: e.dma_start(
                we2[wb][:], s2), inc=LG, inc_k=16)
    p.wait("gpsimd", [(LG, LG.n)])


def emit_comb0(p, T, AR, PS):
    LD = p.sem("LDS")
    ST = p.sem("STS")
    SC = p.sem("SCA")
    VE = p.sem("VEC")

    o = [0]

    def al(n, a=64):
        o[0] = (o[0] + a - 1) // a * a
        r = o[0]
        o[0] += n
        return r

    xt = [AR.view(F32, al(DC * TC * 4), [128, DC, TC]) for _ in range(2)]
    s0t = [AR.view(BF16, al(DC * TC * 2), [128, DC, TC]) for _ in range(2)]
    act = [AR.view(F32, al(DC * TC * 4), [128, DC, TC]) for _ in range(2)]
    gt = AR.view(F32, al(DC * TC * 4), [128, DC, TC])
    x1t = [AR.view(F32, al(DC * TC * 4), [128, DC, TC]) for _ in range(2)]
    assert o[0] <= AR.nbytes

    done_v = {}
    done_st = {}
    for it in range(NT):
        bi = it % 2
        sl = slice(it * TC, (it + 1) * TC)
        w0 = [(VE, done_v[it - 2])] if it >= 2 else []
        ld1 = p.op("sync", lambda e, bi=bi, sl=sl: e.dma_start(
            xt[bi][:], T["xT"][:, :, sl]), waits=w0, inc=LD, inc_k=16)
        ld2 = p.op("sync", lambda e, bi=bi, sl=sl: e.dma_start(
            s0t[bi][:], T["s0"][:, :, sl]), inc=LD, inc_k=16)
        ld3 = p.op("sync", lambda e, bi=bi, sl=sl: e.dma_start(
            act[bi][:], T["acc"][:, :, sl]), inc=LD, inc_k=16)
        wsi = [(LD, ld3)]
        if it >= 1:
            wsi.append((VE, done_v[it - 1]))
        si_t = p.op("scalar", lambda e, bi=bi: e.activation(
            gt.rearrange("p a b -> p (a b)"),
            act[bi].rearrange("p a b -> p (a b)"), AF.Silu),
            waits=wsi, inc=SC)
        v1 = p.op("vector", lambda e, bi=bi: e.tensor_tensor(
            gt[:], s0t[bi][:], gt[:], ALU.mult),
            waits=[(SC, si_t), (LD, ld2)], inc=VE)
        wv2 = [(VE, v1), (LD, ld1)]
        if it >= 2:
            wv2.append((ST, done_st[it - 2]))
        v2 = p.op("vector", lambda e, bi=bi: e.tensor_tensor(
            x1t[bi][:], xt[bi][:], gt[:], ALU.add), waits=wv2, inc=VE)
        done_v[it] = v2
        done_st[it] = p.op("sync", lambda e, bi=bi, sl=sl: e.dma_start(
            T["x1"][:, :, sl], x1t[bi][:]), waits=[(VE, v2)], inc=ST, inc_k=16)
    p.wait("sync", [(ST, ST.n)])


def emit_attn1(p, T, AR, SM, PS):
    nc = p.nc
    LD = p.sem("LDS")
    LG = p.sem("LDG")
    ST = p.sem("STS")
    MM = p.sem("TEN")
    SC = p.sem("SCA")
    VE = p.sem("VEC")

    o = [0]

    def al(n, a=64):
        o[0] = (o[0] + a - 1) // a * a
        r = o[0]
        o[0] += n
        return r

    Call = AR.view(BF16, al(16 * 128 * 2), [128, 16, 128])
    veffM = AR.view(BF16, al(16 * 64 * 2), [128, 16, 64])
    Gb = AR.view(BF16, al(128 * 2), [128, 128])
    Eh = AR.view(BF16, al(128 * 2), [64, 128])
    ckvb = AR.view(BF16, al(S * 2), [128, S])
    cqb = AR.view(BF16, al(S * 2), [128, S])
    qlb = [AR.view(BF16, al(S * 2), [128, S]) for _ in range(2)]
    wvh = [AR.view(BF16, al(32 * 65 * 2), [128, 32, 65]) for _ in range(2)]
    expT = AR.view(BF16, al(4 * TC * 2), [128, 4, TC])
    msk = SM.view(BF16, 0, [128, 4, TC])
    invr = [AR.view(F32, al(TC * 4), [1, TC]) for _ in range(2)]
    bcb = [AR.view(F32, al(TC * 4), [64, TC]) for _ in range(2)]
    onesr = AR.view(F32, al(512), [1, 64])
    shift = AR.view(F32, al(64), [128, 1])
    wdqT_b = AR.view(BF16, al(DC * 128 * 2), [128, DC, 128])
    wdkvT_b = AR.view(BF16, al(DC * 128 * 2), [128, DC, 128])
    wdq_b = AR.view(BF16, al(D * 2), [128, D])
    wuq_b = AR.view(BF16, al(DC * 128 * 2), [128, DC, 128])
    wuqr_b = AR.view(BF16, al(16 * 128 * 2), [64, 16 * 128])
    wuk_b = AR.view(BF16, al(DC * 128 * 2), [128, DC, 128])
    wuv_b = AR.view(BF16, al(DC * 128 * 2), [128, DC, 128])
    woT_b = AR.view(BF16, al(DC * D * 2), [128, DC, D])
    xn1_off = al(DC * S * 2)
    xn1b = AR.view(BF16, xn1_off, [128, DC, S])
    s1st = AR.view(BF16, xn1_off, [128, DC, S])
    assert o[0] <= AR.nbytes, o[0]

    ps_qk = [PS[0], PS[1]]
    ps_pv = [PS[2], PS[3]]
    ps_bc = [PS[4], PS[6]]
    ps_wv = PS[5]
    ps_g = PS[7]

    lwt = None
    for dst, srcn, rr in (
            (wdqT_b, "wdqT", "(c p) m -> p c m"),
            (wdkvT_b, "wdkvT", "(c p) m -> p c m"),
            (wdq_b, "wdq", None), (wuq_b, "wuq", "(c p) m -> p c m"),
            (wuqr_b, "wuqr", "d h m -> d (h m)"),
            (wuk_b, "wuk", "(c p) m -> p c m"),
            (wuv_b, "wuv", "(c p) m -> p c m"),
            (woT_b, "woT", "(c p) m -> p c m")):
        srcap = T[srcn][:] if rr is None else T[srcn].rearrange(rr, **({"p": 128} if "p" in rr else {}))
        lwt = p.op("gpsimd", lambda e, d=dst, s=srcap: e.dma_start(d[:], s),
                   inc=LG, inc_k=16)
    ldm = p.op("gpsimd", lambda e: e.dma_start(
        msk[:], T["masks"].rearrange("r p t -> p r t")), inc=LG, inc_k=16)
    p.op("vector", lambda e: e.memset(onesr[:], 1.0), inc=VE)
    p.op("vector", lambda e: e.memset(shift[:], -EXP_SHIFT), inc=VE)
    p.op("vector", lambda e: e.memset(wvh[0][:, :, 64:65], 1.0), inc=VE)
    p.op("vector", lambda e: e.memset(wvh[1][:, :, 64:65], 1.0), inc=VE)
    const_v = VE.n
    ldx = p.op("sync", lambda e: e.dma_start(xn1b[:], T["xn1"][:]), inc=LD, inc_k=16)

    qk_guard = {0: None, 1: None}
    for wsl, dstp in ((wdqT_b, cqb), (wdkvT_b, ckvb)):
        for n in range(NT):
            bank = n % 2
            psb = ps_qk[bank]
            w0 = [(LD, ldx), (LG, lwt)]
            if qk_guard[bank] is not None:
                w0.append(qk_guard[bank])
            for c in range(DC):
                mmt = p.op("tensor", lambda e, c=c, n=n, psb=psb, wsl=wsl: e.matmul(
                    psb[:], wsl[:, c, :], xn1b[:, c, n * TC:(n + 1) * TC],
                    start=(c == 0), stop=(c == DC - 1)),
                    waits=w0 if c == 0 else [], inc=MM if c == DC - 1 else None)
            ev = p.op("scalar", lambda e, n=n, psb=psb, dstp=dstp: e.copy(
                dstp[:, n * TC:(n + 1) * TC], psb[:]), waits=[(MM, mmt)], inc=SC)
            qk_guard[bank] = (SC, ev)
    cqckv_done = qk_guard[1]

    for c in range(DC):
        mmt = p.op("tensor", lambda e, c=c: e.matmul(
            ps_g[:, 0:128], wuq_b[:, c, :], wuk_b[:, c, :],
            start=(c == 0), stop=(c == DC - 1)),
            waits=[(LG, lwt)] if c == 0 else [], inc=MM if c == DC - 1 else None)
    g_ev = p.op("vector", lambda e: e.tensor_copy(Gb[:], ps_g[:, 0:128]),
                waits=[(MM, mmt)], inc=VE)

    evC_guard = {0: None, 1: None}
    mmC_prev = None
    for h in range(H):
        hb = h % 2
        wE = [(VE, g_ev)]
        if mmC_prev is not None:
            wE.append((MM, mmC_prev))
        mmE = p.op("tensor", lambda e, h=h, hb=hb: e.matmul(
            ps_pv[hb][0:64, 0:128], wdq_b[:, h * 64:(h + 1) * 64], Gb[:],
            start=True, stop=True), waits=wE, inc=MM)
        evE = p.op("vector", lambda e, hb=hb: e.tensor_copy(
            Eh[:], ps_pv[hb][0:64, 0:128]), waits=[(MM, mmE)], inc=VE)
        wC = [(VE, evE)]
        if evC_guard[hb] is not None:
            wC.append((VE, evC_guard[hb]))
        mmC_prev = p.op("tensor", lambda e, h=h, hb=hb: e.matmul(
            ps_bc[hb][:, 0:128],
            wuqr_b[:, h * 128:(h + 1) * 128], Eh[:],
            start=True, stop=True), waits=wC, inc=MM)
        evC_guard[hb] = p.op("vector", lambda e, h=h, hb=hb: e.tensor_copy(
            Call[:, h, :], ps_bc[hb][:, 0:128]), waits=[(MM, mmC_prev)], inc=VE)

    for n2 in range(2):
        psb = ps_qk[n2]
        w0 = [qk_guard[n2]] if qk_guard[n2] is not None else []
        for c in range(DC):
            mmt = p.op("tensor", lambda e, c=c, n2=n2, psb=psb: e.matmul(
                psb[:], wuv_b[:, c, :], woT_b[:, c, n2 * TC:(n2 + 1) * TC],
                start=(c == 0), stop=(c == DC - 1)),
                waits=w0 if c == 0 else [], inc=MM if c == DC - 1 else None)
        ve = p.op("vector", lambda e, n2=n2, psb=psb: e.tensor_copy(
            veffM[:, n2 * 8:(n2 + 1) * 8, :],
            psb[:].rearrange("p (a b) -> p a b", b=64)), waits=[(MM, mmt)], inc=VE)
        qk_guard[n2] = (VE, ve)
    veff_done = qk_guard[1][1]

    prev_pv = {}
    pv_bank_free = {0: None, 1: None}
    invr_free = {0: None, 1: None}
    bcb_free = {0: None, 1: None}
    bc_bank_free = {0: None, 1: None}
    ql_read_done = {}
    wv_read_done = {}
    ql_ready = {}
    wv_ready = {}
    wv_evac_last = None
    last_tt = None
    st_prev = {}
    for h in range(H):
        hb2 = h % 2
        base = (h % 2) * 64
        chunk = h // 2
        for n in range(NT):
            bank = n % 2
            psb = ps_qk[bank]
            w0 = []
            if qk_guard[bank] is not None:
                w0.append(qk_guard[bank])
            mmt = p.op("tensor", lambda e, h=h, n=n, psb=psb: e.matmul(
                psb[:], Call[:, h, :], cqb[:, n * TC:(n + 1) * TC],
                start=True, stop=True), waits=w0, inc=MM)
            wev = [(MM, mmt)]
            if h >= 2:
                wev.append((MM, ql_read_done[h - 2]))
            ev = p.op("vector", lambda e, n=n, psb=psb, hb2=hb2: e.tensor_copy(
                qlb[hb2][:, n * TC:(n + 1) * TC], psb[:]), waits=wev, inc=VE)
            qk_guard[bank] = (VE, ev)
        ql_ready[h] = ev
        for grp in range(4):
            w0 = []
            if wv_evac_last is not None:
                w0.append((VE, wv_evac_last))
            for k8 in range(8):
                ktc = grp * 8 + k8
                mmt = p.op("tensor", lambda e, h=h, ktc=ktc, k8=k8: e.matmul(
                    ps_wv[:, k8 * 64:(k8 + 1) * 64],
                    ckvb[:, ktc * 128:(ktc + 1) * 128], veffM[:, h, :],
                    start=True, stop=True),
                    waits=w0 if k8 == 0 else [], inc=MM if k8 == 7 else None)
            wev = [(MM, mmt)]
            if h >= 2:
                wev.append((MM, wv_read_done[h - 2]))
            wv_evac_last = p.op("vector", lambda e, grp=grp, hb2=hb2: e.tensor_copy(
                wvh[hb2][:, grp * 8:(grp + 1) * 8, 0:64],
                ps_wv[:].rearrange("p (a b) -> p a b", b=64)),
                waits=wev, inc=VE)
        wv_ready[h] = wv_evac_last
        for qc in range(NT):
            nkt = 4 * (qc + 1)
            qv = qc % 2
            qsl = slice(qc * TC, (qc + 1) * TC)
            for ktc in range(nkt):
                bank = ktc % 2
                slot = ktc % 4
                psb = ps_qk[bank]
                wqk = []
                if qk_guard[bank] is not None:
                    wqk.append(qk_guard[bank])
                if qc == 0 and ktc == 0:
                    wqk.append((VE, ql_ready[h]))
                    if h == 0:
                        wqk.append((LG, ldm))
                        wqk.append((VE, const_v))
                qk_t = p.op("tensor", lambda e, ktc=ktc, qsl=qsl, psb=psb, hb2=hb2:
                            e.matmul(psb[:], ckvb[:, ktc * 128:(ktc + 1) * 128],
                                     qlb[hb2][:, qsl], start=True, stop=True),
                            waits=wqk, inc=MM)
                wex = [(MM, qk_t)]
                if prev_pv.get(slot) is not None:
                    wex.append((MM, prev_pv[slot]))
                ex_t = p.op("scalar", lambda e, slot=slot, psb=psb: e.activation(
                    expT[:, slot, :], psb[:], AF.Exp, bias=shift[:, 0:1],
                    scale=0.125), waits=wex, inc=SC)
                qk_guard[bank] = (SC, ex_t)
                r = ktc - 4 * qc
                dep = (SC, ex_t)
                if 0 <= r <= 3:
                    mk_t = p.op("vector", lambda e, slot=slot, r=r: e.tensor_tensor(
                        expT[:, slot, :], expT[:, slot, :], msk[:, r, :],
                        ALU.mult), waits=[(SC, ex_t)], inc=VE)
                    dep = (VE, mk_t)
                wpv = [dep]
                if ktc == 0:
                    if qc == 0:
                        wpv.append((VE, wv_ready[h]))
                    if pv_bank_free[qv] is not None:
                        wpv.append((VE, pv_bank_free[qv]))
                pv_t = p.op("tensor",
                            lambda e, slot=slot, ktc=ktc, nkt=nkt, qv=qv, hb2=hb2:
                            e.matmul(ps_pv[qv][0:65, :], wvh[hb2][:, ktc, 0:65],
                                     expT[:, slot, :],
                                     start=(ktc == 0), stop=(ktc == nkt - 1)),
                            waits=wpv, inc=MM)
                prev_pv[slot] = pv_t
            wrc = [(MM, pv_t)]
            if invr_free[qv] is not None:
                wrc.append((MM, invr_free[qv]))
            rc_t = p.op("vector", lambda e, qv=qv: e.reciprocal(
                invr[qv][:], ps_pv[qv][64:65, :]), waits=wrc, inc=VE)
            wbc = [(VE, rc_t)]
            if bc_bank_free[qv] is not None:
                wbc.append((VE, bc_bank_free[qv]))
            bc_t = p.op("tensor", lambda e, qv=qv: e.matmul(
                ps_bc[qv][0:64, :], onesr[0:1, :], invr[qv][:],
                start=True, stop=True), waits=wbc, inc=MM)
            invr_free[qv] = bc_t
            wcp = [(MM, bc_t)]
            if bcb_free[qv] is not None:
                wcp.append((VE, bcb_free[qv]))
            cp_t = p.op("vector", lambda e, qv=qv: e.tensor_copy(
                bcb[qv][:], ps_bc[qv][0:64, :]), waits=wcp, inc=VE)
            bc_bank_free[qv] = cp_t
            tt_t = p.op("vector",
                        lambda e, base=base, chunk=chunk, qsl=qsl, qv=qv:
                        e.tensor_tensor(s1st[base:base + 64, chunk, qsl],
                                        ps_pv[qv][0:64, :], bcb[qv][:],
                                        ALU.mult),
                        waits=[(VE, cp_t)], inc=VE)
            pv_bank_free[qv] = tt_t
            bcb_free[qv] = tt_t
            last_tt = tt_t
        ql_read_done[h] = MM.n
        wv_read_done[h] = MM.n
        if h % 2 == 1:
            wst = [(VE, last_tt)]
            st_prev[chunk] = p.op("sync", lambda e, chunk=chunk: e.dma_start(
                T["s1"][:, chunk, :], s1st[:, chunk, :]), waits=wst,
                inc=ST, inc_k=16)
    p.wait("sync", [(ST, ST.n)])


def emit_ffn1(p, T, AR, SM, PS):
    nc = p.nc
    LD = p.sem("LDS")
    LG = p.sem("LDG")
    ST = p.sem("STS")
    MM = p.sem("TEN")
    SC = p.sem("SCA")
    VE = p.sem("VEC")
    TQ = 1024
    NTQ = S // TQ
    M1 = FF1 // 128

    o = [0]

    def al(n, a=64):
        o[0] = (o[0] + a - 1) // a * a
        r = o[0]
        o[0] += n
        return r

    hb = AR.view(BF16, al(M1 * TQ * 2), [128, M1, TQ])
    xnt = [AR.view(BF16, al(DC * TQ * 2), [128, DC, TQ]) for _ in range(2)]
    wgb = [AR.view(BF16, al(DC * 128 * 2), [128, DC, 128]) for _ in range(2)]
    wub = [AR.view(BF16, al(DC * 128 * 2), [128, DC, 128]) for _ in range(2)]
    wdb = [AR.view(BF16, al(M1 * 128 * 2), [128, M1, 128]) for _ in range(2)]
    sgb = AR.view(BF16, al(TC * 2), [128, TC])
    sub = AR.view(BF16, al(TC * 2), [128, TC])
    g1b = AR.view(F32, al(TC * 4), [128, TC])
    s1t = [AR.view(BF16, al(DC * TC * 2), [128, DC, TC]) for _ in range(2)]
    x1t = [AR.view(F32, al(DC * TC * 4), [128, DC, TC]) for _ in range(2)]
    ost = [AR.view(F32, al(DC * TC * 4), [128, DC, TC]) for _ in range(2)]
    assert o[0] <= AR.nbytes, o[0]

    wg_r = T["wgT"].rearrange("(c p) (m q) -> p c m q", p=128, q=128)
    wu_r = T["wuT"].rearrange("(c p) (m q) -> p c m q", p=128, q=128)
    wd_r = T["wdT"].rearrange("(k p) (d q) -> p k d q", p=128, q=128)

    ps_g = [PS[0], PS[1]]
    ps_u = [PS[2], PS[3]]
    ps_d = [PS[4], PS[5]]

    ld_xn = {}
    ld_wg = {}
    mm_h = {}
    mm_tq = {}
    h_done = {}
    ev_d = {}
    st_done = {}
    s1x1_ld = {}
    prev_sg = {0: None, 1: None}
    prev_su = {0: None, 1: None}
    prev_g1 = {0: None, 1: None}
    last_ht = None
    last_v2 = None

    for m in range(min(2, M1)):
        p.op("gpsimd", lambda e, m=m: e.dma_start(wgb[m % 2][:], wg_r[:, :, m, :]),
             inc=LG, inc_k=16)
        ld_wg[(0, m)] = p.op("gpsimd", lambda e, m=m: e.dma_start(
            wub[m % 2][:], wu_r[:, :, m, :]), inc=LG, inc_k=16)

    for tq in range(NTQ):
        bi = tq % 2
        sl = slice(tq * TQ, (tq + 1) * TQ)
        wxn = [(MM, mm_tq[tq - 2])] if tq >= 2 else []
        ld_xn[tq] = p.op("sync", lambda e, bi=bi, sl=sl: e.dma_start(
            xnt[bi][:], T["xn1"][:, :, sl]), waits=wxn, inc=LD, inc_k=16)
        for m in range(M1):
            wb = m % 2
            nt_tq, nt_m = (tq, m + 2) if m + 2 < M1 else (tq + 1, (m + 2) % M1)
            if (nt_tq, nt_m) not in ld_wg and nt_tq < NTQ:
                wpre = [(MM, mm_h[(tq, m)])] if (tq, m) in mm_h else []
            for n2 in range(2):
                wmm = [(LD, ld_xn[tq]), (LG, ld_wg[(tq, m) if tq > 0 or m > 1 else (0, m)])]
                if prev_sg[n2] is not None:
                    wmm.append((SC, prev_sg[n2]))
                for c in range(DC):
                    gmm = p.op("tensor", lambda e, c=c, m=m, n2=n2, wb=wb, bi=bi:
                               e.matmul(ps_g[n2][:], wgb[wb][:, c, :],
                                        xnt[bi][:, c, n2 * TC:(n2 + 1) * TC],
                                        start=(c == 0), stop=(c == DC - 1)),
                               waits=wmm if c == 0 else [],
                               inc=MM if c == DC - 1 else None)
                wmu = []
                if prev_su[n2] is not None:
                    wmu.append((VE, prev_su[n2]))
                for c in range(DC):
                    umm = p.op("tensor", lambda e, c=c, m=m, n2=n2, wb=wb, bi=bi:
                               e.matmul(ps_u[n2][:], wub[wb][:, c, :],
                                        xnt[bi][:, c, n2 * TC:(n2 + 1) * TC],
                                        start=(c == 0), stop=(c == DC - 1)),
                               waits=wmu if c == 0 else [],
                               inc=MM if c == DC - 1 else None)
                wsg = [(MM, gmm)]
                if last_ht is not None:
                    wsg.append((VE, last_ht))
                sg_t = p.op("scalar", lambda e, n2=n2: e.activation(
                    sgb[:], ps_g[n2][:], AF.Silu, scale=1.0 / 127.0),
                    waits=wsg, inc=SC)
                prev_sg[n2] = sg_t
                wsu = [(MM, umm)]
                if last_ht is not None:
                    wsu.append((VE, last_ht))
                su_t = p.op("vector", lambda e, n2=n2: e.tensor_scalar_mul(
                    sub[:], ps_u[n2][:], 1.0 / 127.0), waits=wsu, inc=VE)
                prev_su[n2] = su_t
                wht = [(VE, su_t), (SC, sg_t)]
                if m == 0 and n2 == 0 and tq >= 1:
                    wht.append((MM, mm_tq[tq - 1]))
                ht = p.op("vector", lambda e, m=m, n2=n2: e.tensor_tensor(
                    hb[:, m, n2 * TC:(n2 + 1) * TC], sgb[:], sub[:], ALU.mult),
                    waits=wht, inc=VE)
                h_done[(tq, m, n2)] = ht
                last_ht = ht
            mm_h[(tq, m)] = MM.n
            if (nt_tq, nt_m) not in ld_wg and nt_tq < NTQ:
                p.op("gpsimd", lambda e, nt_tq=nt_tq, nt_m=nt_m, wb=wb: e.dma_start(
                    wgb[wb][:], wg_r[:, :, nt_m, :]),
                    waits=[(MM, mm_h[(tq, m)])], inc=LG, inc_k=16)
                ld_wg[(nt_tq, nt_m)] = p.op(
                    "gpsimd", lambda e, nt_tq=nt_tq, nt_m=nt_m, wb=wb: e.dma_start(
                        wub[wb][:], wu_r[:, :, nt_m, :]), inc=LG, inc_k=16)
        h_all = VE.n
        for t2 in range(2):
            tgi = tq * 2 + t2
            bj = tgi % 2
            sl2 = slice(tgi * TC, (tgi + 1) * TC)
            wld = []
            if tgi >= 2:
                wld.append((ST, st_done[tgi - 2]))
            s1x1_ld[tgi] = p.op("sync", lambda e, bj=bj, sl2=sl2: e.dma_start(
                s1t[bj][:], T["s1"][:, :, sl2]), waits=wld, inc=LD, inc_k=16)
            s1x1_ld[tgi] = p.op("sync", lambda e, bj=bj, sl2=sl2: e.dma_start(
                x1t[bj][:], T["x1"][:, :, sl2]), inc=LD, inc_k=16)
            for dc in range(DC):
                wb2 = dc % 2
                pu = (tgi, dc - 2) if dc >= 2 else (tgi - 1, DC - 2 + dc)
                wpre = [(MM, ev_d[pu][0])] if pu in ev_d else []
                ldw = p.op("gpsimd", lambda e, dc=dc, wb2=wb2: e.dma_start(
                    wdb[wb2][:], wd_r[:, :, dc, :]), waits=wpre, inc=LG, inc_k=16)
                wmm = [(LG, ldw), (VE, h_all)]
                if prev_g1[wb2] is not None:
                    wmm.append((SC, prev_g1[wb2]))
                for k in range(M1):
                    dmm = p.op("tensor", lambda e, k=k, dc=dc, wb2=wb2, t2=t2:
                               e.matmul(ps_d[wb2][:], wdb[wb2][:, k, :],
                                        hb[:, k, t2 * TC:(t2 + 1) * TC],
                                        start=(k == 0), stop=(k == M1 - 1)),
                               waits=wmm if k == 0 else [],
                               inc=MM if k == M1 - 1 else None)
                wg1 = [(MM, dmm)]
                if last_v2 is not None:
                    wg1.append((VE, last_v2))
                g1_t = p.op("scalar", lambda e, wb2=wb2: e.activation(
                    g1b[:], ps_d[wb2][:], AF.Silu, scale=1.0 / 127.0),
                    waits=wg1, inc=SC)
                prev_g1[wb2] = g1_t
                v1 = p.op("vector", lambda e, dc=dc, bj=bj: e.tensor_tensor(
                    g1b[:], s1t[bj][:, dc, :], g1b[:], ALU.mult),
                    waits=[(SC, g1_t), (LD, s1x1_ld[tgi])], inc=VE)
                wv2 = [(VE, v1)]
                if tgi >= 2 and dc == 0:
                    wv2.append((ST, st_done[tgi - 2]))
                v2 = p.op("vector", lambda e, dc=dc, bj=bj: e.tensor_tensor(
                    ost[bj][:, dc, :], x1t[bj][:, dc, :], g1b[:], ALU.add),
                    waits=wv2, inc=VE)
                last_v2 = v2
                ev_d[(tgi, dc)] = (MM.n, v2)
            st_done[tgi] = p.op("sync", lambda e, bj=bj, sl2=sl2: e.dma_start(
                T["out"][:, :, sl2], ost[bj][:]),
                waits=[(VE, ev_d[(tgi, DC - 1)][1])], inc=ST, inc_k=16)
        mm_tq[tq] = MM.n
    p.wait("sync", [(ST, ST.n)])


def build_program(phase_names, want_outputs=()):
    nc = bass.Bass("TRN2", target_bir_lowering=False)
    phase_names = [ph for ph in PHASE_ORDER if ph in phase_names]
    reads = set()
    writes = set()
    for ph in phase_names:
        r, w = PHASES[ph]
        reads |= set(r)
        writes |= set(w)

    T = {}
    T["xT"] = nc.dram_tensor("xT", [128, DC, S], F32, kind="ExternalInput")
    if "ffn1" in phase_names:
        T["out"] = nc.dram_tensor("out", [128, DC, S], F32, kind="ExternalOutput")
    for name, (shape, dt_) in INTER.items():
        if name in writes:
            kind = "ExternalOutput" if (name in want_outputs) else "Internal"
            T[name] = nc.dram_tensor(name, shape, dt_, kind=kind)
        elif name in reads:
            T[name] = nc.dram_tensor(name, shape, dt_, kind="ExternalInput")
    for name, (shape, dt_) in WEIGHT_SPECS.items():
        T[name] = nc.dram_tensor(name, shape, dt_, kind="ExternalInput")

    AR = Arena(nc, "AR", 195 * 1024)
    SM = Arena(nc, "SM", 8 * 1024)
    PS = [nc.alloc_psum_tensor(f"ps{i}", [128, TC], F32) for i in range(8)]
    p = Prog(nc)

    first = True
    for ph in phase_names:
        if not first:
            p.barrier()
        first = False
        if ph == "ln0":
            emit_ln(p, T, AR, PS, "xT", "xn0", f32_dst="xn0f")
        elif ph == "ln1":
            emit_ln(p, T, AR, PS, "x1", "xn1")
        elif ph == "attn0":
            emit_attn0(p, T, AR, SM, PS)
        elif ph == "router":
            emit_router(p, T, AR, SM, PS)
        elif ph == "moe":
            emit_moe(p, T, AR, SM, PS)
        elif ph == "comb0":
            emit_comb0(p, T, AR, PS)
        elif ph == "attn1":
            emit_attn1(p, T, AR, SM, PS)
        elif ph == "ffn1":
            emit_ffn1(p, T, AR, SM, PS)
        else:
            raise ValueError(ph)
    p.realize()
    return nc


def _in_map_for_core(inp, w, consts, b, needed):
    m = {}
    full = dict(w)
    full.update(consts)
    for name in needed:
        if name == "xT":
            m["xT"] = _host_x(inp, b)
        elif name in full:
            m[name] = full[name]
    return m


def kernel(**inputs):
    w = _host_weights(inputs)
    consts = _host_constants()
    nc = build_program(PHASE_ORDER)
    needed = ["xT"] + list(WEIGHT_SPECS.keys())
    in_maps = [_in_map_for_core(inputs, w, consts, b, needed) for b in range(NCORE)]
    res = run_bass_kernel_spmd(nc, in_maps, core_ids=list(range(NCORE)))
    out = np.empty((NCORE, S, D), dtype=np.float32)
    for b in range(NCORE):
        o = res.results[b]["out"]
        out[b] = o.transpose(1, 0, 2).reshape(D, S).T
    return out


# version 06
# speedup vs baseline: 1.3096x; 1.3096x over previous
import math
import numpy as np
import ml_dtypes
import concourse.bass as bass
import concourse.mybir as mybir
from concourse.bass_utils import run_bass_kernel_spmd

F32 = mybir.dt.float32
F32R = mybir.dt.float32r
BF16 = mybir.dt.bfloat16
I32 = mybir.dt.int32
U8 = mybir.dt.uint8
AF = mybir.ActivationFunctionType
AX = mybir.AxisListType
ALU = mybir.AluOpType

NCORE = 8
S = 4096
D = 1024
DC = D // 128
H = 16
HD = 64
TC = 512
NT = S // TC
MOE_I = 4096
NEXP = 9
HH = 2048
FF1 = 3072
KVL = 128
EXP_SHIFT = 3.0

ENGINES = ("sync", "scalar", "vector", "tensor", "gpsimd")


def _tile_vec(v, p=128):
    n = v.shape[0]
    return np.ascontiguousarray(v.reshape(n // p, p).T)


def _host_constants():
    half = HD // 2
    inv_freq = (1.0 / (10000.0 ** (np.arange(half, dtype=np.float32) / half)))
    t = np.arange(S, dtype=np.float32)
    freqs = t[:, None] * inv_freq[None, :]
    cosT = np.cos(freqs).T.astype(np.float32)
    sinT = np.sin(freqs).T.astype(np.float32)
    CC4 = np.ascontiguousarray(np.concatenate([cosT] * 4, axis=0))
    SS4 = np.ascontiguousarray(
        np.concatenate([-sinT, sinT, -sinT, sinT], axis=0))
    i = np.arange(128)[:, None]
    j = np.arange(TC)[None, :]
    masks = np.stack([(i <= j - 128 * r).astype(np.float32) for r in range(4)])
    ident32 = np.eye(128, dtype=np.float32)
    ident16 = np.eye(128, dtype=ml_dtypes.bfloat16)
    return {
        "cc4": CC4, "ss4": SS4, "masks": np.ascontiguousarray(masks),
        "ident32": ident32, "ident16": ident16,
    }


def _host_weights(inp):
    f = np.float32
    w = {}
    w["wqkvT"] = np.ascontiguousarray(inp["L0_attn_w_qkv"].T)
    w["woutT"] = np.ascontiguousarray(inp["L0_attn_w_out"].T)
    w["wr1"] = np.ascontiguousarray(inp["L0_ffn_Wr1"])
    w["br1t"] = _tile_vec(inp["L0_ffn_br1"])
    w["wr2"] = np.ascontiguousarray(inp["L0_ffn_Wr2"])
    w["br2"] = np.ascontiguousarray(inp["L0_ffn_br2"].reshape(8, 1))
    w["ws1"] = np.ascontiguousarray(inp["L0_ffn_Ws1"])
    w["bs1t"] = _tile_vec(inp["L0_ffn_bs1"])
    w["ws2"] = np.ascontiguousarray(inp["L0_ffn_Ws2"])
    w["bs2t"] = _tile_vec(inp["L0_ffn_bs2"])
    w["we1"] = np.ascontiguousarray(inp["L0_ffn_We1"])
    w["be1t"] = np.ascontiguousarray(
        np.stack([_tile_vec(inp["L0_ffn_be1"][e]) for e in range(8)]))
    w["we2"] = np.ascontiguousarray(inp["L0_ffn_We2"])
    w["be2t"] = np.ascontiguousarray(
        np.stack([_tile_vec(inp["L0_ffn_be2"][e]) for e in range(8)]))
    w["wdqT"] = np.ascontiguousarray(inp["L1_attn_W_dq"].T)
    w["wdq"] = np.ascontiguousarray(inp["L1_attn_W_dq"])
    w["wuq"] = np.ascontiguousarray(inp["L1_attn_W_uq"])
    w["wuqr"] = np.ascontiguousarray(
        inp["L1_attn_W_uq"].reshape(16, 64, 128).transpose(1, 0, 2))
    w["wdkvT"] = np.ascontiguousarray(inp["L1_attn_W_dkv"].T)
    w["wuk"] = np.ascontiguousarray(inp["L1_attn_W_uk"])
    w["wuv"] = np.ascontiguousarray(inp["L1_attn_W_uv"])
    w["woT"] = np.ascontiguousarray(inp["L1_attn_W_o"].T)
    w["wgT"] = np.ascontiguousarray(inp["L1_ffn_w_gate"].T)
    w["wuT"] = np.ascontiguousarray(inp["L1_ffn_w_up"].T)
    w["wdT"] = np.ascontiguousarray(inp["L1_ffn_w_down"].T)
    return w


def _host_x(inp, b):
    xb = inp["L0_attn_x"][b]
    return np.ascontiguousarray(xb.T.reshape(DC, 128, S).transpose(1, 0, 2))


INTER = {
    "xn0": ([128, DC, S], BF16),
    "xn0f": ([128, DC, S], F32),
    "ctx0": ([128, DC, S], BF16),
    "s0": ([128, DC, S], BF16),
    "w9": ([NEXP, S], F32),
    "acc": ([128, DC, S], F32),
    "x1": ([128, DC, S], F32),
    "xn1": ([128, DC, S], BF16),
    "s1": ([128, DC, S], BF16),
}

WEIGHT_SPECS = {
    "wqkvT": ([D, 3 * D], F32), "woutT": ([D, D], F32),
    "wr1": ([D, 256], F32), "br1t": ([128, 2], F32),
    "wr2": ([256, 8], F32), "br2": ([8, 1], F32),
    "ws1": ([D, MOE_I], F32), "bs1t": ([128, 32], F32),
    "ws2": ([MOE_I, D], F32), "bs2t": ([128, 8], F32),
    "we1": ([8, D, MOE_I], F32), "be1t": ([8, 128, 32], F32),
    "we2": ([8, MOE_I, D], F32), "be2t": ([8, 128, 8], F32),
    "wdqT": ([D, KVL], F32), "wdq": ([KVL, D], F32),
    "wuq": ([D, KVL], F32), "wuqr": ([64, 16, KVL], F32),
    "wdkvT": ([D, KVL], F32),
    "wuk": ([D, KVL], F32), "wuv": ([D, KVL], F32), "woT": ([D, D], F32),
    "wgT": ([D, FF1], I32), "wuT": ([D, FF1], I32), "wdT": ([FF1, D], I32),
    "cc4": ([128, S], F32), "ss4": ([128, S], F32),
    "masks": ([4, 128, TC], F32),
    "ident32": ([128, 128], F32), "ident16": ([128, 128], BF16),
}

PHASES = {
    "ln0": ([], ["xn0", "xn0f"]),
    "attn0": (["xn0"], ["s0"]),
    "router": (["xn0f"], ["w9"]),
    "moe": (["xn0", "w9"], ["acc"]),
    "comb0": (["s0", "acc"], ["x1"]),
    "ln1": (["x1"], ["xn1"]),
    "attn1": (["xn1"], ["s1"]),
    "ffn1": (["xn1", "x1", "s1"], []),
}
PHASE_ORDER = ["ln0", "attn0", "router", "moe", "comb0", "ln1", "attn1", "ffn1"]


class CSem:
    def __init__(self, nc, name):
        self.h = nc.alloc_semaphore(name)
        self.n = 0


class Prog:
    def __init__(self, nc):
        self.nc = nc
        self.q = {e: [] for e in ENGINES}
        self.sems = {}

    def sem(self, name):
        if name not in self.sems:
            self.sems[name] = CSem(self.nc, name)
        return self.sems[name]

    def emit(self, eng, fn):
        self.q[eng].append(fn)

    def op(self, eng, call, waits=(), inc=None, inc_k=1):
        tgt = None
        if inc is not None:
            inc.n += inc_k
            tgt = inc.n
        wl = [(cs.h, v) for (cs, v) in waits]
        if inc is not None:
            ih, iv = inc.h, inc_k

            def fn(e, call=call, wl=wl, ih=ih, iv=iv):
                for h, v in wl:
                    e.wait_ge(h, v)
                call(e).then_inc(ih, iv)
        else:
            def fn(e, call=call, wl=wl):
                for h, v in wl:
                    e.wait_ge(h, v)
                call(e)
        self.emit(eng, fn)
        return tgt

    def wait(self, eng, waits):
        wl = [(cs.h, v) for (cs, v) in waits]

        def fn(e, wl=wl):
            for h, v in wl:
                e.wait_ge(h, v)
        self.emit(eng, fn)

    def barrier(self):
        bar = self.sem("BAR")
        bar.n += len(ENGINES)
        tgt = bar.n
        for eng in ENGINES:
            def fn(e, h=bar.h, t=tgt):
                e.drain().then_inc(h, 1)
                e.wait_ge(h, t)
            self.emit(eng, fn)

    def realize(self):
        with self.nc.Block() as blk:
            blk.sync(lambda e: [f(e) for f in self.q["sync"]])
            blk.scalar(lambda e: [f(e) for f in self.q["scalar"]])
            blk.vector(lambda e: [f(e) for f in self.q["vector"]])
            blk.tensor(lambda e: [f(e) for f in self.q["tensor"]])
            blk.gpsimd(lambda e: [f(e) for f in self.q["gpsimd"]])


class Arena:
    def __init__(self, nc, name, nbytes):
        self.nc = nc
        self.name = name
        self.t = nc.alloc_sbuf_tensor(name, [128, nbytes], U8)
        self.nbytes = nbytes
        self.addr = nc.lookup_mloc(self.t).addr
        self._n_alias = 0

    def alias(self, dtype, off, shape):
        esz = {F32: 4, F32R: 4, BF16: 2, I32: 4, U8: 1}[dtype]
        cols = 1
        for s_ in shape[1:]:
            cols *= s_
        assert off + cols * esz <= self.nbytes
        self._n_alias += 1
        t = self.nc.alloc_sbuf_tensor_at(
            f"{self.name}_al{self._n_alias}", [shape[0], cols], dtype,
            offset=self.addr + off)
        ap = t[:, :]
        if len(shape) == 3:
            ap = ap.rearrange("p (a b) -> p a b", a=shape[1], b=shape[2])
        elif len(shape) == 4:
            ap = ap.rearrange("p (a b c) -> p a b c",
                              a=shape[1], b=shape[2], c=shape[3])
        return ap

    def view(self, dtype, off, shape):
        esz = {F32: 4, F32R: 4, BF16: 2, I32: 4, U8: 1}[dtype]
        assert off % 64 == 0, off
        cols = 1
        for s_ in shape[1:]:
            cols *= s_
        assert off + cols * esz <= self.nbytes, (off, cols * esz, self.nbytes)
        ap = self.t.bitcast(dtype)[0:shape[0], off // esz: off // esz + cols]
        if len(shape) == 2:
            return ap
        if len(shape) == 3:
            return ap.rearrange("p (a b) -> p a b", a=shape[1], b=shape[2])
        if len(shape) == 4:
            return ap.rearrange("p (a b c) -> p a b c",
                                a=shape[1], b=shape[2], c=shape[3])
        raise ValueError(shape)


def emit_ln(p, T, AR, PS, src_name, dst_name, f32_dst=None):
    nc = p.nc
    LD = p.sem("LDS")
    ST = p.sem("STS")
    LG = p.sem("LDG")
    MM = p.sem("TEN")
    SC = p.sem("SCA")
    VE = p.sem("VEC")
    src, dst = T[src_name], T[dst_name]
    dstf = T[f32_dst] if f32_dst else None

    xt = [AR.view(F32, 0, [128, DC, TC]), AR.view(F32, 16384, [128, DC, TC])]
    x2 = AR.view(F32, 32768, [128, DC, TC])
    xo = [AR.view(F32, 49152, [128, DC, TC]), AR.view(F32, 81920, [128, DC, TC])]
    srow = AR.view(F32, 114688, [1, 6, TC])
    ones8 = AR.view(F32, 127232, [128, 8])
    ones_row = AR.view(F32, 127296, [1, 128])

    ps_sum = PS[0][0:8, :]
    ps_sq = PS[1][0:8, :]
    ps_nm = PS[2]
    ps_rs = PS[3]

    p.op("vector", lambda e: e.memset(ones8[:], 1.0), inc=VE)
    p.op("vector", lambda e: e.memset(ones_row[:], 1.0), inc=VE)
    ones_ready = VE.n

    st_base = ST.n
    ln_done = {}
    ln_store_lg = {}
    ln_store_st = {}
    prev_t1 = prev_t2 = prev_n2 = None
    for it in range(NT):
        buf = xt[it % 2]
        w = []
        if it >= 2:
            w.append((VE, ln_done[it - 2]))
        ld_t = p.op("sync", lambda e, b=buf, i=it: e.dma_start(
            b[:], src[:, :, i * TC:(i + 1) * TC]), waits=w, inc=LD, inc_k=16)
        wsq = [(LD, ld_t)]
        if prev_n2 is not None:
            wsq.append((VE, prev_n2))
        sq_t = p.op("vector", lambda e, b=buf: e.tensor_tensor(
            x2[:], b[:], b[:], ALU.mult), waits=wsq, inc=VE)
        for c in range(DC):
            w0 = [(LD, ld_t), (VE, ones_ready)]
            if c == 0 and prev_t1 is not None:
                w0.append((VE, prev_t1))
            p.op("tensor", lambda e, b=buf, c=c: e.matmul(
                ps_sum[:], ones8[:], b[:, c, :], start=(c == 0), stop=(c == DC - 1)),
                waits=w0 if c == 0 else [])
        for c in range(DC):
            last = c == DC - 1
            w0 = [(VE, sq_t)]
            if c == 0 and prev_t2 is not None:
                w0.append((VE, prev_t2))
            t = p.op("tensor", lambda e, c=c: e.matmul(
                ps_sq[:], ones8[:], x2[:, c, :], start=(c == 0), stop=(c == DC - 1)),
                waits=w0 if c == 0 else [],
                inc=MM if last else None)
        mm_stats = MM.n
        def rc(call, waits=()):
            return p.op("vector", call, waits=waits, inc=VE)
        t1 = rc(lambda e: e.tensor_scalar_mul(srow[:, 0, :], ps_sum[0:1, :], 1.0 / D),
                waits=[(MM, mm_stats)])
        t2 = rc(lambda e: e.tensor_scalar(srow[:, 1, :], ps_sq[0:1, :], 1.0 / D, None,
                                          ALU.mult), waits=[(VE, t1)])
        t3 = rc(lambda e: e.tensor_tensor(srow[:, 4, :], srow[:, 0, :], srow[:, 0, :],
                                          ALU.mult), waits=[(VE, t2)])
        t4 = rc(lambda e: e.tensor_tensor(srow[:, 1, :], srow[:, 1, :], srow[:, 4, :],
                                          ALU.subtract), waits=[(VE, t3)])
        t4b = rc(lambda e: e.tensor_scalar_add(srow[:, 1, :], srow[:, 1, :], 1e-5),
                 waits=[(VE, t4)])
        t5 = p.op("scalar", lambda e: e.activation(
            srow[:, 2, :], srow[:, 1, :], AF.Sqrt), waits=[(VE, t4b)], inc=SC)
        t6 = rc(lambda e: e.reciprocal(srow[:, 2, :], srow[:, 2, :]), waits=[(SC, t5)])
        t7 = rc(lambda e: e.tensor_scalar_mul(srow[:, 3, :], srow[:, 0, :], -1.0),
                waits=[(VE, t6)])
        wbc = [(VE, t7)]
        if prev_n2 is not None:
            wbc.append((VE, prev_n2))
        p.op("tensor", lambda e: e.matmul(
            ps_nm[:], ones_row[:], srow[:, 3, :], start=True, stop=True),
            waits=wbc)
        bc_t = p.op("tensor", lambda e: e.matmul(
            ps_rs[:], ones_row[:], srow[:, 2, :], start=True, stop=True),
            inc=MM)
        ob = xo[it % 2]
        wst = []
        if it >= 2:
            wst.append((LG, ln_store_lg[it - 2]))
            if dstf is not None:
                wst.append((ST, ln_store_st[it - 2]))
        n1 = p.op("vector", lambda e, b=buf, ob=ob: e.tensor_tensor(
            x2[:], b[:], ps_nm[:, None, :].broadcast_to([128, DC, TC]), ALU.add),
            waits=[(MM, bc_t)] + wst, inc=VE)
        n2 = p.op("vector", lambda e, ob=ob: e.tensor_tensor(
            ob[:], x2[:], ps_rs[:, None, :].broadcast_to([128, DC, TC]), ALU.mult),
            waits=[(VE, n1)], inc=VE)
        ln_done[it] = n2
        prev_t1, prev_t2, prev_n2 = t1, t2, n2
        ln_store_lg[it] = p.op("gpsimd", lambda e, ob=ob, i=it: e.dma_start(
            dst[:, :, i * TC:(i + 1) * TC], ob[:]), waits=[(VE, n2)], inc=LG, inc_k=16)
        if dstf is not None:
            ln_store_st[it] = p.op("sync", lambda e, ob=ob, i=it: e.dma_start(
                dstf[:, :, i * TC:(i + 1) * TC], ob[:]), waits=[(VE, n2)],
                inc=ST, inc_k=16)
    p.wait("sync", [(ST, ST.n)])
    p.wait("gpsimd", [(LG, LG.n)])


def emit_attn0(p, T, AR, SM, PS):
    nc = p.nc
    LD = p.sem("LDS")
    LG = p.sem("LDG")
    ST = p.sem("STS")
    MM = p.sem("TEN")
    SC = p.sem("SCA")
    VE = p.sem("VEC")
    PL = p.sem("PLC")

    o = [0]

    def al(n, a=64):
        o[0] = (o[0] + a - 1) // a * a
        r = o[0]
        o[0] += n
        return r

    xn = AR.view(BF16, al(DC * S * 2), [128, DC, S])
    ctxa = AR.view(BF16, al(DC * S * 2), [128, DC, S])
    scratch0 = o[0]
    cc4b = AR.view(BF16, al(S * 2), [128, S])
    ss4b = AR.view(BF16, al(S * 2), [128, S])
    qpb = AR.view(BF16, al(S * 2), [128, S])
    kpb = AR.view(BF16, al(S * 2), [128, S])
    vt = [AR.view(BF16, al(32 * 65 * 2), [128, 32, 65]) for _ in range(2)]
    wqb = [AR.view(BF16, al(DC * 128 * 2), [128, DC, 128]) for _ in range(2)]
    wkb = [AR.view(BF16, al(DC * 128 * 2), [128, DC, 128]) for _ in range(2)]
    wvb = [AR.view(BF16, al(DC * 128 * 2), [128, DC, 128]) for _ in range(2)]
    qsw2 = [AR.view(BF16, al(TC * 2), [128, TC]) for _ in range(2)]
    expT = AR.view(BF16, al(4 * TC * 2), [128, 4, TC])
    nrow = [SM.view(F32, 4096 + i * TC * 4, [1, TC]) for i in range(2)]
    bcb = [AR.view(F32, al(TC * 4), [64, TC]) for _ in range(2)]
    onesr = AR.view(F32, al(512), [1, 64])
    shift = AR.view(F32, al(64), [128, 1])
    msk = SM.view(BF16, 0, [128, 4, TC])
    assert o[0] <= AR.nbytes, o[0]
    o[0] = scratch0
    wo = AR.view(BF16, al(DC * DC * 128 * 2), [128, DC, DC, 128])
    s0st = [AR.view(BF16, al(DC * TC * 2), [128, DC, TC]) for _ in range(2)]
    assert o[0] <= AR.nbytes, o[0]

    ps_qk = [PS[0], PS[1]]
    ps_pv = [PS[2], PS[3]]
    ps_bc = [PS[4], PS[6]]
    ps_v = PS[5]
    ps_p = PS[7]

    p.op("vector", lambda e: e.memset(shift[:], -EXP_SHIFT), inc=VE)
    p.op("vector", lambda e: e.memset(onesr[:], 1.0), inc=VE)
    p.op("vector", lambda e: e.memset(vt[0][:, :, 64:65], 1.0), inc=VE)
    p.op("vector", lambda e: e.memset(vt[1][:, :, 64:65], 1.0), inc=VE)
    const_v = VE.n
    ld0 = p.op("sync", lambda e: e.dma_start(xn[:], T["xn0"][:]), inc=LD, inc_k=16)
    p.op("gpsimd", lambda e: e.dma_start(cc4b[:], T["cc4"][:]), inc=LG, inc_k=16)
    ld1 = p.op("gpsimd", lambda e: e.dma_start(ss4b[:], T["ss4"][:]),
               inc=LG, inc_k=16)
    ldm = p.op("gpsimd", lambda e: e.dma_start(
        msk[:], T["masks"].rearrange("r p t -> p r t")), inc=LG, inc_k=16)

    wqkvT = T["wqkvT"].rearrange("(c p) n -> p c n", p=128)

    wready = {}
    for dst, col0 in ((wqb[0], 0), (wkb[0], D), (wvb[0], 2 * D)):
        wready[0] = p.op("gpsimd", lambda e, d=dst, c=col0: e.dma_start(
            d[:], wqkvT[:, :, c:c + 128]), inc=LG, inc_k=16)

    prev_pv = {}
    pv_bank_free = {0: None, 1: None}
    nrow_free = {0: None, 1: None}
    bcb_free = {0: None, 1: None}
    bc_bank_free = {0: None, 1: None}
    qk_guard = {0: None, 1: None}
    qk_done = {0: None, 1: None}
    pend = [None]
    attn_end = {}
    proj_end = {}
    vev_prev = None
    ps_p_free = {0: None}
    pl_prev = {0: None, 1: None}
    tt2_prev = {0: None, 1: None}

    def flush_norm():
        qv2, base2, g2, qsl2, rc_cnt, _ = pend[0]
        pend[0] = None
        wbc = [(VE, rc_cnt)]
        if bc_bank_free[qv2] is not None:
            wbc.append((VE, bc_bank_free[qv2]))
        bc_t = p.op("tensor", lambda e, qv2=qv2: e.matmul(
            ps_bc[qv2][0:64, :], onesr[0:1, :], nrow[qv2][:],
            start=True, stop=True), waits=wbc, inc=MM)
        nrow_free[qv2] = bc_t
        wcp = [(MM, bc_t)]
        if bcb_free[qv2] is not None:
            wcp.append((VE, bcb_free[qv2]))
        cp_t = p.op("vector", lambda e, qv2=qv2: e.tensor_copy(
            bcb[qv2][:], ps_bc[qv2][0:64, :]), waits=wcp, inc=VE)
        bc_bank_free[qv2] = cp_t
        rcp_t = p.op("vector", lambda e, qv2=qv2: e.reciprocal(
            bcb[qv2][:], bcb[qv2][:]), waits=[(VE, cp_t)], inc=VE)
        tt_t = p.op("vector",
                    lambda e, base2=base2, g2=g2, qsl2=qsl2, qv2=qv2:
                    e.tensor_tensor(ctxa[base2:base2 + 64, g2, qsl2],
                                    ps_pv[qv2][0:64, :], bcb[qv2][:],
                                    ALU.mult),
                    waits=[(VE, rcp_t)], inc=VE)
        pv_bank_free[qv2] = tt_t
        bcb_free[qv2] = tt_t
        return tt_t

    for g in range(H // 2):
        wb = g % 2
        for wi, (wsl, dstp) in enumerate(((wqb[wb], qpb), (wkb[wb], kpb))):
            for ch in range(NT):
                cp = ch % 2
                sl = slice(ch * TC, (ch + 1) * TC)
                w0 = [(LD, ld0), (LG, wready[g])]
                if ps_p_free[0] is not None:
                    w0.append(ps_p_free[0])
                for c in range(DC):
                    mmt = p.op("tensor",
                               lambda e, c=c, sl=sl, wsl=wsl: e.matmul(
                                   ps_p[:], wsl[:, c, :], xn[:, c, sl],
                                   start=(c == 0), stop=(c == DC - 1)),
                               waits=w0 if c == 0 else [],
                               inc=MM if c == DC - 1 else None)
                    w0 = []
                wev = [(MM, mmt), (LG, ld1)]
                if tt2_prev[cp] is not None:
                    wev.append((VE, tt2_prev[cp]))
                for q_ in range(4):
                    src = (q_ ^ 1) * 32
                    p.op("vector",
                         lambda e, q_=q_, src=src, cp=cp, sl=sl:
                         e.tensor_tensor(
                             qsw2[cp][q_ * 32:(q_ + 1) * 32, :],
                             ps_p[src:src + 32, :],
                             ss4b[q_ * 32:(q_ + 1) * 32, sl], ALU.mult),
                         waits=wev if q_ == 0 else [], inc=VE)
                    wev = []
                wtt = []
                if wi == 0 and ch == 0 and g >= 1:
                    wtt.append((MM, attn_end[g - 1]))
                t1 = p.op("vector", lambda e, sl=sl, dstp=dstp:
                          e.tensor_tensor(dstp[:, sl], ps_p[:],
                                          cc4b[:, sl], ALU.mult),
                          waits=wtt, inc=VE)
                ps_p_free[0] = (VE, t1)
                t2 = p.op("vector", lambda e, cp=cp, sl=sl, dstp=dstp:
                          e.tensor_tensor(dstp[:, sl], dstp[:, sl],
                                          qsw2[cp][:], ALU.add),
                          waits=[(VE, t1)], inc=VE)
                tt2_prev[cp] = t2
        for tq in range(8):
            w0 = [(LD, ld0), (LG, wready[g])]
            if vev_prev is not None:
                w0.append((VE, vev_prev))
            for sub in range(4):
                for c in range(DC):
                    mmt = p.op("tensor",
                               lambda e, c=c, sub=sub, tq=tq, wb=wb: e.matmul(
                                   ps_v[:, sub * 128:(sub + 1) * 128],
                                   xn[:, c, tq * 512 + sub * 128:
                                      tq * 512 + sub * 128 + 128],
                                   wvb[wb][:, c, :],
                                   start=(c == 0), stop=(c == DC - 1)),
                               waits=w0 if (sub == 0 and c == 0) else [],
                               inc=MM if (sub == 3 and c == DC - 1) else None)
            pv4 = ps_v.rearrange("p (s n) -> p s n", s=4)
            wv0 = [(MM, mmt)]
            if g >= 1 and tq == 0:
                wv0.append((MM, attn_end[g - 1]))
            p.op("vector", lambda e, tq=tq: e.tensor_copy(
                vt[0][:, tq * 4:(tq + 1) * 4, 0:64], pv4[:, :, 0:64]),
                waits=wv0, inc=VE)
            vev_prev = p.op("vector", lambda e, tq=tq: e.tensor_copy(
                vt[1][:, tq * 4:(tq + 1) * 4, 0:64], pv4[:, :, 64:128]),
                inc=VE)
        vproj_done = vev_prev
        proj_end[g] = MM.n
        if g + 1 < H // 2:
            wg = (g + 1) % 2
            wld_guard = [(MM, proj_end[g - 1])] if g >= 1 else []
            for dst, col0 in ((wqb[wg], (g + 1) * 128),
                              (wkb[wg], D + (g + 1) * 128),
                              (wvb[wg], 2 * D + (g + 1) * 128)):
                wready[g + 1] = p.op("gpsimd", lambda e, d=dst, c=col0:
                                     e.dma_start(d[:], wqkvT[:, :, c:c + 128]),
                                     waits=wld_guard, inc=LG, inc_k=16)
                wld_guard = []

        for hh in range(2):
            base = hh * 64
            myvt = vt[hh]

            def emit_qk(ktc, qsl, extra):
                bank = ktc % 2
                psb = ps_qk[bank]
                wqk = list(extra)
                if qk_guard[bank] is not None:
                    wqk.append(qk_guard[bank])
                qk_done[bank] = p.op(
                    "tensor", lambda e, ktc=ktc, qsl=qsl, psb=psb, base=base:
                    e.matmul(psb[:],
                             kpb[base:base + 64, ktc * 128:(ktc + 1) * 128],
                             qpb[base:base + 64, qsl], start=True, stop=True),
                    waits=wqk, inc=MM)

            for qc in range(NT):
                nkt = 4 * (qc + 1)
                qv = qc % 2
                qsl = slice(qc * TC, (qc + 1) * TC)
                extra0 = []
                if qc == 0 and hh == 0:
                    extra0.append((VE, tt2_prev[1]))
                    if g == 0:
                        extra0.append((LG, ldm))
                        extra0.append((VE, const_v))
                emit_qk(0, qsl, extra0)
                if nkt > 1:
                    emit_qk(1, qsl, [])
                for ktc in range(nkt):
                    bank = ktc % 2
                    slot = ktc % 4
                    psb = ps_qk[bank]
                    if ktc == 1 and pend[0] is not None:
                        flush_norm()
                    wex = [(MM, qk_done[bank])]
                    if prev_pv.get(slot) is not None:
                        wex.append((MM, prev_pv[slot]))
                    ex_t = p.op("scalar", lambda e, slot=slot, psb=psb:
                                e.activation(expT[:, slot, :], psb[:], AF.Exp,
                                             bias=shift[:, 0:1], scale=0.125),
                                waits=wex, inc=SC)
                    qk_guard[bank] = (SC, ex_t)
                    r = ktc - 4 * qc
                    dep = (SC, ex_t)
                    if 0 <= r <= 3:
                        mk_t = p.op("vector", lambda e, slot=slot, r=r:
                                    e.tensor_tensor(expT[:, slot, :],
                                                    expT[:, slot, :],
                                                    msk[:, r, :], ALU.mult),
                                    waits=[(SC, ex_t)], inc=VE)
                        dep = (VE, mk_t)
                    wpv = [dep]
                    if ktc == 0:
                        if qc == 0 and hh == 0:
                            wpv.append((VE, vproj_done))
                        if pv_bank_free[qv] is not None:
                            wpv.append((VE, pv_bank_free[qv]))
                    pv_t = p.op("tensor",
                                lambda e, slot=slot, ktc=ktc, nkt=nkt, qv=qv,
                                myvt=myvt:
                                e.matmul(ps_pv[qv][0:65, :],
                                         myvt[:, ktc, 0:65], expT[:, slot, :],
                                         start=(ktc == 0), stop=(ktc == nkt - 1)),
                                waits=wpv, inc=MM)
                    prev_pv[slot] = pv_t
                    if ktc + 2 < nkt:
                        emit_qk(ktc + 2, qsl, [])
                wrc = [(MM, prev_pv[(nkt - 1) % 4])]
                if nrow_free[qv] is not None:
                    wrc.append((MM, nrow_free[qv]))
                rc_t = p.op("vector", lambda e, qv=qv: e.tensor_copy(
                    nrow[qv][:], ps_pv[qv][64:65, :]), waits=wrc, inc=VE)
                pend[0] = (qv, base, g, qsl, rc_t, False)
        attn_end[g] = MM.n
    final_tt = flush_norm()

    lw = p.op("gpsimd", lambda e: e.dma_start(
        wo[:], T["woutT"].rearrange("(fc p) (dc m) -> p fc dc m", p=128, m=128)),
        waits=[(MM, attn_end[H // 2 - 1])], inc=LG, inc_k=16)
    ev_done = {}
    st_done = {}
    for it in range(NT):
        bi = it % 2
        for dc in range(DC):
            bank = dc % 2
            wmm = []
            if it == 0 and dc == 0:
                wmm = [(LG, lw), (VE, final_tt)]
            prev = it * DC + dc - 2
            if prev >= 0:
                wmm.append((SC, ev_done[prev]))
            for fc in range(DC):
                mmt = p.op("tensor",
                           lambda e, fc=fc, dc=dc, it=it, bank=bank: e.matmul(
                               ps_qk[bank][:], wo[:, fc, dc, :],
                               ctxa[:, fc, it * TC:(it + 1) * TC],
                               start=(fc == 0), stop=(fc == DC - 1)),
                           waits=wmm if fc == 0 else [],
                           inc=MM if fc == DC - 1 else None)
                wmm = []
            wev = [(MM, mmt)]
            if it >= 2 and dc == 0:
                wev.append((ST, st_done[it - 2]))
            ev_done[it * DC + dc] = p.op(
                "scalar", lambda e, dc=dc, bi=bi, bank=bank: e.copy(
                    s0st[bi][:, dc, :], ps_qk[bank][:]), waits=wev, inc=SC)
        st_done[it] = p.op("sync", lambda e, it=it, bi=bi: e.dma_start(
            T["s0"][:, :, it * TC:(it + 1) * TC], s0st[bi][:]),
            waits=[(SC, ev_done[it * DC + DC - 1])], inc=ST, inc_k=16)
    p.wait("sync", [(ST, ST.n)])


def emit_router(p, T, AR, SM, PS):
    nc = p.nc
    LD = p.sem("LDS")
    LG = p.sem("LDG")
    ST = p.sem("STS")
    MM = p.sem("TEN")
    SC = p.sem("SCA")
    VE = p.sem("VEC")

    o = [0]

    def al(n, a=64):
        o[0] = (o[0] + a - 1) // a * a
        r = o[0]
        o[0] += n
        return r

    xnt = [AR.alias(F32R, al(DC * TC * 4), [128, DC, TC]) for _ in range(2)]
    r1 = AR.alias(F32R, al(2 * S * 4), [128, 2, S])
    wr1b = AR.alias(F32R, al(DC * 256 * 4), [128, DC, 256])
    wr2b = AR.alias(F32R, al(2 * 8 * 4), [128, 2, 8])
    br1b = AR.view(F32, al(2 * 4), [128, 2])
    br2b = AR.view(F32, al(4), [8, 1])
    lgrow = AR.view(F32, al(S * 4), [8, S])
    idf = AR.view(F32, al(128 * 4), [128, 128])
    gTM = AR.view(F32, al(32 * 8 * 4), [128, 32, 8])
    e8 = AR.view(F32, al(32 * 8 * 4), [128, 32, 8])
    m1 = AR.view(F32, al(32 * 8 * 4), [128, 32, 8])
    m2 = AR.view(F32, al(32 * 8 * 4), [128, 32, 8])
    w9tm = AR.view(F32, al(32 * 9 * 4), [128, 32, 9])
    cols = AR.view(F32, al(32 * 8 * 4), [128, 8, 32])
    w9sb = AR.view(F32, al(S * 4), [9, S])
    assert o[0] <= AR.nbytes

    ps_r = [PS[0], PS[1]]
    ps_l = PS[2][0:8, :]
    ps_t = PS[3]
    ps_t9 = PS[4][0:9, :]

    lw = p.op("gpsimd", lambda e: e.dma_start(
        wr1b[:], T["wr1"].rearrange("(c p) m -> p c m", p=128)), inc=LG, inc_k=16)
    lw = p.op("gpsimd", lambda e: e.dma_start(
        wr2b[:], T["wr2"].rearrange("(c p) m -> p c m", p=128)), inc=LG, inc_k=16)
    p.op("sync", lambda e: e.dma_start(br1b[:], T["br1t"][:]), inc=LD, inc_k=16)
    p.op("sync", lambda e: e.dma_start(br2b[:], T["br2"][:]), inc=LD, inc_k=16)
    ld_c = p.op("sync", lambda e: e.dma_start(idf[:], T["ident32"][:]),
                inc=LD, inc_k=16)

    ev_prev = {0: None, 1: None}
    ev_prev_tile = {}
    lg_prev = None
    for it in range(NT):
        bi = it % 2
        w0 = [(LG, lw), (LD, ld_c)]
        if it >= 2:
            w0.append((SC, ev_prev_tile[it - 2]))
        ld_t = p.op("gpsimd", lambda e, it=it, bi=bi: e.dma_start(
            xnt[bi][:], T["xn0f"][:, :, it * TC:(it + 1) * TC]),
            waits=w0, inc=LG, inc_k=16)
        for m_ in range(2):
            wmm = [(LD, ld_t)]
            if ev_prev[m_] is not None:
                wmm.append((SC, ev_prev[m_]))
            wmm = [(LG, ld_t)] + wmm[1:]
            for c in range(DC):
                mmt = p.op("tensor", lambda e, c=c, m_=m_, bi=bi: e.matmul(
                    ps_r[m_][:], wr1b[:, c, m_ * 128:(m_ + 1) * 128],
                    xnt[bi][:, c, :], start=(c == 0), stop=(c == DC - 1)),
                    waits=wmm if c == 0 else [], inc=MM if c == DC - 1 else None)
            ev_prev[m_] = p.op("scalar", lambda e, m_=m_, it=it: e.activation(
                r1[:, m_, it * TC:(it + 1) * TC], ps_r[m_][:], AF.Relu,
                bias=br1b[:, m_:m_ + 1]), waits=[(MM, mmt)], inc=SC)
        ev_prev_tile[it] = ev_prev[1]
        wl = [(SC, ev_prev[1])]
        if lg_prev is not None:
            wl.append((SC, lg_prev))
        for c2 in range(2):
            mmt = p.op("tensor", lambda e, c2=c2, it=it: e.matmul(
                ps_l[:], wr2b[:, c2, :], r1[:, c2, it * TC:(it + 1) * TC],
                start=(c2 == 0), stop=(c2 == 1)),
                waits=wl if c2 == 0 else [], inc=MM if c2 == 1 else None)
        lg_prev = p.op("scalar", lambda e, it=it: e.activation(
            lgrow[:, it * TC:(it + 1) * TC], ps_l[:], AF.Identity,
            bias=br2b[:, 0:1]), waits=[(MM, mmt)], inc=SC)

    tp_prev = None
    for grp in range(2):
        wtp = [(SC, lg_prev)]
        if tp_prev is not None:
            wtp.append((VE, tp_prev))
        for i in range(16):
            tt = grp * 16 + i
            mmt = p.op("tensor", lambda e, i=i, tt=tt: e.transpose(
                ps_t[:, i * 8:(i + 1) * 8], lgrow[:, tt * 128:(tt + 1) * 128],
                idf[0:8, 0:8]), waits=wtp if i == 0 else [],
                inc=MM if i == 15 else None)
        pt3 = ps_t.rearrange("p (i v) -> p i v", v=8)
        tp_prev = p.op("vector", lambda e, grp=grp: e.tensor_copy(
            gTM[:, grp * 16:(grp + 1) * 16, :], pt3[:, 0:16, :]),
            waits=[(MM, mmt)], inc=VE)

    def vop(call, waits=()):
        return p.op("vector", call, waits=waits, inc=VE)
    mx = cols[:, 0, :]
    e1c = cols[:, 1, :]
    e2c = cols[:, 2, :]
    Sc_ = cols[:, 3, :]
    rS = cols[:, 4, :]
    w1 = cols[:, 5, :]
    w2 = cols[:, 6, :]
    tc_ = cols[:, 7, :]
    t0 = vop(lambda e: e.reduce_max(mx, gTM[:], axis=AX.X), waits=[(VE, tp_prev)])
    t1 = vop(lambda e: e.tensor_tensor(
        e8[:], gTM[:], mx[:, :, None].broadcast_to([128, 32, 8]), ALU.subtract),
        waits=[(VE, t0)])
    t2 = p.op("scalar", lambda e: e.activation(
        e8.rearrange("p a b -> p (a b)"), e8.rearrange("p a b -> p (a b)"),
        AF.Exp), waits=[(VE, t1)], inc=SC)
    t3 = vop(lambda e: e.reduce_sum(Sc_, e8[:], axis=AX.X), waits=[(SC, t2)])
    t4 = vop(lambda e: e.tensor_tensor(
        m1[:], gTM[:], mx[:, :, None].broadcast_to([128, 32, 8]), ALU.is_ge),
        waits=[(SC, t2)])
    t5 = vop(lambda e: e.reduce_max(e1c, e8[:], axis=AX.X), waits=[(VE, t4)])
    t6 = vop(lambda e: e.tensor_scalar(m2[:], m1[:], -1.0, 1.0, ALU.mult, ALU.add),
             waits=[(VE, t4)])
    t7 = vop(lambda e: e.tensor_tensor(gTM[:], e8[:], m2[:], ALU.mult),
             waits=[(VE, t6)])
    t8 = vop(lambda e: e.reduce_max(e2c, gTM[:], axis=AX.X), waits=[(VE, t7)])
    t9 = vop(lambda e: e.tensor_tensor(
        m2[:], gTM[:], e2c[:, :, None].broadcast_to([128, 32, 8]), ALU.is_ge),
        waits=[(VE, t8)])
    t10 = vop(lambda e: e.tensor_tensor(tc_, e1c, e2c, ALU.subtract),
              waits=[(VE, t9), (VE, t5), (VE, t3)])
    t11 = vop(lambda e: e.reciprocal(rS, Sc_), waits=[(VE, t10)])
    t12 = vop(lambda e: e.tensor_tensor(tc_, tc_, rS, ALU.mult), waits=[(VE, t11)])
    t13 = vop(lambda e: e.tensor_scalar_mul(tc_, tc_, 0.5), waits=[(VE, t12)])
    t14 = p.op("scalar", lambda e: e.activation(w1, tc_, AF.Sigmoid),
               waits=[(VE, t13)], inc=SC)
    t15 = vop(lambda e: e.tensor_scalar(w2, w1, -1.0, 1.0, ALU.mult, ALU.add),
              waits=[(SC, t14)])
    t16 = vop(lambda e: e.memset(w9tm[:, :, 0:1], 1.0))
    t17 = vop(lambda e: e.tensor_tensor(
        m1[:], m1[:], w1[:, :, None].broadcast_to([128, 32, 8]), ALU.mult),
        waits=[(VE, t15)])
    t18 = vop(lambda e: e.tensor_tensor(
        m2[:], m2[:], w2[:, :, None].broadcast_to([128, 32, 8]), ALU.mult),
        waits=[(VE, t17)])
    t19 = vop(lambda e: e.tensor_tensor(w9tm[:, :, 1:9], m1[:], m2[:], ALU.add),
              waits=[(VE, t18)])

    tp_prev = None
    for grp in range(8):
        wtp = [(VE, t19)]
        if tp_prev is not None:
            wtp.append((SC, tp_prev))
        for i in range(4):
            tt = grp * 4 + i
            mmt = p.op("tensor", lambda e, i=i, tt=tt: e.transpose(
                ps_t9[:, i * 128:(i + 1) * 128],
                w9tm.rearrange("p a b -> p (a b)")[:, tt * 9:(tt + 1) * 9],
                idf[:, :]), waits=wtp if i == 0 else [],
                inc=MM if i == 3 else None)
        pt4 = ps_t9.rearrange("p (i v) -> p i v", v=128)
        tp_prev = p.op("scalar", lambda e, grp=grp: e.copy(
            w9sb.rearrange("p (i v) -> p i v", v=128)[:, grp * 4:(grp + 1) * 4, :],
            pt4[:, 0:4, :]), waits=[(MM, mmt)], inc=SC)
    p.op("sync", lambda e: e.dma_start(T["w9"][:], w9sb[:]),
         waits=[(SC, tp_prev)], inc=ST, inc_k=16)
    p.wait("sync", [(ST, ST.n)])


def emit_moe(p, T, AR, SM, PS):
    nc = p.nc
    LD = p.sem("LDS")
    LG = p.sem("LDG")
    MM = p.sem("TEN")
    SC = p.sem("SCA")
    VE = p.sem("VEC")

    o = [0]

    def al(n, a=64):
        o[0] = (o[0] + a - 1) // a * a
        r = o[0]
        o[0] += n
        return r

    we1 = [AR.view(BF16, al(DC * 16 * 128 * 2), [128, DC, 16, 128]) for _ in range(2)]
    we2 = [AR.view(BF16, al(16 * DC * 128 * 2), [128, 16, DC, 128]) for _ in range(2)]
    ab = AR.view(BF16, al(16 * TC * 2), [128, 16, TC])
    xnt = [AR.view(BF16, al(DC * TC * 2), [128, DC, TC]) for _ in range(2)]
    accst = [AR.view(F32, al(DC * TC * 4), [128, DC, TC]) for _ in range(2)]
    assert o[0] <= AR.nbytes, o[0]
    wbc = [SM.view(F32, i * 2048, [128, TC]) for i in range(2)]
    be1b = SM.view(F32, 4096, [128, 32])
    be2b = SM.view(F32, 4096 + 128, [128, 8])

    ps_a = [PS[0], PS[1]]
    ps_h = [PS[2], PS[3]]

    we1_r = T["we1"].rearrange("e (c p) (m q) -> e p c m q", p=128, q=128)
    we2_r = T["we2"].rearrange("e (k p) (d q) -> e p k d q", p=128, q=128)
    ws1_r = T["ws1"].rearrange("(c p) (m q) -> p c m q", p=128, q=128)
    ws2_r = T["ws2"].rearrange("(k p) (d q) -> p k d q", p=128, q=128)

    def w_srcs(ei, hh):
        if ei == 0:
            return (ws1_r[:, :, hh * 16:(hh + 1) * 16, :],
                    ws2_r[:, hh * 16:(hh + 1) * 16, :, :])
        return (we1_r[ei - 1, :, :, hh * 16:(hh + 1) * 16, :],
                we2_r[ei - 1, :, hh * 16:(hh + 1) * 16, :, :])

    steps = [(ei, hh) for ei in range(NEXP) for hh in range(2)]
    ld_w = {}
    mm_step = {}
    mm_tile = {}
    ac_t = {}
    sil_last = {}
    stt_last_h0 = {}
    prev_sil = {}
    prev_stt = {}
    prev_stt_tile_map = {}

    for si in range(2):
        ei, hh = steps[si]
        s1, s2 = w_srcs(ei, hh)
        p.op("gpsimd", lambda e, s1=s1, si=si: e.dma_start(we1[si % 2][:], s1),
             inc=LG, inc_k=16)
        ld_w[si] = p.op("gpsimd", lambda e, s2=s2, si=si: e.dma_start(
            we2[si % 2][:], s2), inc=LG, inc_k=16)

    bias_ld = {}
    p.op("sync", lambda e: e.dma_start(be1b[:], T["bs1t"][:]), inc=LD, inc_k=16)
    bias_ld[0] = p.op("sync", lambda e: e.dma_start(be2b[:], T["bs2t"][:]),
                      inc=LD, inc_k=16)

    for si, (ei, hh) in enumerate(steps):
        wb = si % 2
        if hh == 0 and ei > 0:
            wbias = [(SC, sil_last[si - 1]), (VE, stt_last_h0[ei - 1])]
            p.op("sync", lambda e, ei=ei: e.dma_start(
                be1b[:], T["be1t"][ei - 1]), waits=wbias, inc=LD, inc_k=16)
            bias_ld[ei] = p.op("sync", lambda e, ei=ei: e.dma_start(
                be2b[:], T["be2t"][ei - 1]), inc=LD, inc_k=16)
        sec_prev = None
        for tg in range(NT):
            tb = tg % 2
            sl = slice(tg * TC, (tg + 1) * TC)
            idx = si * NT + tg
            wxn = []
            pu = (si, tg - 2) if tg >= 2 else (si - 1, NT - 2 + tg)
            if pu in mm_tile:
                wxn.append((MM, mm_tile[pu]))
            ld_x = p.op("sync", lambda e, tb=tb, sl=sl: e.dma_start(
                xnt[tb][:], T["xn0"][:, :, sl]), waits=wxn, inc=LD, inc_k=16)
            wwb = []
            if pu in prev_stt_tile_map:
                wwb.append((VE, prev_stt_tile_map[pu]))
            ld_wb = p.op("gpsimd", lambda e, tb=tb, ei=ei, sl=sl: e.dma_start(
                wbc[tb][:], T["w9"][ei:ei + 1, sl].partition_broadcast(128)),
                waits=wwb, inc=LG, inc_k=16)
            for m_ in range(16):
                bank = m_ % 2
                wmm = [(LD, ld_x), (LG, ld_w[si])]
                if prev_sil.get(bank) is not None:
                    wmm.append((SC, prev_sil[bank]))
                for c in range(DC):
                    mmt = p.op("tensor",
                               lambda e, c=c, m_=m_, wb=wb, tb=tb, bank=bank:
                               e.matmul(ps_a[bank][:], we1[wb][:, c, m_, :],
                                        xnt[tb][:, c, :],
                                        start=(c == 0), stop=(c == DC - 1)),
                               waits=wmm if c == 0 else [],
                               inc=MM if c == DC - 1 else None)
                wsl = [(MM, mmt)]
                if m_ == 0:
                    wsl.append((LD, bias_ld[ei]))
                    if sec_prev is not None:
                        wsl.append((MM, sec_prev))
                prev_sil[bank] = p.op(
                    "scalar", lambda e, m_=m_, bank=bank, hh=hh: e.activation(
                        ab[:, m_, :], ps_a[bank][:], AF.Silu,
                        bias=be1b[:, hh * 16 + m_:hh * 16 + m_ + 1]),
                    waits=wsl, inc=SC)
            sil_all = SC.n
            stt_t = None
            for d in range(DC):
                bank = d % 2
                wmm = [(SC, sil_all)] if d == 0 else []
                if prev_stt.get(bank) is not None:
                    wmm.append((VE, prev_stt[bank]))
                for k in range(16):
                    mmt = p.op("tensor",
                               lambda e, k=k, d=d, wb=wb, bank=bank:
                               e.matmul(ps_h[bank][:], we2[wb][:, k, d, :],
                                        ab[:, k, :],
                                        start=(k == 0), stop=(k == 15)),
                               waits=wmm if k == 0 else [],
                               inc=MM if k == 15 else None)
                wev = [(MM, mmt), (LG, ld_wb)]
                if d == 0:
                    if (idx - 2) >= 0 and (idx - 2) in ac_t:
                        wev.append((LG, ac_t[idx - 2]))
                if hh == 0:
                    stt_t = p.op(
                        "vector",
                        lambda e, d=d, tb=tb, bank=bank, idx=idx:
                        e.scalar_tensor_tensor(
                            accst[idx % 2][:, d, :], ps_h[bank][:],
                            be2b[:, d:d + 1], wbc[tb][:], ALU.add, ALU.mult),
                        waits=wev, inc=VE)
                else:
                    stt_t = p.op(
                        "vector",
                        lambda e, d=d, tb=tb, bank=bank, idx=idx: e.tensor_tensor(
                            accst[idx % 2][:, d, :], ps_h[bank][:], wbc[tb][:],
                            ALU.mult), waits=wev, inc=VE)
                prev_stt[bank] = stt_t
            prev_stt_tile_map[(si, tg)] = stt_t
            op_ = ALU.bypass if si == 0 else ALU.add
            ac_t[idx] = p.op("gpsimd", lambda e, sl=sl, op_=op_, idx=idx:
                             e.dma_start(T["acc"][:, :, sl], accst[idx % 2][:],
                                         accum_op=op_),
                             waits=[(VE, stt_t)], inc=LG, inc_k=16)
            mm_tile[(si, tg)] = MM.n
            sec_prev = MM.n
        mm_step[si] = MM.n
        sil_last[si] = SC.n
        if hh == 0:
            stt_last_h0[ei] = VE.n
        if si + 2 < len(steps):
            nei, nhh = steps[si + 2]
            s1, s2 = w_srcs(nei, nhh)
            p.op("gpsimd", lambda e, s1=s1, wb=wb: e.dma_start(we1[wb][:], s1),
                 waits=[(MM, mm_step[si])], inc=LG, inc_k=16)
            ld_w[si + 2] = p.op("gpsimd", lambda e, s2=s2, wb=wb: e.dma_start(
                we2[wb][:], s2), inc=LG, inc_k=16)
    p.wait("gpsimd", [(LG, LG.n)])


def emit_comb0(p, T, AR, PS):
    LD = p.sem("LDS")
    ST = p.sem("STS")
    SC = p.sem("SCA")
    VE = p.sem("VEC")

    o = [0]

    def al(n, a=64):
        o[0] = (o[0] + a - 1) // a * a
        r = o[0]
        o[0] += n
        return r

    xt = [AR.view(F32, al(DC * TC * 4), [128, DC, TC]) for _ in range(2)]
    s0t = [AR.view(BF16, al(DC * TC * 2), [128, DC, TC]) for _ in range(2)]
    act = [AR.view(F32, al(DC * TC * 4), [128, DC, TC]) for _ in range(2)]
    gt = AR.view(F32, al(DC * TC * 4), [128, DC, TC])
    x1t = [AR.view(F32, al(DC * TC * 4), [128, DC, TC]) for _ in range(2)]
    assert o[0] <= AR.nbytes

    done_v = {}
    done_st = {}
    for it in range(NT):
        bi = it % 2
        sl = slice(it * TC, (it + 1) * TC)
        w0 = [(VE, done_v[it - 2])] if it >= 2 else []
        ld1 = p.op("sync", lambda e, bi=bi, sl=sl: e.dma_start(
            xt[bi][:], T["xT"][:, :, sl]), waits=w0, inc=LD, inc_k=16)
        ld2 = p.op("sync", lambda e, bi=bi, sl=sl: e.dma_start(
            s0t[bi][:], T["s0"][:, :, sl]), inc=LD, inc_k=16)
        ld3 = p.op("sync", lambda e, bi=bi, sl=sl: e.dma_start(
            act[bi][:], T["acc"][:, :, sl]), inc=LD, inc_k=16)
        wsi = [(LD, ld3)]
        if it >= 1:
            wsi.append((VE, done_v[it - 1]))
        si_t = p.op("scalar", lambda e, bi=bi: e.activation(
            gt.rearrange("p a b -> p (a b)"),
            act[bi].rearrange("p a b -> p (a b)"), AF.Silu),
            waits=wsi, inc=SC)
        v1 = p.op("vector", lambda e, bi=bi: e.tensor_tensor(
            gt[:], s0t[bi][:], gt[:], ALU.mult),
            waits=[(SC, si_t), (LD, ld2)], inc=VE)
        wv2 = [(VE, v1), (LD, ld1)]
        if it >= 2:
            wv2.append((ST, done_st[it - 2]))
        v2 = p.op("vector", lambda e, bi=bi: e.tensor_tensor(
            x1t[bi][:], xt[bi][:], gt[:], ALU.add), waits=wv2, inc=VE)
        done_v[it] = v2
        done_st[it] = p.op("sync", lambda e, bi=bi, sl=sl: e.dma_start(
            T["x1"][:, :, sl], x1t[bi][:]), waits=[(VE, v2)], inc=ST, inc_k=16)
    p.wait("sync", [(ST, ST.n)])


def emit_attn1(p, T, AR, SM, PS):
    nc = p.nc
    LD = p.sem("LDS")
    LG = p.sem("LDG")
    ST = p.sem("STS")
    MM = p.sem("TEN")
    SC = p.sem("SCA")
    VE = p.sem("VEC")

    o = [0]

    def al(n, a=64):
        o[0] = (o[0] + a - 1) // a * a
        r = o[0]
        o[0] += n
        return r

    Call = AR.view(BF16, al(16 * 128 * 2), [128, 16, 128])
    veffM = AR.view(BF16, al(16 * 64 * 2), [128, 16, 64])
    Gb = AR.view(BF16, al(128 * 2), [128, 128])
    Eh = AR.view(BF16, al(128 * 2), [64, 128])
    ckvb = AR.view(BF16, al(S * 2), [128, S])
    cqb = AR.view(BF16, al(S * 2), [128, S])
    qlb = [AR.view(BF16, al(S * 2), [128, S]) for _ in range(2)]
    wvh = [AR.view(BF16, al(32 * 65 * 2), [128, 32, 65]) for _ in range(2)]
    expT = AR.view(BF16, al(4 * TC * 2), [128, 4, TC])
    msk = SM.view(BF16, 0, [128, 4, TC])
    nrow = [AR.view(F32, al(TC * 4), [1, TC]) for _ in range(2)]
    bcb = [AR.view(F32, al(TC * 4), [64, TC]) for _ in range(2)]
    onesr = AR.view(F32, al(512), [1, 64])
    shift = AR.view(F32, al(64), [128, 1])
    wdqT_b = AR.view(BF16, al(DC * 128 * 2), [128, DC, 128])
    wdkvT_b = AR.view(BF16, al(DC * 128 * 2), [128, DC, 128])
    wdq_b = AR.view(BF16, al(D * 2), [128, D])
    wuq_b = AR.view(BF16, al(DC * 128 * 2), [128, DC, 128])
    wuqr_b = AR.view(BF16, al(16 * 128 * 2), [64, 16 * 128])
    wuk_b = AR.view(BF16, al(DC * 128 * 2), [128, DC, 128])
    wuv_b = AR.view(BF16, al(DC * 128 * 2), [128, DC, 128])
    woT_b = AR.view(BF16, al(DC * D * 2), [128, DC, D])
    xn1_off = al(DC * S * 2)
    xn1b = AR.view(BF16, xn1_off, [128, DC, S])
    s1st = AR.view(BF16, xn1_off, [128, DC, S])
    assert o[0] <= AR.nbytes, o[0]

    ps_qk = [PS[0], PS[1]]
    ps_pv = [PS[2], PS[3]]
    ps_bc = [PS[4], PS[6]]
    ps_wv = PS[5]
    ps_g = PS[7]

    lwt = None
    for dst, srcn, rr in (
            (wdqT_b, "wdqT", "(c p) m -> p c m"),
            (wdkvT_b, "wdkvT", "(c p) m -> p c m"),
            (wdq_b, "wdq", None), (wuq_b, "wuq", "(c p) m -> p c m"),
            (wuqr_b, "wuqr", "d h m -> d (h m)"),
            (wuk_b, "wuk", "(c p) m -> p c m"),
            (wuv_b, "wuv", "(c p) m -> p c m"),
            (woT_b, "woT", "(c p) m -> p c m")):
        srcap = T[srcn][:] if rr is None else T[srcn].rearrange(rr, **({"p": 128} if "p" in rr else {}))
        lwt = p.op("gpsimd", lambda e, d=dst, s=srcap: e.dma_start(d[:], s),
                   inc=LG, inc_k=16)
    ldm = p.op("gpsimd", lambda e: e.dma_start(
        msk[:], T["masks"].rearrange("r p t -> p r t")), inc=LG, inc_k=16)
    p.op("vector", lambda e: e.memset(onesr[:], 1.0), inc=VE)
    p.op("vector", lambda e: e.memset(shift[:], -EXP_SHIFT), inc=VE)
    p.op("vector", lambda e: e.memset(wvh[0][:, :, 64:65], 1.0), inc=VE)
    p.op("vector", lambda e: e.memset(wvh[1][:, :, 64:65], 1.0), inc=VE)
    const_v = VE.n
    ldx = p.op("sync", lambda e: e.dma_start(xn1b[:], T["xn1"][:]), inc=LD, inc_k=16)

    qk_guard = {0: None, 1: None}
    for wsl, dstp in ((wdqT_b, cqb), (wdkvT_b, ckvb)):
        for n in range(NT):
            bank = n % 2
            psb = ps_qk[bank]
            w0 = [(LD, ldx), (LG, lwt)]
            if qk_guard[bank] is not None:
                w0.append(qk_guard[bank])
            for c in range(DC):
                mmt = p.op("tensor", lambda e, c=c, n=n, psb=psb, wsl=wsl: e.matmul(
                    psb[:], wsl[:, c, :], xn1b[:, c, n * TC:(n + 1) * TC],
                    start=(c == 0), stop=(c == DC - 1)),
                    waits=w0 if c == 0 else [], inc=MM if c == DC - 1 else None)
            ev = p.op("scalar", lambda e, n=n, psb=psb, dstp=dstp: e.copy(
                dstp[:, n * TC:(n + 1) * TC], psb[:]), waits=[(MM, mmt)], inc=SC)
            qk_guard[bank] = (SC, ev)
    cqckv_done = qk_guard[1]

    for c in range(DC):
        mmt = p.op("tensor", lambda e, c=c: e.matmul(
            ps_g[:, 0:128], wuq_b[:, c, :], wuk_b[:, c, :],
            start=(c == 0), stop=(c == DC - 1)),
            waits=[(LG, lwt)] if c == 0 else [], inc=MM if c == DC - 1 else None)
    g_ev = p.op("vector", lambda e: e.tensor_copy(Gb[:], ps_g[:, 0:128]),
                waits=[(MM, mmt)], inc=VE)

    evC_guard = {0: None, 1: None}
    mmC_prev = None
    for h in range(H):
        hb = h % 2
        wE = [(VE, g_ev)]
        if mmC_prev is not None:
            wE.append((MM, mmC_prev))
        mmE = p.op("tensor", lambda e, h=h, hb=hb: e.matmul(
            ps_pv[hb][0:64, 0:128], wdq_b[:, h * 64:(h + 1) * 64], Gb[:],
            start=True, stop=True), waits=wE, inc=MM)
        evE = p.op("vector", lambda e, hb=hb: e.tensor_copy(
            Eh[:], ps_pv[hb][0:64, 0:128]), waits=[(MM, mmE)], inc=VE)
        wC = [(VE, evE)]
        if evC_guard[hb] is not None:
            wC.append((VE, evC_guard[hb]))
        mmC_prev = p.op("tensor", lambda e, h=h, hb=hb: e.matmul(
            ps_bc[hb][:, 0:128],
            wuqr_b[:, h * 128:(h + 1) * 128], Eh[:],
            start=True, stop=True), waits=wC, inc=MM)
        evC_guard[hb] = p.op("vector", lambda e, h=h, hb=hb: e.tensor_copy(
            Call[:, h, :], ps_bc[hb][:, 0:128]), waits=[(MM, mmC_prev)], inc=VE)

    for n2 in range(2):
        psb = ps_qk[n2]
        w0 = [qk_guard[n2]] if qk_guard[n2] is not None else []
        for c in range(DC):
            mmt = p.op("tensor", lambda e, c=c, n2=n2, psb=psb: e.matmul(
                psb[:], wuv_b[:, c, :], woT_b[:, c, n2 * TC:(n2 + 1) * TC],
                start=(c == 0), stop=(c == DC - 1)),
                waits=w0 if c == 0 else [], inc=MM if c == DC - 1 else None)
        ve = p.op("vector", lambda e, n2=n2, psb=psb: e.tensor_copy(
            veffM[:, n2 * 8:(n2 + 1) * 8, :],
            psb[:].rearrange("p (a b) -> p a b", b=64)), waits=[(MM, mmt)], inc=VE)
        qk_guard[n2] = (VE, ve)
    veff_done = qk_guard[1][1]

    prev_pv = {}
    pv_bank_free = {0: None, 1: None}
    invr_free = {0: None, 1: None}
    bcb_free = {0: None, 1: None}
    bc_bank_free = {0: None, 1: None}
    ql_read_done = {}
    wv_read_done = {}
    ql_ready = {}
    wv_ready = {}
    wv_evac_last = None
    qk_done = {0: None, 1: None}
    pend = [None]
    for h in range(H):
        hb2 = h % 2
        base = (h % 2) * 64
        chunk = h // 2
        for n in range(NT):
            bank = n % 2
            psb = ps_qk[bank]
            w0 = []
            if qk_guard[bank] is not None:
                w0.append(qk_guard[bank])
            mmt = p.op("tensor", lambda e, h=h, n=n, psb=psb: e.matmul(
                psb[:], Call[:, h, :], cqb[:, n * TC:(n + 1) * TC],
                start=True, stop=True), waits=w0, inc=MM)
            wev = [(MM, mmt)]
            if h >= 2:
                wev.append((MM, ql_read_done[h - 2]))
            ev = p.op("vector", lambda e, n=n, psb=psb, hb2=hb2: e.tensor_copy(
                qlb[hb2][:, n * TC:(n + 1) * TC], psb[:]), waits=wev, inc=VE)
            qk_guard[bank] = (VE, ev)
        ql_ready[h] = ev
        for grp in range(4):
            w0 = []
            if wv_evac_last is not None:
                w0.append((VE, wv_evac_last))
            for k8 in range(8):
                ktc = grp * 8 + k8
                mmt = p.op("tensor", lambda e, h=h, ktc=ktc, k8=k8: e.matmul(
                    ps_wv[:, k8 * 64:(k8 + 1) * 64],
                    ckvb[:, ktc * 128:(ktc + 1) * 128], veffM[:, h, :],
                    start=True, stop=True),
                    waits=w0 if k8 == 0 else [], inc=MM if k8 == 7 else None)
            wev = [(MM, mmt)]
            if h >= 2:
                wev.append((MM, wv_read_done[h - 2]))
            wv_evac_last = p.op("vector", lambda e, grp=grp, hb2=hb2: e.tensor_copy(
                wvh[hb2][:, grp * 8:(grp + 1) * 8, 0:64],
                ps_wv[:].rearrange("p (a b) -> p a b", b=64)),
                waits=wev, inc=VE)
        wv_ready[h] = wv_evac_last
        def emit_qk(ktc, qsl, extra):
            bank = ktc % 2
            psb = ps_qk[bank]
            wqk = list(extra)
            if qk_guard[bank] is not None:
                wqk.append(qk_guard[bank])
            qk_done[bank] = p.op(
                "tensor", lambda e, ktc=ktc, qsl=qsl, psb=psb, hb2=hb2:
                e.matmul(psb[:], ckvb[:, ktc * 128:(ktc + 1) * 128],
                         qlb[hb2][:, qsl], start=True, stop=True),
                waits=wqk, inc=MM)

        def flush_norm():
            qv2, base2, chunk2, qsl2, rc_cnt, do_store = pend[0]
            pend[0] = None
            wbc = [(VE, rc_cnt)]
            if bc_bank_free[qv2] is not None:
                wbc.append((VE, bc_bank_free[qv2]))
            bc_t = p.op("tensor", lambda e, qv2=qv2: e.matmul(
                ps_bc[qv2][0:64, :], onesr[0:1, :], nrow[qv2][:],
                start=True, stop=True), waits=wbc, inc=MM)
            invr_free[qv2] = bc_t
            wcp = [(MM, bc_t)]
            if bcb_free[qv2] is not None:
                wcp.append((VE, bcb_free[qv2]))
            cp_t = p.op("vector", lambda e, qv2=qv2: e.tensor_copy(
                bcb[qv2][:], ps_bc[qv2][0:64, :]), waits=wcp, inc=VE)
            bc_bank_free[qv2] = cp_t
            rcp_t = p.op("vector", lambda e, qv2=qv2: e.reciprocal(
                bcb[qv2][:], bcb[qv2][:]), waits=[(VE, cp_t)], inc=VE)
            tt_t = p.op("vector",
                        lambda e, base2=base2, chunk2=chunk2, qsl2=qsl2, qv2=qv2:
                        e.tensor_tensor(s1st[base2:base2 + 64, chunk2, qsl2],
                                        ps_pv[qv2][0:64, :], bcb[qv2][:],
                                        ALU.mult),
                        waits=[(VE, rcp_t)], inc=VE)
            pv_bank_free[qv2] = tt_t
            bcb_free[qv2] = tt_t
            if do_store:
                p.op("sync", lambda e, chunk2=chunk2: e.dma_start(
                    T["s1"][:, chunk2, :], s1st[:, chunk2, :]),
                    waits=[(VE, tt_t)], inc=ST, inc_k=16)

        for qc in range(NT):
            nkt = 4 * (qc + 1)
            qv = qc % 2
            qsl = slice(qc * TC, (qc + 1) * TC)
            extra0 = []
            if qc == 0:
                extra0.append((VE, ql_ready[h]))
                if h == 0:
                    extra0.append((LG, ldm))
                    extra0.append((VE, const_v))
            emit_qk(0, qsl, extra0)
            if nkt > 1:
                emit_qk(1, qsl, [])
            for ktc in range(nkt):
                bank = ktc % 2
                slot = ktc % 4
                psb = ps_qk[bank]
                if ktc == 1 and pend[0] is not None:
                    flush_norm()
                wex = [(MM, qk_done[bank])]
                if prev_pv.get(slot) is not None:
                    wex.append((MM, prev_pv[slot]))
                ex_t = p.op("scalar", lambda e, slot=slot, psb=psb: e.activation(
                    expT[:, slot, :], psb[:], AF.Exp, bias=shift[:, 0:1],
                    scale=0.125), waits=wex, inc=SC)
                qk_guard[bank] = (SC, ex_t)
                r = ktc - 4 * qc
                dep = (SC, ex_t)
                if 0 <= r <= 3:
                    mk_t = p.op("vector", lambda e, slot=slot, r=r: e.tensor_tensor(
                        expT[:, slot, :], expT[:, slot, :], msk[:, r, :],
                        ALU.mult), waits=[(SC, ex_t)], inc=VE)
                    dep = (VE, mk_t)
                wpv = [dep]
                if ktc == 0:
                    if qc == 0:
                        wpv.append((VE, wv_ready[h]))
                    if pv_bank_free[qv] is not None:
                        wpv.append((VE, pv_bank_free[qv]))
                pv_t = p.op("tensor",
                            lambda e, slot=slot, ktc=ktc, nkt=nkt, qv=qv, hb2=hb2:
                            e.matmul(ps_pv[qv][0:65, :], wvh[hb2][:, ktc, 0:65],
                                     expT[:, slot, :],
                                     start=(ktc == 0), stop=(ktc == nkt - 1)),
                            waits=wpv, inc=MM)
                prev_pv[slot] = pv_t
                if ktc + 2 < nkt:
                    emit_qk(ktc + 2, qsl, [])
            wrc = [(MM, pv_t)]
            if invr_free[qv] is not None:
                wrc.append((MM, invr_free[qv]))
            rc_t = p.op("vector", lambda e, qv=qv: e.tensor_copy(
                nrow[qv][:], ps_pv[qv][64:65, :]), waits=wrc, inc=VE)
            pend[0] = (qv, base, chunk, qsl, rc_t,
                       h % 2 == 1 and qc == NT - 1)
        ql_read_done[h] = MM.n
        wv_read_done[h] = MM.n
    if pend[0] is not None:
        flush_norm()
    p.wait("sync", [(ST, ST.n)])


def emit_ffn1(p, T, AR, SM, PS):
    nc = p.nc
    LD = p.sem("LDS")
    LG = p.sem("LDG")
    ST = p.sem("STS")
    MM = p.sem("TEN")
    SC = p.sem("SCA")
    VE = p.sem("VEC")
    TQ = 1024
    NTQ = S // TQ
    M1 = FF1 // 128

    o = [0]

    def al(n, a=64):
        o[0] = (o[0] + a - 1) // a * a
        r = o[0]
        o[0] += n
        return r

    hb = AR.view(BF16, al(M1 * TQ * 2), [128, M1, TQ])
    xnt = [AR.view(BF16, al(DC * TQ * 2), [128, DC, TQ]) for _ in range(2)]
    wgb = [AR.view(BF16, al(DC * 128 * 2), [128, DC, 128]) for _ in range(2)]
    wub = [AR.view(BF16, al(DC * 128 * 2), [128, DC, 128]) for _ in range(2)]
    wdb = [AR.view(BF16, al(M1 * 128 * 2), [128, M1, 128]) for _ in range(2)]
    sgb = AR.view(BF16, al(TC * 2), [128, TC])
    sub = AR.view(BF16, al(TC * 2), [128, TC])
    g1b = AR.view(F32, al(TC * 4), [128, TC])
    s1t = [AR.view(BF16, al(DC * TC * 2), [128, DC, TC]) for _ in range(2)]
    x1t = [AR.view(F32, al(DC * TC * 4), [128, DC, TC]) for _ in range(2)]
    ost = [AR.view(F32, al(DC * TC * 4), [128, DC, TC]) for _ in range(2)]
    assert o[0] <= AR.nbytes, o[0]

    wg_r = T["wgT"].rearrange("(c p) (m q) -> p c m q", p=128, q=128)
    wu_r = T["wuT"].rearrange("(c p) (m q) -> p c m q", p=128, q=128)
    wd_r = T["wdT"].rearrange("(k p) (d q) -> p k d q", p=128, q=128)

    ps_g = [PS[0], PS[1]]
    ps_u = [PS[2], PS[3]]
    ps_d = [PS[4], PS[5]]

    ld_xn = {}
    ld_wg = {}
    mm_h = {}
    mm_tq = {}
    h_done = {}
    ev_d = {}
    st_done = {}
    s1x1_ld = {}
    prev_sg = {0: None, 1: None}
    prev_su = {0: None, 1: None}
    prev_g1 = {0: None, 1: None}
    last_ht = None
    last_v2 = None

    for m in range(min(2, M1)):
        p.op("gpsimd", lambda e, m=m: e.dma_start(wgb[m % 2][:], wg_r[:, :, m, :]),
             inc=LG, inc_k=16)
        ld_wg[(0, m)] = p.op("gpsimd", lambda e, m=m: e.dma_start(
            wub[m % 2][:], wu_r[:, :, m, :]), inc=LG, inc_k=16)

    for tq in range(NTQ):
        bi = tq % 2
        sl = slice(tq * TQ, (tq + 1) * TQ)
        wxn = [(MM, mm_tq[tq - 2])] if tq >= 2 else []
        ld_xn[tq] = p.op("sync", lambda e, bi=bi, sl=sl: e.dma_start(
            xnt[bi][:], T["xn1"][:, :, sl]), waits=wxn, inc=LD, inc_k=16)
        for m in range(M1):
            wb = m % 2
            nt_tq, nt_m = (tq, m + 2) if m + 2 < M1 else (tq + 1, (m + 2) % M1)
            if (nt_tq, nt_m) not in ld_wg and nt_tq < NTQ:
                wpre = [(MM, mm_h[(tq, m)])] if (tq, m) in mm_h else []
            for n2 in range(2):
                wmm = [(LD, ld_xn[tq]), (LG, ld_wg[(tq, m) if tq > 0 or m > 1 else (0, m)])]
                if prev_sg[n2] is not None:
                    wmm.append((SC, prev_sg[n2]))
                for c in range(DC):
                    gmm = p.op("tensor", lambda e, c=c, m=m, n2=n2, wb=wb, bi=bi:
                               e.matmul(ps_g[n2][:], wgb[wb][:, c, :],
                                        xnt[bi][:, c, n2 * TC:(n2 + 1) * TC],
                                        start=(c == 0), stop=(c == DC - 1)),
                               waits=wmm if c == 0 else [],
                               inc=MM if c == DC - 1 else None)
                wmu = []
                if prev_su[n2] is not None:
                    wmu.append((VE, prev_su[n2]))
                for c in range(DC):
                    umm = p.op("tensor", lambda e, c=c, m=m, n2=n2, wb=wb, bi=bi:
                               e.matmul(ps_u[n2][:], wub[wb][:, c, :],
                                        xnt[bi][:, c, n2 * TC:(n2 + 1) * TC],
                                        start=(c == 0), stop=(c == DC - 1)),
                               waits=wmu if c == 0 else [],
                               inc=MM if c == DC - 1 else None)
                wsg = [(MM, gmm)]
                if last_ht is not None:
                    wsg.append((VE, last_ht))
                sg_t = p.op("scalar", lambda e, n2=n2: e.activation(
                    sgb[:], ps_g[n2][:], AF.Silu, scale=1.0 / 127.0),
                    waits=wsg, inc=SC)
                prev_sg[n2] = sg_t
                wsu = [(MM, umm)]
                if last_ht is not None:
                    wsu.append((VE, last_ht))
                su_t = p.op("vector", lambda e, n2=n2: e.tensor_scalar_mul(
                    sub[:], ps_u[n2][:], 1.0 / 127.0), waits=wsu, inc=VE)
                prev_su[n2] = su_t
                wht = [(VE, su_t), (SC, sg_t)]
                if m == 0 and n2 == 0 and tq >= 1:
                    wht.append((MM, mm_tq[tq - 1]))
                ht = p.op("vector", lambda e, m=m, n2=n2: e.tensor_tensor(
                    hb[:, m, n2 * TC:(n2 + 1) * TC], sgb[:], sub[:], ALU.mult),
                    waits=wht, inc=VE)
                h_done[(tq, m, n2)] = ht
                last_ht = ht
            mm_h[(tq, m)] = MM.n
            if (nt_tq, nt_m) not in ld_wg and nt_tq < NTQ:
                p.op("gpsimd", lambda e, nt_tq=nt_tq, nt_m=nt_m, wb=wb: e.dma_start(
                    wgb[wb][:], wg_r[:, :, nt_m, :]),
                    waits=[(MM, mm_h[(tq, m)])], inc=LG, inc_k=16)
                ld_wg[(nt_tq, nt_m)] = p.op(
                    "gpsimd", lambda e, nt_tq=nt_tq, nt_m=nt_m, wb=wb: e.dma_start(
                        wub[wb][:], wu_r[:, :, nt_m, :]), inc=LG, inc_k=16)
        h_all = VE.n
        for t2 in range(2):
            tgi = tq * 2 + t2
            bj = tgi % 2
            sl2 = slice(tgi * TC, (tgi + 1) * TC)
            wld = []
            if tgi >= 2:
                wld.append((ST, st_done[tgi - 2]))
            s1x1_ld[tgi] = p.op("sync", lambda e, bj=bj, sl2=sl2: e.dma_start(
                s1t[bj][:], T["s1"][:, :, sl2]), waits=wld, inc=LD, inc_k=16)
            s1x1_ld[tgi] = p.op("sync", lambda e, bj=bj, sl2=sl2: e.dma_start(
                x1t[bj][:], T["x1"][:, :, sl2]), inc=LD, inc_k=16)
            for dc in range(DC):
                wb2 = dc % 2
                pu = (tgi, dc - 2) if dc >= 2 else (tgi - 1, DC - 2 + dc)
                wpre = [(MM, ev_d[pu][0])] if pu in ev_d else []
                ldw = p.op("gpsimd", lambda e, dc=dc, wb2=wb2: e.dma_start(
                    wdb[wb2][:], wd_r[:, :, dc, :]), waits=wpre, inc=LG, inc_k=16)
                wmm = [(LG, ldw), (VE, h_all)]
                if prev_g1[wb2] is not None:
                    wmm.append((SC, prev_g1[wb2]))
                for k in range(M1):
                    dmm = p.op("tensor", lambda e, k=k, dc=dc, wb2=wb2, t2=t2:
                               e.matmul(ps_d[wb2][:], wdb[wb2][:, k, :],
                                        hb[:, k, t2 * TC:(t2 + 1) * TC],
                                        start=(k == 0), stop=(k == M1 - 1)),
                               waits=wmm if k == 0 else [],
                               inc=MM if k == M1 - 1 else None)
                wg1 = [(MM, dmm)]
                if last_v2 is not None:
                    wg1.append((VE, last_v2))
                g1_t = p.op("scalar", lambda e, wb2=wb2: e.activation(
                    g1b[:], ps_d[wb2][:], AF.Silu, scale=1.0 / 127.0),
                    waits=wg1, inc=SC)
                prev_g1[wb2] = g1_t
                v1 = p.op("vector", lambda e, dc=dc, bj=bj: e.tensor_tensor(
                    g1b[:], s1t[bj][:, dc, :], g1b[:], ALU.mult),
                    waits=[(SC, g1_t), (LD, s1x1_ld[tgi])], inc=VE)
                wv2 = [(VE, v1)]
                if tgi >= 2 and dc == 0:
                    wv2.append((ST, st_done[tgi - 2]))
                v2 = p.op("vector", lambda e, dc=dc, bj=bj: e.tensor_tensor(
                    ost[bj][:, dc, :], x1t[bj][:, dc, :], g1b[:], ALU.add),
                    waits=wv2, inc=VE)
                last_v2 = v2
                ev_d[(tgi, dc)] = (MM.n, v2)
            st_done[tgi] = p.op("sync", lambda e, bj=bj, sl2=sl2: e.dma_start(
                T["out"][:, :, sl2], ost[bj][:]),
                waits=[(VE, ev_d[(tgi, DC - 1)][1])], inc=ST, inc_k=16)
        mm_tq[tq] = MM.n
    p.wait("sync", [(ST, ST.n)])


def build_program(phase_names, want_outputs=()):
    nc = bass.Bass("TRN2", target_bir_lowering=False)
    phase_names = [ph for ph in PHASE_ORDER if ph in phase_names]
    reads = set()
    writes = set()
    for ph in phase_names:
        r, w = PHASES[ph]
        reads |= set(r)
        writes |= set(w)

    T = {}
    T["xT"] = nc.dram_tensor("xT", [128, DC, S], F32, kind="ExternalInput")
    if "ffn1" in phase_names:
        T["out"] = nc.dram_tensor("out", [128, DC, S], F32, kind="ExternalOutput")
    for name, (shape, dt_) in INTER.items():
        if name in writes:
            kind = "ExternalOutput" if (name in want_outputs) else "Internal"
            T[name] = nc.dram_tensor(name, shape, dt_, kind=kind)
        elif name in reads:
            T[name] = nc.dram_tensor(name, shape, dt_, kind="ExternalInput")
    for name, (shape, dt_) in WEIGHT_SPECS.items():
        T[name] = nc.dram_tensor(name, shape, dt_, kind="ExternalInput")

    AR = Arena(nc, "AR", 195 * 1024)
    SM = Arena(nc, "SM", 8 * 1024)
    PS = [nc.alloc_psum_tensor(f"ps{i}", [128, TC], F32) for i in range(8)]
    p = Prog(nc)

    first = True
    for ph in phase_names:
        if not first:
            p.barrier()
        first = False
        if ph == "ln0":
            emit_ln(p, T, AR, PS, "xT", "xn0", f32_dst="xn0f")
        elif ph == "ln1":
            emit_ln(p, T, AR, PS, "x1", "xn1")
        elif ph == "attn0":
            emit_attn0(p, T, AR, SM, PS)
        elif ph == "router":
            emit_router(p, T, AR, SM, PS)
        elif ph == "moe":
            emit_moe(p, T, AR, SM, PS)
        elif ph == "comb0":
            emit_comb0(p, T, AR, PS)
        elif ph == "attn1":
            emit_attn1(p, T, AR, SM, PS)
        elif ph == "ffn1":
            emit_ffn1(p, T, AR, SM, PS)
        else:
            raise ValueError(ph)
    p.realize()
    return nc


def _in_map_for_core(inp, w, consts, b, needed):
    m = {}
    full = dict(w)
    full.update(consts)
    for name in needed:
        if name == "xT":
            m["xT"] = _host_x(inp, b)
        elif name in full:
            m[name] = full[name]
    return m


def kernel(**inputs):
    w = _host_weights(inputs)
    consts = _host_constants()
    nc = build_program(PHASE_ORDER)
    needed = ["xT"] + list(WEIGHT_SPECS.keys())
    in_maps = [_in_map_for_core(inputs, w, consts, b, needed) for b in range(NCORE)]
    res = run_bass_kernel_spmd(nc, in_maps, core_ids=list(range(NCORE)))
    out = np.empty((NCORE, S, D), dtype=np.float32)
    for b in range(NCORE):
        o = res.results[b]["out"]
        out[b] = o.transpose(1, 0, 2).reshape(D, S).T
    return out


# version 11 (final)
# speedup vs baseline: 1.4111x; 1.0775x over previous
import math
import numpy as np
import ml_dtypes
import concourse.bass as bass
import concourse.mybir as mybir
from concourse.bass_utils import run_bass_kernel_spmd

F32 = mybir.dt.float32
F32R = mybir.dt.float32r
BF16 = mybir.dt.bfloat16
I32 = mybir.dt.int32
U8 = mybir.dt.uint8
AF = mybir.ActivationFunctionType
AX = mybir.AxisListType
ALU = mybir.AluOpType

NCORE = 8
S = 4096
D = 1024
DC = D // 128
H = 16
HD = 64
TC = 512
NT = S // TC
MOE_I = 4096
NEXP = 9
HH = 2048
FF1 = 3072
KVL = 128
EXP_SHIFT = 3.0

ENGINES = ("sync", "scalar", "vector", "tensor", "gpsimd")


def _tile_vec(v, p=128):
    n = v.shape[0]
    return np.ascontiguousarray(v.reshape(n // p, p).T)


def _host_constants():
    half = HD // 2
    inv_freq = (1.0 / (10000.0 ** (np.arange(half, dtype=np.float32) / half)))
    t = np.arange(S, dtype=np.float32)
    freqs = t[:, None] * inv_freq[None, :]
    cosT = np.cos(freqs).T.astype(np.float32)
    sinT = np.sin(freqs).T.astype(np.float32)
    CC4 = np.ascontiguousarray(np.concatenate([cosT] * 4, axis=0))
    SS4 = np.ascontiguousarray(
        np.concatenate([-sinT, sinT, -sinT, sinT], axis=0))
    i = np.arange(128)[:, None]
    j = np.arange(TC)[None, :]
    masks = np.stack([(i <= j - 128 * r).astype(np.float32) for r in range(4)])
    ident32 = np.eye(128, dtype=np.float32)
    ident16 = np.eye(128, dtype=ml_dtypes.bfloat16)
    return {
        "cc4": CC4, "ss4": SS4, "masks": np.ascontiguousarray(masks),
        "ident32": ident32, "ident16": ident16,
    }


def _host_weights(inp):
    f = np.float32
    w = {}
    w["wqkvT"] = np.ascontiguousarray(inp["L0_attn_w_qkv"].T)
    w["woutT"] = np.ascontiguousarray(inp["L0_attn_w_out"].T)
    w["wr1"] = np.ascontiguousarray(inp["L0_ffn_Wr1"])
    w["br1t"] = _tile_vec(inp["L0_ffn_br1"])
    w["wr2"] = np.ascontiguousarray(inp["L0_ffn_Wr2"])
    w["br2"] = np.ascontiguousarray(inp["L0_ffn_br2"].reshape(8, 1))
    w["ws1"] = np.ascontiguousarray(inp["L0_ffn_Ws1"])
    w["bs1t"] = _tile_vec(inp["L0_ffn_bs1"])
    w["ws2"] = np.ascontiguousarray(inp["L0_ffn_Ws2"])
    w["bs2t"] = _tile_vec(inp["L0_ffn_bs2"])
    w["we1"] = np.ascontiguousarray(inp["L0_ffn_We1"])
    w["be1t"] = np.ascontiguousarray(
        np.stack([_tile_vec(inp["L0_ffn_be1"][e]) for e in range(8)]))
    w["we2"] = np.ascontiguousarray(inp["L0_ffn_We2"])
    w["be2t"] = np.ascontiguousarray(
        np.stack([_tile_vec(inp["L0_ffn_be2"][e]) for e in range(8)]))
    w["wdqT"] = np.ascontiguousarray(inp["L1_attn_W_dq"].T)
    w["wdq"] = np.ascontiguousarray(inp["L1_attn_W_dq"])
    w["wuq"] = np.ascontiguousarray(inp["L1_attn_W_uq"])
    w["wuqr"] = np.ascontiguousarray(
        inp["L1_attn_W_uq"].reshape(16, 64, 128).transpose(1, 0, 2))
    w["wdkvT"] = np.ascontiguousarray(inp["L1_attn_W_dkv"].T)
    w["wuk"] = np.ascontiguousarray(inp["L1_attn_W_uk"])
    w["wuv"] = np.ascontiguousarray(inp["L1_attn_W_uv"])
    w["woT"] = np.ascontiguousarray(inp["L1_attn_W_o"].T)
    w["wgT"] = np.ascontiguousarray(inp["L1_ffn_w_gate"].T)
    w["wuT"] = np.ascontiguousarray(inp["L1_ffn_w_up"].T)
    w["wdT"] = np.ascontiguousarray(inp["L1_ffn_w_down"].T)
    return w


def _host_x(inp, b):
    xb = inp["L0_attn_x"][b]
    return np.ascontiguousarray(xb.T.reshape(DC, 128, S).transpose(1, 0, 2))


INTER = {
    "xn0": ([128, DC, S], BF16),
    "xn0f": ([128, DC, S], F32),
    "ctx0": ([128, DC, S], BF16),
    "s0": ([128, DC, S], BF16),
    "w9": ([NEXP, S], F32),
    "acc": ([128, DC, S], F32),
    "x1": ([128, DC, S], F32),
    "xn1": ([128, DC, S], BF16),
    "s1": ([128, DC, S], BF16),
}

WEIGHT_SPECS = {
    "wqkvT": ([D, 3 * D], F32), "woutT": ([D, D], F32),
    "wr1": ([D, 256], F32), "br1t": ([128, 2], F32),
    "wr2": ([256, 8], F32), "br2": ([8, 1], F32),
    "ws1": ([D, MOE_I], F32), "bs1t": ([128, 32], F32),
    "ws2": ([MOE_I, D], F32), "bs2t": ([128, 8], F32),
    "we1": ([8, D, MOE_I], F32), "be1t": ([8, 128, 32], F32),
    "we2": ([8, MOE_I, D], F32), "be2t": ([8, 128, 8], F32),
    "wdqT": ([D, KVL], F32), "wdq": ([KVL, D], F32),
    "wuq": ([D, KVL], F32), "wuqr": ([64, 16, KVL], F32),
    "wdkvT": ([D, KVL], F32),
    "wuk": ([D, KVL], F32), "wuv": ([D, KVL], F32), "woT": ([D, D], F32),
    "wgT": ([D, FF1], I32), "wuT": ([D, FF1], I32), "wdT": ([FF1, D], I32),
    "cc4": ([128, S], F32), "ss4": ([128, S], F32),
    "masks": ([4, 128, TC], F32),
    "ident32": ([128, 128], F32), "ident16": ([128, 128], BF16),
}

PHASES = {
    "ln0": ([], ["xn0", "xn0f"]),
    "attn0": (["xn0"], ["s0"]),
    "router": (["xn0f"], ["w9"]),
    "moe": (["xn0", "w9"], ["acc"]),
    "combln": (["s0", "acc"], ["x1", "xn1"]),
    "attn1": (["xn1"], ["s1"]),
    "ffn1": (["xn1", "x1", "s1"], []),
}
PHASE_ORDER = ["ln0", "attn0", "router", "moe", "combln", "attn1", "ffn1"]


class CSem:
    def __init__(self, nc, name):
        self.h = nc.alloc_semaphore(name)
        self.n = 0


class Prog:
    def __init__(self, nc):
        self.nc = nc
        self.q = {e: [] for e in ENGINES}
        self.sems = {}

    def sem(self, name):
        if name not in self.sems:
            self.sems[name] = CSem(self.nc, name)
        return self.sems[name]

    def emit(self, eng, fn):
        self.q[eng].append(fn)

    def op(self, eng, call, waits=(), inc=None, inc_k=1):
        tgt = None
        if inc is not None:
            inc.n += inc_k
            tgt = inc.n
        wl = [(cs.h, v) for (cs, v) in waits]
        if inc is not None:
            ih, iv = inc.h, inc_k

            def fn(e, call=call, wl=wl, ih=ih, iv=iv):
                for h, v in wl:
                    e.wait_ge(h, v)
                call(e).then_inc(ih, iv)
        else:
            def fn(e, call=call, wl=wl):
                for h, v in wl:
                    e.wait_ge(h, v)
                call(e)
        self.emit(eng, fn)
        return tgt

    def wait(self, eng, waits):
        wl = [(cs.h, v) for (cs, v) in waits]

        def fn(e, wl=wl):
            for h, v in wl:
                e.wait_ge(h, v)
        self.emit(eng, fn)

    def barrier(self):
        bar = self.sem("BAR")
        bar.n += len(ENGINES)
        tgt = bar.n
        for eng in ENGINES:
            def fn(e, h=bar.h, t=tgt):
                e.drain().then_inc(h, 1)
                e.wait_ge(h, t)
            self.emit(eng, fn)

    def realize(self):
        with self.nc.Block() as blk:
            blk.sync(lambda e: [f(e) for f in self.q["sync"]])
            blk.scalar(lambda e: [f(e) for f in self.q["scalar"]])
            blk.vector(lambda e: [f(e) for f in self.q["vector"]])
            blk.tensor(lambda e: [f(e) for f in self.q["tensor"]])
            blk.gpsimd(lambda e: [f(e) for f in self.q["gpsimd"]])


class Arena:
    def __init__(self, nc, name, nbytes):
        self.nc = nc
        self.name = name
        self.t = nc.alloc_sbuf_tensor(name, [128, nbytes], U8)
        self.nbytes = nbytes
        self.addr = nc.lookup_mloc(self.t).addr
        self._n_alias = 0

    def alias(self, dtype, off, shape):
        esz = {F32: 4, F32R: 4, BF16: 2, I32: 4, U8: 1}[dtype]
        cols = 1
        for s_ in shape[1:]:
            cols *= s_
        assert off + cols * esz <= self.nbytes
        self._n_alias += 1
        t = self.nc.alloc_sbuf_tensor_at(
            f"{self.name}_al{self._n_alias}", [shape[0], cols], dtype,
            offset=self.addr + off)
        ap = t[:, :]
        if len(shape) == 3:
            ap = ap.rearrange("p (a b) -> p a b", a=shape[1], b=shape[2])
        elif len(shape) == 4:
            ap = ap.rearrange("p (a b c) -> p a b c",
                              a=shape[1], b=shape[2], c=shape[3])
        return ap

    def view(self, dtype, off, shape):
        esz = {F32: 4, F32R: 4, BF16: 2, I32: 4, U8: 1}[dtype]
        assert off % 64 == 0, off
        cols = 1
        for s_ in shape[1:]:
            cols *= s_
        assert off + cols * esz <= self.nbytes, (off, cols * esz, self.nbytes)
        ap = self.t.bitcast(dtype)[0:shape[0], off // esz: off // esz + cols]
        if len(shape) == 2:
            return ap
        if len(shape) == 3:
            return ap.rearrange("p (a b) -> p a b", a=shape[1], b=shape[2])
        if len(shape) == 4:
            return ap.rearrange("p (a b c) -> p a b c",
                                a=shape[1], b=shape[2], c=shape[3])
        raise ValueError(shape)


def emit_ln(p, T, AR, PS, src_name, dst_name, f32_dst=None):
    nc = p.nc
    LD = p.sem("LDS")
    ST = p.sem("STS")
    LG = p.sem("LDG")
    MM = p.sem("TEN")
    SC = p.sem("SCA")
    VE = p.sem("VEC")
    src, dst = T[src_name], T[dst_name]
    dstf = T[f32_dst] if f32_dst else None

    xt = [AR.view(F32, 0, [128, DC, TC]), AR.view(F32, 16384, [128, DC, TC])]
    x2 = AR.view(F32, 32768, [128, DC, TC])
    xo = [AR.view(F32, 49152, [128, DC, TC]), AR.view(F32, 81920, [128, DC, TC])]
    srow = AR.view(F32, 114688, [1, 6, TC])
    ones8 = AR.view(F32, 127232, [128, 8])
    ones_row = AR.view(F32, 127296, [1, 128])

    ps_sum = PS[0][0:8, :]
    ps_sq = PS[1][0:8, :]
    ps_nm = PS[2]
    ps_rs = PS[3]

    p.op("vector", lambda e: e.memset(ones8[:], 1.0), inc=VE)
    p.op("vector", lambda e: e.memset(ones_row[:], 1.0), inc=VE)
    ones_ready = VE.n

    st_base = ST.n
    ln_done = {}
    ln_store_lg = {}
    ln_store_st = {}
    prev_t1 = prev_t2 = prev_n2 = None
    for it in range(NT):
        buf = xt[it % 2]
        w = []
        if it >= 2:
            w.append((VE, ln_done[it - 2]))
        ld_t = p.op("sync", lambda e, b=buf, i=it: e.dma_start(
            b[:], src[:, :, i * TC:(i + 1) * TC]), waits=w, inc=LD, inc_k=16)
        wsq = [(LD, ld_t)]
        if prev_n2 is not None:
            wsq.append((VE, prev_n2))
        sq_t = p.op("vector", lambda e, b=buf: e.tensor_tensor(
            x2[:], b[:], b[:], ALU.mult), waits=wsq, inc=VE)
        for c in range(DC):
            w0 = [(LD, ld_t), (VE, ones_ready)]
            if c == 0 and prev_t1 is not None:
                w0.append((VE, prev_t1))
            p.op("tensor", lambda e, b=buf, c=c: e.matmul(
                ps_sum[:], ones8[:], b[:, c, :], start=(c == 0), stop=(c == DC - 1)),
                waits=w0 if c == 0 else [])
        for c in range(DC):
            last = c == DC - 1
            w0 = [(VE, sq_t)]
            if c == 0 and prev_t2 is not None:
                w0.append((VE, prev_t2))
            t = p.op("tensor", lambda e, c=c: e.matmul(
                ps_sq[:], ones8[:], x2[:, c, :], start=(c == 0), stop=(c == DC - 1)),
                waits=w0 if c == 0 else [],
                inc=MM if last else None)
        mm_stats = MM.n
        def rc(call, waits=()):
            return p.op("vector", call, waits=waits, inc=VE)
        t1 = rc(lambda e: e.tensor_scalar_mul(srow[:, 0, :], ps_sum[0:1, :], 1.0 / D),
                waits=[(MM, mm_stats)])
        t2 = rc(lambda e: e.tensor_scalar(srow[:, 1, :], ps_sq[0:1, :], 1.0 / D, None,
                                          ALU.mult), waits=[(VE, t1)])
        t3 = rc(lambda e: e.tensor_tensor(srow[:, 4, :], srow[:, 0, :], srow[:, 0, :],
                                          ALU.mult), waits=[(VE, t2)])
        t4 = rc(lambda e: e.tensor_tensor(srow[:, 1, :], srow[:, 1, :], srow[:, 4, :],
                                          ALU.subtract), waits=[(VE, t3)])
        t4b = rc(lambda e: e.tensor_scalar_add(srow[:, 1, :], srow[:, 1, :], 1e-5),
                 waits=[(VE, t4)])
        t5 = p.op("scalar", lambda e: e.activation(
            srow[:, 2, :], srow[:, 1, :], AF.Sqrt), waits=[(VE, t4b)], inc=SC)
        t6 = rc(lambda e: e.reciprocal(srow[:, 2, :], srow[:, 2, :]), waits=[(SC, t5)])
        t7 = rc(lambda e: e.tensor_scalar_mul(srow[:, 3, :], srow[:, 0, :], -1.0),
                waits=[(VE, t6)])
        wbc = [(VE, t7)]
        if prev_n2 is not None:
            wbc.append((VE, prev_n2))
        p.op("tensor", lambda e: e.matmul(
            ps_nm[:], ones_row[:], srow[:, 3, :], start=True, stop=True),
            waits=wbc)
        bc_t = p.op("tensor", lambda e: e.matmul(
            ps_rs[:], ones_row[:], srow[:, 2, :], start=True, stop=True),
            inc=MM)
        ob = xo[it % 2]
        wst = []
        if it >= 2:
            wst.append((LG, ln_store_lg[it - 2]))
            if dstf is not None:
                wst.append((ST, ln_store_st[it - 2]))
        n1 = p.op("vector", lambda e, b=buf, ob=ob: e.tensor_tensor(
            x2[:], b[:], ps_nm[:, None, :].broadcast_to([128, DC, TC]), ALU.add),
            waits=[(MM, bc_t)] + wst, inc=VE)
        n2 = p.op("vector", lambda e, ob=ob: e.tensor_tensor(
            ob[:], x2[:], ps_rs[:, None, :].broadcast_to([128, DC, TC]), ALU.mult),
            waits=[(VE, n1)], inc=VE)
        ln_done[it] = n2
        prev_t1, prev_t2, prev_n2 = t1, t2, n2
        ln_store_lg[it] = p.op("gpsimd", lambda e, ob=ob, i=it: e.dma_start(
            dst[:, :, i * TC:(i + 1) * TC], ob[:]), waits=[(VE, n2)], inc=LG, inc_k=16)
        if dstf is not None:
            ln_store_st[it] = p.op("sync", lambda e, ob=ob, i=it: e.dma_start(
                dstf[:, :, i * TC:(i + 1) * TC], ob[:]), waits=[(VE, n2)],
                inc=ST, inc_k=16)
    p.wait("sync", [(ST, ST.n)])
    p.wait("gpsimd", [(LG, LG.n)])


def emit_attn0(p, T, AR, SM, PS, PSD):
    nc = p.nc
    LD = p.sem("LDS")
    LG = p.sem("LDG")
    ST = p.sem("STS")
    MM = p.sem("TEN")
    SC = p.sem("SCA")
    VE = p.sem("VEC")
    PL = p.sem("PLC")

    o = [0]

    def al(n, a=64):
        o[0] = (o[0] + a - 1) // a * a
        r = o[0]
        o[0] += n
        return r

    xn = AR.view(BF16, al(DC * S * 2), [128, DC, S])
    ctxa = AR.view(BF16, al(DC * S * 2), [128, DC, S])
    scratch0 = o[0]
    cc4b = AR.view(BF16, al(S * 2), [128, S])
    ss4b = AR.view(BF16, al(S * 2), [128, S])
    qpad = [AR.view(BF16, al(S * 2), [128, S]) for _ in range(2)]
    kpb = AR.view(BF16, al(S * 2), [128, S])
    vt = [AR.view(BF16, al(32 * 65 * 2), [128, 32, 65]) for _ in range(2)]
    wqb = [AR.view(BF16, al(DC * 128 * 2), [128, DC, 128]) for _ in range(2)]
    wkb = [AR.view(BF16, al(DC * 128 * 2), [128, DC, 128]) for _ in range(2)]
    wvb = [AR.view(BF16, al(DC * 128 * 2), [128, DC, 128]) for _ in range(2)]
    qsw2 = [AR.view(BF16, al(TC * 2), [128, TC]) for _ in range(2)]
    expT = AR.view(BF16, al(4 * TC * 2), [128, 4, TC])
    nrow = [SM.view(BF16, 4096 + i * TC * 2, [128, TC]) for i in range(2)]
    bcb = [AR.view(F32, al(TC * 4), [64, TC]),
           SM.view(F32, 6144, [64, TC])]
    onesP = AR.view(BF16, al(256), [128, 128])
    shift = AR.view(F32, al(64), [128, 1])
    msk = SM.view(BF16, 0, [128, 4, TC])

    assert o[0] <= AR.nbytes, o[0]
    o[0] = scratch0
    wo = AR.view(BF16, al(DC * DC * 128 * 2), [128, DC, DC, 128])
    s0st = [AR.view(BF16, al(DC * TC * 2), [128, DC, TC]) for _ in range(2)]
    assert o[0] <= AR.nbytes, o[0]

    qk4 = [[PS[0], PS[1]], [PS[6], PS[7]]]
    qkD = [PSD[0], PSD[3]]
    ps_pv = [PS[2], PS[3]]
    ps_bc1 = PS[4]
    ps_v = PS[5]
    ps_p = PS[0]
    expf = expT.rearrange("p a b -> p (a b)")

    p.op("vector", lambda e: e.memset(shift[:], -EXP_SHIFT), inc=VE)
    p.op("vector", lambda e: e.memset(onesP[:], 0.0), inc=VE)
    p.op("vector", lambda e: e.memset(onesP[0:1, :], 1.0), inc=VE)
    p.op("vector", lambda e: e.memset(nrow[0][:], 0.0), inc=VE)
    p.op("vector", lambda e: e.memset(nrow[1][:], 0.0), inc=VE)
    p.op("vector", lambda e: e.memset(qpad[0][64:128, :], 0.0), inc=VE)
    p.op("vector", lambda e: e.memset(qpad[1][0:64, :], 0.0), inc=VE)
    p.op("vector", lambda e: e.memset(vt[0][:, :, 64:65], 1.0), inc=VE)
    p.op("vector", lambda e: e.memset(vt[1][:, :, 64:65], 1.0), inc=VE)
    const_v = VE.n
    ld0 = p.op("sync", lambda e: e.dma_start(xn[:], T["xn0"][:]), inc=LD, inc_k=16)
    for wi_ in range(60):
        p.op("tensor", lambda e: e.matmul(
            PS[1][:, 0:128], onesP[:], onesP[:], start=True, stop=True),
            waits=[(VE, const_v)] if wi_ == 0 else [])
    p.op("gpsimd", lambda e: e.dma_start(cc4b[:], T["cc4"][:]), inc=LG, inc_k=16)
    ld1 = p.op("gpsimd", lambda e: e.dma_start(ss4b[:], T["ss4"][:]),
               inc=LG, inc_k=16)
    ldm = p.op("gpsimd", lambda e: e.dma_start(
        msk[:], T["masks"].rearrange("r p t -> p r t")), inc=LG, inc_k=16)

    wqkvT = T["wqkvT"].rearrange("(c p) n -> p c n", p=128)

    wready = {}
    for dst, col0 in ((wqb[0], 0), (wkb[0], D), (wvb[0], 2 * D)):
        wready[0] = p.op("gpsimd", lambda e, d=dst, c=col0: e.dma_start(
            d[:], wqkvT[:, :, c:c + 128]), inc=LG, inc_k=16)

    prev_pv = {}
    pv_bank_free = {0: None, 1: None}
    nrow_free = {0: None, 1: None}
    bcb_free = {0: None, 1: None}
    bc_bank_free = {0: None}
    qkd_guard = {0: None, 1: None}
    qk_pair_done = {0: None, 1: None}
    pend = [None]
    attn_end = {}
    proj_end = {}
    vev_prev = None
    ps_p_free = {0: None}
    tt2_prev = {0: None, 1: None}

    def flush_norm():
        qv2, base2, g2, qsl2, rc_cnt, _ = pend[0]
        pend[0] = None
        wbc = [(VE, rc_cnt)]
        if bc_bank_free[0] is not None:
            wbc.append((VE, bc_bank_free[0]))
        bc_t = p.op("tensor", lambda e, qv2=qv2: e.matmul(
            ps_bc1[:, :], onesP[:, :], nrow[qv2][:],
            start=True, stop=True), waits=wbc, inc=MM)
        nrow_free[qv2] = bc_t
        wcp = [(MM, bc_t)]
        if bcb_free[qv2] is not None:
            wcp.append((VE, bcb_free[qv2]))
        cp_t = p.op("vector", lambda e, qv2=qv2: e.tensor_copy(
            bcb[qv2][:], ps_bc1[0:64, :]), waits=wcp, inc=VE)
        bc_bank_free[0] = cp_t
        rcp_t = p.op("vector", lambda e, qv2=qv2: e.reciprocal(
            bcb[qv2][:], bcb[qv2][:]), waits=[(VE, cp_t)], inc=VE)
        tt_t = p.op("vector",
                    lambda e, base2=base2, g2=g2, qsl2=qsl2, qv2=qv2:
                    e.tensor_tensor(ctxa[base2:base2 + 64, g2, qsl2],
                                    ps_pv[qv2][0:64, :], bcb[qv2][:],
                                    ALU.mult),
                    waits=[(VE, rcp_t)], inc=VE)
        pv_bank_free[qv2] = tt_t
        bcb_free[qv2] = tt_t
        return tt_t

    for g in range(H // 2):
        wb = g % 2
        for wi, (wsl, dstp) in enumerate(((wqb[wb], None), (wkb[wb], kpb))):
            for ch in range(NT):
                cp = ch % 2
                sl = slice(ch * TC, (ch + 1) * TC)
                w0 = [(LD, ld0), (LG, wready[g])]
                if wi == 0 and ch == 0 and qkd_guard[0] is not None:
                    w0.append(qkd_guard[0])
                if ps_p_free[0] is not None:
                    w0.append(ps_p_free[0])
                for c in range(DC):
                    mmt = p.op("tensor",
                               lambda e, c=c, sl=sl, wsl=wsl: e.matmul(
                                   ps_p[:], wsl[:, c, :], xn[:, c, sl],
                                   start=(c == 0), stop=(c == DC - 1)),
                               waits=w0 if c == 0 else [],
                               inc=MM if c == DC - 1 else None)
                    w0 = []
                wev = [(MM, mmt), (LG, ld1)]
                if tt2_prev[cp] is not None:
                    wev.append((VE, tt2_prev[cp]))
                for q_ in range(4):
                    src = (q_ ^ 1) * 32
                    p.op("vector",
                         lambda e, q_=q_, src=src, cp=cp, sl=sl:
                         e.tensor_tensor(
                             qsw2[cp][q_ * 32:(q_ + 1) * 32, :],
                             ps_p[src:src + 32, :],
                             ss4b[q_ * 32:(q_ + 1) * 32, sl], ALU.mult),
                         waits=wev if q_ == 0 else [], inc=VE)
                    wev = []
                wtt = []
                if wi == 0 and ch == 0 and g >= 1:
                    wtt.append((MM, attn_end[g - 1]))
                if wi == 0:
                    for hh2 in range(2):
                        b2 = hh2 * 64
                        t1 = p.op("vector",
                                  lambda e, sl=sl, b2=b2, hh2=hh2:
                                  e.tensor_tensor(
                                      qpad[hh2][b2:b2 + 64, sl],
                                      ps_p[b2:b2 + 64, :],
                                      cc4b[b2:b2 + 64, sl], ALU.mult),
                                  waits=wtt, inc=VE)
                        wtt = []
                        t2 = p.op("vector",
                                  lambda e, cp=cp, sl=sl, b2=b2, hh2=hh2:
                                  e.tensor_tensor(
                                      qpad[hh2][b2:b2 + 64, sl],
                                      qpad[hh2][b2:b2 + 64, sl],
                                      qsw2[cp][b2:b2 + 64, :], ALU.add),
                                  waits=[(VE, t1)], inc=VE)
                    ps_p_free[0] = (VE, t1)
                else:
                    t1 = p.op("vector", lambda e, sl=sl, dstp=dstp:
                              e.tensor_tensor(dstp[:, sl], ps_p[:],
                                              cc4b[:, sl], ALU.mult),
                              waits=wtt, inc=VE)
                    ps_p_free[0] = (VE, t1)
                    t2 = p.op("vector", lambda e, cp=cp, sl=sl, dstp=dstp:
                              e.tensor_tensor(dstp[:, sl], dstp[:, sl],
                                              qsw2[cp][:], ALU.add),
                              waits=[(VE, t1)], inc=VE)
                tt2_prev[cp] = t2
        qkd_guard[0] = ps_p_free[0]
        for tq in range(8):
            w0 = [(LD, ld0), (LG, wready[g])]
            if vev_prev is not None:
                w0.append((VE, vev_prev))
            if tq == 0 and qkd_guard[1] is not None:
                w0.append(qkd_guard[1])
            for sub in range(4):
                for c in range(DC):
                    mmt = p.op("tensor",
                               lambda e, c=c, sub=sub, tq=tq, wb=wb: e.matmul(
                                   ps_v[:, sub * 128:(sub + 1) * 128],
                                   xn[:, c, tq * 512 + sub * 128:
                                      tq * 512 + sub * 128 + 128],
                                   wvb[wb][:, c, :],
                                   start=(c == 0), stop=(c == DC - 1)),
                               waits=w0 if (sub == 0 and c == 0) else [],
                               inc=MM if (sub == 3 and c == DC - 1) else None)
                w0 = []
            pv4 = ps_v.rearrange("p (s n) -> p s n", s=4)
            p.op("vector", lambda e, tq=tq: e.tensor_copy(
                vt[0][:, tq * 4:(tq + 1) * 4, 0:64], pv4[:, :, 0:64]),
                waits=[(MM, mmt)], inc=VE)
            vev_prev = p.op("vector", lambda e, tq=tq: e.tensor_copy(
                vt[1][:, tq * 4:(tq + 1) * 4, 0:64], pv4[:, :, 64:128]),
                inc=VE)
        vproj_done = vev_prev
        proj_end[g] = MM.n
        if g + 1 < H // 2:
            wg = (g + 1) % 2
            wld_guard = [(MM, proj_end[g - 1])] if g >= 1 else []
            for dst, col0 in ((wqb[wg], (g + 1) * 128),
                              (wkb[wg], D + (g + 1) * 128),
                              (wvb[wg], 2 * D + (g + 1) * 128)):
                wready[g + 1] = p.op("gpsimd", lambda e, d=dst, c=col0:
                                     e.dma_start(d[:], wqkvT[:, :, c:c + 128]),
                                     waits=wld_guard, inc=LG, inc_k=16)
                wld_guard = []

        for hh in range(2):
            base = hh * 64
            myvt = vt[hh]

            def emit_qk_pair(j, qsl, extra):
                m = j % 2
                wqk = list(extra)
                if qkd_guard[m] is not None:
                    wqk.append(qkd_guard[m])
                b0 = 2 * j
                p.op("tensor", lambda e, b0=b0, qsl=qsl, m=m, hh=hh:
                     e.matmul(qk4[m][0][:],
                              kpb[:, b0 * 128:(b0 + 1) * 128],
                              qpad[hh][:, qsl], start=True, stop=True),
                     waits=wqk)
                qk_pair_done[m] = p.op(
                    "tensor", lambda e, b0=b0, qsl=qsl, m=m, hh=hh:
                    e.matmul(qk4[m][1][:],
                             kpb[:, (b0 + 1) * 128:(b0 + 2) * 128],
                             qpad[hh][:, qsl], start=True, stop=True),
                    inc=MM)

            for qc in range(NT):
                nkt = 4 * (qc + 1)
                npair = nkt // 2
                qv = qc % 2
                qsl = slice(qc * TC, (qc + 1) * TC)
                extra0 = []
                if qc == 0 and hh == 0:
                    extra0.append((VE, tt2_prev[1]))
                    if g == 0:
                        extra0.append((LG, ldm))
                        extra0.append((VE, const_v))
                emit_qk_pair(0, qsl, extra0)
                emit_qk_pair(1, qsl, [])
                for j in range(npair):
                    m = j % 2
                    s = (2 * j) % 4
                    if j == 1 and pend[0] is not None:
                        flush_norm()
                    wex = [(MM, qk_pair_done[m])]
                    if prev_pv.get(s + 1) is not None:
                        wex.append((MM, prev_pv[s + 1]))
                    ex_t = p.op("scalar", lambda e, s=s, m=m: e.activation(
                        expf[:, s * TC:(s + 2) * TC], qkD[m][:], AF.Exp,
                        bias=shift[:, 0:1], scale=0.125), waits=wex, inc=SC)
                    qkd_guard[m] = (SC, ex_t)
                    mk = {}
                    for t in range(2):
                        ktc = 2 * j + t
                        r = ktc - 4 * qc
                        if 0 <= r <= 3:
                            mk[t] = p.op("vector",
                                         lambda e, slot=s + t, r=r:
                                         e.tensor_tensor(
                                             expT[:, slot, :], expT[:, slot, :],
                                             msk[:, r, :], ALU.mult),
                                         waits=[(SC, ex_t)], inc=VE)
                    pv_t = None
                    for t in range(2):
                        ktc = 2 * j + t
                        wpv = []
                        if t == 0:
                            wpv.append((VE, mk[0]) if 0 in mk else (SC, ex_t))
                            if j == 0:
                                if qc == 0 and hh == 0:
                                    wpv.append((VE, vproj_done))
                                if pv_bank_free[qv] is not None:
                                    wpv.append((VE, pv_bank_free[qv]))
                        elif 1 in mk:
                            wpv.append((VE, mk[1]))
                        pv_t = p.op("tensor",
                                    lambda e, slot=s + t, ktc=ktc, nkt=nkt,
                                    qv=qv, myvt=myvt:
                                    e.matmul(ps_pv[qv][0:65, :],
                                             myvt[:, ktc, 0:65],
                                             expT[:, slot, :],
                                             start=(ktc == 0),
                                             stop=(ktc == nkt - 1)),
                                    waits=wpv, inc=MM if t == 1 else None)
                    prev_pv[s] = prev_pv[s + 1] = pv_t
                    if j + 2 < npair:
                        emit_qk_pair(j + 2, qsl, [])
                wrc = [(MM, pv_t)]
                if nrow_free[qv] is not None:
                    wrc.append((MM, nrow_free[qv]))
                rc_t = p.op("vector", lambda e, qv=qv: e.tensor_copy(
                    nrow[qv][0:1, :], ps_pv[qv][64:65, :]), waits=wrc, inc=VE)
                pend[0] = (qv, base, g, qsl, rc_t, False)
        attn_end[g] = MM.n
    final_tt = flush_norm()

    lw = p.op("gpsimd", lambda e: e.dma_start(
        wo[:], T["woutT"].rearrange("(fc p) (dc m) -> p fc dc m", p=128, m=128)),
        waits=[(MM, attn_end[H // 2 - 1])], inc=LG, inc_k=16)
    ev_done = {}
    st_done = {}
    for it in range(NT):
        bi = it % 2
        for dc in range(DC):
            bank = dc % 2
            wmm = []
            if it == 0 and dc == 0:
                wmm = [(LG, lw), (VE, final_tt)]
            prev = it * DC + dc - 2
            if prev >= 0:
                wmm.append((SC, ev_done[prev]))
            for fc in range(DC):
                mmt = p.op("tensor",
                           lambda e, fc=fc, dc=dc, it=it, bank=bank: e.matmul(
                               qk4[0][bank][:], wo[:, fc, dc, :],
                               ctxa[:, fc, it * TC:(it + 1) * TC],
                               start=(fc == 0), stop=(fc == DC - 1)),
                           waits=wmm if fc == 0 else [],
                           inc=MM if fc == DC - 1 else None)
                wmm = []
            wev = [(MM, mmt)]
            if it >= 2 and dc == 0:
                wev.append((ST, st_done[it - 2]))
            ev_done[it * DC + dc] = p.op(
                "scalar", lambda e, dc=dc, bi=bi, bank=bank: e.copy(
                    s0st[bi][:, dc, :], qk4[0][bank][:]), waits=wev, inc=SC)
        st_done[it] = p.op("sync", lambda e, it=it, bi=bi: e.dma_start(
            T["s0"][:, :, it * TC:(it + 1) * TC], s0st[bi][:]),
            waits=[(SC, ev_done[it * DC + DC - 1])], inc=ST, inc_k=16)
    p.wait("sync", [(ST, ST.n)])


def emit_router(p, T, AR, SM, PS):
    nc = p.nc
    LD = p.sem("LDS")
    LG = p.sem("LDG")
    ST = p.sem("STS")
    MM = p.sem("TEN")
    SC = p.sem("SCA")
    VE = p.sem("VEC")

    o = [0]

    def al(n, a=64):
        o[0] = (o[0] + a - 1) // a * a
        r = o[0]
        o[0] += n
        return r

    xnt = [AR.alias(F32R, al(DC * TC * 4), [128, DC, TC]) for _ in range(2)]
    r1 = AR.alias(F32R, al(2 * S * 4), [128, 2, S])
    wr1b = AR.alias(F32R, al(DC * 256 * 4), [128, DC, 256])
    wr2b = AR.alias(F32R, al(2 * 8 * 4), [128, 2, 8])
    br1b = AR.view(F32, al(2 * 4), [128, 2])
    br2b = AR.view(F32, al(4), [8, 1])
    lgrow = AR.view(F32, al(S * 4), [8, S])
    idf = AR.view(F32, al(128 * 4), [128, 128])
    gTM = AR.view(F32, al(32 * 8 * 4), [128, 32, 8])
    e8 = AR.view(F32, al(32 * 8 * 4), [128, 32, 8])
    m1 = AR.view(F32, al(32 * 8 * 4), [128, 32, 8])
    m2 = AR.view(F32, al(32 * 8 * 4), [128, 32, 8])
    w9tm = AR.view(F32, al(32 * 9 * 4), [128, 32, 9])
    cols = AR.view(F32, al(32 * 8 * 4), [128, 8, 32])
    w9sb = AR.view(F32, al(S * 4), [9, S])
    assert o[0] <= AR.nbytes

    ps_r = [PS[0], PS[1]]
    ps_l = PS[2][0:8, :]
    ps_t = PS[3]
    ps_t9 = PS[4][0:9, :]

    lw = p.op("gpsimd", lambda e: e.dma_start(
        wr1b[:], T["wr1"].rearrange("(c p) m -> p c m", p=128)), inc=LG, inc_k=16)
    lw = p.op("gpsimd", lambda e: e.dma_start(
        wr2b[:], T["wr2"].rearrange("(c p) m -> p c m", p=128)), inc=LG, inc_k=16)
    p.op("sync", lambda e: e.dma_start(br1b[:], T["br1t"][:]), inc=LD, inc_k=16)
    p.op("sync", lambda e: e.dma_start(br2b[:], T["br2"][:]), inc=LD, inc_k=16)
    ld_c = p.op("sync", lambda e: e.dma_start(idf[:], T["ident32"][:]),
                inc=LD, inc_k=16)

    ev_prev = {0: None, 1: None}
    ev_prev_tile = {}
    lg_prev = None
    for it in range(NT):
        bi = it % 2
        w0 = [(LG, lw), (LD, ld_c)]
        if it >= 2:
            w0.append((SC, ev_prev_tile[it - 2]))
        ld_t = p.op("gpsimd", lambda e, it=it, bi=bi: e.dma_start(
            xnt[bi][:], T["xn0f"][:, :, it * TC:(it + 1) * TC]),
            waits=w0, inc=LG, inc_k=16)
        for m_ in range(2):
            wmm = [(LD, ld_t)]
            if ev_prev[m_] is not None:
                wmm.append((SC, ev_prev[m_]))
            wmm = [(LG, ld_t)] + wmm[1:]
            for c in range(DC):
                mmt = p.op("tensor", lambda e, c=c, m_=m_, bi=bi: e.matmul(
                    ps_r[m_][:], wr1b[:, c, m_ * 128:(m_ + 1) * 128],
                    xnt[bi][:, c, :], start=(c == 0), stop=(c == DC - 1)),
                    waits=wmm if c == 0 else [], inc=MM if c == DC - 1 else None)
            ev_prev[m_] = p.op("scalar", lambda e, m_=m_, it=it: e.activation(
                r1[:, m_, it * TC:(it + 1) * TC], ps_r[m_][:], AF.Relu,
                bias=br1b[:, m_:m_ + 1]), waits=[(MM, mmt)], inc=SC)
        ev_prev_tile[it] = ev_prev[1]
        wl = [(SC, ev_prev[1])]
        if lg_prev is not None:
            wl.append((SC, lg_prev))
        for c2 in range(2):
            mmt = p.op("tensor", lambda e, c2=c2, it=it: e.matmul(
                ps_l[:], wr2b[:, c2, :], r1[:, c2, it * TC:(it + 1) * TC],
                start=(c2 == 0), stop=(c2 == 1)),
                waits=wl if c2 == 0 else [], inc=MM if c2 == 1 else None)
        lg_prev = p.op("scalar", lambda e, it=it: e.activation(
            lgrow[:, it * TC:(it + 1) * TC], ps_l[:], AF.Identity,
            bias=br2b[:, 0:1]), waits=[(MM, mmt)], inc=SC)

    tp_prev = None
    for grp in range(2):
        wtp = [(SC, lg_prev)]
        if tp_prev is not None:
            wtp.append((VE, tp_prev))
        for i in range(16):
            tt = grp * 16 + i
            mmt = p.op("tensor", lambda e, i=i, tt=tt: e.transpose(
                ps_t[:, i * 8:(i + 1) * 8], lgrow[:, tt * 128:(tt + 1) * 128],
                idf[0:8, 0:8]), waits=wtp if i == 0 else [],
                inc=MM if i == 15 else None)
        pt3 = ps_t.rearrange("p (i v) -> p i v", v=8)
        tp_prev = p.op("vector", lambda e, grp=grp: e.tensor_copy(
            gTM[:, grp * 16:(grp + 1) * 16, :], pt3[:, 0:16, :]),
            waits=[(MM, mmt)], inc=VE)

    def vop(call, waits=()):
        return p.op("vector", call, waits=waits, inc=VE)
    mx = cols[:, 0, :]
    e1c = cols[:, 1, :]
    e2c = cols[:, 2, :]
    Sc_ = cols[:, 3, :]
    rS = cols[:, 4, :]
    w1 = cols[:, 5, :]
    w2 = cols[:, 6, :]
    tc_ = cols[:, 7, :]
    t0 = vop(lambda e: e.reduce_max(mx, gTM[:], axis=AX.X), waits=[(VE, tp_prev)])
    t1 = vop(lambda e: e.tensor_tensor(
        e8[:], gTM[:], mx[:, :, None].broadcast_to([128, 32, 8]), ALU.subtract),
        waits=[(VE, t0)])
    t2 = p.op("scalar", lambda e: e.activation(
        e8.rearrange("p a b -> p (a b)"), e8.rearrange("p a b -> p (a b)"),
        AF.Exp), waits=[(VE, t1)], inc=SC)
    t3 = vop(lambda e: e.reduce_sum(Sc_, e8[:], axis=AX.X), waits=[(SC, t2)])
    t4 = vop(lambda e: e.tensor_tensor(
        m1[:], gTM[:], mx[:, :, None].broadcast_to([128, 32, 8]), ALU.is_ge),
        waits=[(SC, t2)])
    t5 = vop(lambda e: e.reduce_max(e1c, e8[:], axis=AX.X), waits=[(VE, t4)])
    t6 = vop(lambda e: e.tensor_scalar(m2[:], m1[:], -1.0, 1.0, ALU.mult, ALU.add),
             waits=[(VE, t4)])
    t7 = vop(lambda e: e.tensor_tensor(gTM[:], e8[:], m2[:], ALU.mult),
             waits=[(VE, t6)])
    t8 = vop(lambda e: e.reduce_max(e2c, gTM[:], axis=AX.X), waits=[(VE, t7)])
    t9 = vop(lambda e: e.tensor_tensor(
        m2[:], gTM[:], e2c[:, :, None].broadcast_to([128, 32, 8]), ALU.is_ge),
        waits=[(VE, t8)])
    t10 = vop(lambda e: e.tensor_tensor(tc_, e1c, e2c, ALU.subtract),
              waits=[(VE, t9), (VE, t5), (VE, t3)])
    t11 = vop(lambda e: e.reciprocal(rS, Sc_), waits=[(VE, t10)])
    t12 = vop(lambda e: e.tensor_tensor(tc_, tc_, rS, ALU.mult), waits=[(VE, t11)])
    t13 = vop(lambda e: e.tensor_scalar_mul(tc_, tc_, 0.5), waits=[(VE, t12)])
    t14 = p.op("scalar", lambda e: e.activation(w1, tc_, AF.Sigmoid),
               waits=[(VE, t13)], inc=SC)
    t15 = vop(lambda e: e.tensor_scalar(w2, w1, -1.0, 1.0, ALU.mult, ALU.add),
              waits=[(SC, t14)])
    t16 = vop(lambda e: e.memset(w9tm[:, :, 0:1], 1.0))
    t17 = vop(lambda e: e.tensor_tensor(
        m1[:], m1[:], w1[:, :, None].broadcast_to([128, 32, 8]), ALU.mult),
        waits=[(VE, t15)])
    t18 = vop(lambda e: e.tensor_tensor(
        m2[:], m2[:], w2[:, :, None].broadcast_to([128, 32, 8]), ALU.mult),
        waits=[(VE, t17)])
    t19 = vop(lambda e: e.tensor_tensor(w9tm[:, :, 1:9], m1[:], m2[:], ALU.add),
              waits=[(VE, t18)])

    tp_prev = None
    for grp in range(8):
        wtp = [(VE, t19)]
        if tp_prev is not None:
            wtp.append((SC, tp_prev))
        for i in range(4):
            tt = grp * 4 + i
            mmt = p.op("tensor", lambda e, i=i, tt=tt: e.transpose(
                ps_t9[:, i * 128:(i + 1) * 128],
                w9tm.rearrange("p a b -> p (a b)")[:, tt * 9:(tt + 1) * 9],
                idf[:, :]), waits=wtp if i == 0 else [],
                inc=MM if i == 3 else None)
        pt4 = ps_t9.rearrange("p (i v) -> p i v", v=128)
        tp_prev = p.op("scalar", lambda e, grp=grp: e.copy(
            w9sb.rearrange("p (i v) -> p i v", v=128)[:, grp * 4:(grp + 1) * 4, :],
            pt4[:, 0:4, :]), waits=[(MM, mmt)], inc=SC)
    p.op("sync", lambda e: e.dma_start(T["w9"][:], w9sb[:]),
         waits=[(SC, tp_prev)], inc=ST, inc_k=16)
    p.wait("sync", [(ST, ST.n)])


def emit_moe(p, T, AR, SM, PS):
    nc = p.nc
    LD = p.sem("LDS")
    LG = p.sem("LDG")
    MM = p.sem("TEN")
    SC = p.sem("SCA")
    VE = p.sem("VEC")

    o = [0]

    def al(n, a=64):
        o[0] = (o[0] + a - 1) // a * a
        r = o[0]
        o[0] += n
        return r

    we1 = [AR.view(BF16, al(DC * 16 * 128 * 2), [128, DC, 16, 128]) for _ in range(2)]
    we2 = [AR.view(BF16, al(16 * DC * 128 * 2), [128, 16, DC, 128]) for _ in range(2)]
    ab = AR.view(BF16, al(16 * TC * 2), [128, 16, TC])
    xnt = [AR.view(BF16, al(DC * TC * 2), [128, DC, TC]) for _ in range(2)]
    accst = [AR.view(F32, al(DC * TC * 4), [128, DC, TC]) for _ in range(2)]
    assert o[0] <= AR.nbytes, o[0]
    wbc = [SM.view(F32, i * 2048, [128, TC]) for i in range(2)]
    be1b = SM.view(F32, 4096, [128, 32])
    be2b = SM.view(F32, 4096 + 128, [128, 8])

    ps_a = [PS[0], PS[1]]
    ps_h = [PS[2], PS[3]]

    we1_r = T["we1"].rearrange("e (c p) (m q) -> e p c m q", p=128, q=128)
    we2_r = T["we2"].rearrange("e (k p) (d q) -> e p k d q", p=128, q=128)
    ws1_r = T["ws1"].rearrange("(c p) (m q) -> p c m q", p=128, q=128)
    ws2_r = T["ws2"].rearrange("(k p) (d q) -> p k d q", p=128, q=128)

    def w_srcs(ei, hh):
        if ei == 0:
            return (ws1_r[:, :, hh * 16:(hh + 1) * 16, :],
                    ws2_r[:, hh * 16:(hh + 1) * 16, :, :])
        return (we1_r[ei - 1, :, :, hh * 16:(hh + 1) * 16, :],
                we2_r[ei - 1, :, hh * 16:(hh + 1) * 16, :, :])

    steps = [(ei, hh) for ei in range(NEXP) for hh in range(2)]
    ld_w = {}
    mm_step = {}
    mm_tile = {}
    ac_t = {}
    sil_last = {}
    stt_last_h0 = {}
    prev_sil = {}
    prev_stt = {}
    prev_stt_tile_map = {}

    for si in range(2):
        ei, hh = steps[si]
        s1, s2 = w_srcs(ei, hh)
        p.op("gpsimd", lambda e, s1=s1, si=si: e.dma_start(we1[si % 2][:], s1),
             inc=LG, inc_k=16)
        ld_w[si] = p.op("gpsimd", lambda e, s2=s2, si=si: e.dma_start(
            we2[si % 2][:], s2), inc=LG, inc_k=16)

    bias_ld = {}
    p.op("sync", lambda e: e.dma_start(be1b[:], T["bs1t"][:]), inc=LD, inc_k=16)
    bias_ld[0] = p.op("sync", lambda e: e.dma_start(be2b[:], T["bs2t"][:]),
                      inc=LD, inc_k=16)

    for si, (ei, hh) in enumerate(steps):
        wb = si % 2
        if hh == 0 and ei > 0:
            wbias = [(SC, sil_last[si - 1]), (VE, stt_last_h0[ei - 1])]
            p.op("sync", lambda e, ei=ei: e.dma_start(
                be1b[:], T["be1t"][ei - 1]), waits=wbias, inc=LD, inc_k=16)
            bias_ld[ei] = p.op("sync", lambda e, ei=ei: e.dma_start(
                be2b[:], T["be2t"][ei - 1]), inc=LD, inc_k=16)
        sec_prev = None
        for tg in range(NT):
            tb = tg % 2
            sl = slice(tg * TC, (tg + 1) * TC)
            idx = si * NT + tg
            wxn = []
            pu = (si, tg - 2) if tg >= 2 else (si - 1, NT - 2 + tg)
            if pu in mm_tile:
                wxn.append((MM, mm_tile[pu]))
            ld_x = p.op("sync", lambda e, tb=tb, sl=sl: e.dma_start(
                xnt[tb][:], T["xn0"][:, :, sl]), waits=wxn, inc=LD, inc_k=16)
            wwb = []
            if pu in prev_stt_tile_map:
                wwb.append((VE, prev_stt_tile_map[pu]))
            ld_wb = p.op("gpsimd", lambda e, tb=tb, ei=ei, sl=sl: e.dma_start(
                wbc[tb][:], T["w9"][ei:ei + 1, sl].partition_broadcast(128)),
                waits=wwb, inc=LG, inc_k=16)
            for m_ in range(16):
                bank = m_ % 2
                wmm = [(LD, ld_x), (LG, ld_w[si])]
                if prev_sil.get(bank) is not None:
                    wmm.append((SC, prev_sil[bank]))
                for c in range(DC):
                    mmt = p.op("tensor",
                               lambda e, c=c, m_=m_, wb=wb, tb=tb, bank=bank:
                               e.matmul(ps_a[bank][:], we1[wb][:, c, m_, :],
                                        xnt[tb][:, c, :],
                                        start=(c == 0), stop=(c == DC - 1)),
                               waits=wmm if c == 0 else [],
                               inc=MM if c == DC - 1 else None)
                wsl = [(MM, mmt)]
                if m_ == 0:
                    wsl.append((LD, bias_ld[ei]))
                    if sec_prev is not None:
                        wsl.append((MM, sec_prev))
                prev_sil[bank] = p.op(
                    "scalar", lambda e, m_=m_, bank=bank, hh=hh: e.activation(
                        ab[:, m_, :], ps_a[bank][:], AF.Silu,
                        bias=be1b[:, hh * 16 + m_:hh * 16 + m_ + 1]),
                    waits=wsl, inc=SC)
            sil_all = SC.n
            stt_t = None
            for d in range(DC):
                bank = d % 2
                wmm = [(SC, sil_all)] if d == 0 else []
                if prev_stt.get(bank) is not None:
                    wmm.append((VE, prev_stt[bank]))
                for k in range(16):
                    mmt = p.op("tensor",
                               lambda e, k=k, d=d, wb=wb, bank=bank:
                               e.matmul(ps_h[bank][:], we2[wb][:, k, d, :],
                                        ab[:, k, :],
                                        start=(k == 0), stop=(k == 15)),
                               waits=wmm if k == 0 else [],
                               inc=MM if k == 15 else None)
                wev = [(MM, mmt), (LG, ld_wb)]
                if d == 0:
                    if (idx - 2) >= 0 and (idx - 2) in ac_t:
                        wev.append((LG, ac_t[idx - 2]))
                if hh == 0:
                    stt_t = p.op(
                        "vector",
                        lambda e, d=d, tb=tb, bank=bank, idx=idx:
                        e.scalar_tensor_tensor(
                            accst[idx % 2][:, d, :], ps_h[bank][:],
                            be2b[:, d:d + 1], wbc[tb][:], ALU.add, ALU.mult),
                        waits=wev, inc=VE)
                else:
                    stt_t = p.op(
                        "vector",
                        lambda e, d=d, tb=tb, bank=bank, idx=idx: e.tensor_tensor(
                            accst[idx % 2][:, d, :], ps_h[bank][:], wbc[tb][:],
                            ALU.mult), waits=wev, inc=VE)
                prev_stt[bank] = stt_t
            prev_stt_tile_map[(si, tg)] = stt_t
            op_ = ALU.bypass if si == 0 else ALU.add
            ac_t[idx] = p.op("gpsimd", lambda e, sl=sl, op_=op_, idx=idx:
                             e.dma_start(T["acc"][:, :, sl], accst[idx % 2][:],
                                         accum_op=op_),
                             waits=[(VE, stt_t)], inc=LG, inc_k=16)
            mm_tile[(si, tg)] = MM.n
            sec_prev = MM.n
        mm_step[si] = MM.n
        sil_last[si] = SC.n
        if hh == 0:
            stt_last_h0[ei] = VE.n
        if si + 2 < len(steps):
            nei, nhh = steps[si + 2]
            s1, s2 = w_srcs(nei, nhh)
            p.op("gpsimd", lambda e, s1=s1, wb=wb: e.dma_start(we1[wb][:], s1),
                 waits=[(MM, mm_step[si])], inc=LG, inc_k=16)
            ld_w[si + 2] = p.op("gpsimd", lambda e, s2=s2, wb=wb: e.dma_start(
                we2[wb][:], s2), inc=LG, inc_k=16)
    p.wait("gpsimd", [(LG, LG.n)])


def emit_comb_ln(p, T, AR, PS):
    LD = p.sem("LDS")
    LG = p.sem("LDG")
    ST = p.sem("STS")
    MM = p.sem("TEN")
    SC = p.sem("SCA")
    VE = p.sem("VEC")

    o = [0]

    def al(n, a=64):
        o[0] = (o[0] + a - 1) // a * a
        r = o[0]
        o[0] += n
        return r

    xt = [AR.view(F32, al(DC * TC * 4), [128, DC, TC]) for _ in range(2)]
    s0t = [AR.view(BF16, al(DC * TC * 2), [128, DC, TC]) for _ in range(2)]
    act = [AR.view(F32, al(DC * TC * 4), [128, DC, TC]) for _ in range(2)]
    gt = AR.view(F32, al(DC * TC * 4), [128, DC, TC])
    x1t = [AR.view(F32, al(DC * TC * 4), [128, DC, TC]) for _ in range(2)]
    xnt = [AR.view(BF16, al(DC * TC * 2), [128, DC, TC]) for _ in range(2)]
    srow = AR.view(F32, al(6 * TC * 4), [1, 6, TC])
    ones8 = AR.view(F32, al(128 * 8 * 4 // 128 * 128), [128, 8])
    ones_row = AR.view(F32, al(512), [1, 128])
    assert o[0] <= AR.nbytes

    ps_sum = PS[0][0:8, :]
    ps_sq = PS[1][0:8, :]
    ps_nm = PS[2]
    ps_rs = PS[3]

    p.op("vector", lambda e: e.memset(ones8[:], 1.0), inc=VE)
    p.op("vector", lambda e: e.memset(ones_row[:], 1.0), inc=VE)
    ones_ready = VE.n

    done_v = {}
    done_st = {}
    ln_done = {}
    ln_store = {}
    prev_t1 = prev_t2 = prev_n2 = None
    for it in range(NT):
        bi = it % 2
        sl = slice(it * TC, (it + 1) * TC)
        w0 = [(VE, done_v[it - 2])] if it >= 2 else []
        ld1 = p.op("sync", lambda e, bi=bi, sl=sl: e.dma_start(
            xt[bi][:], T["xT"][:, :, sl]), waits=w0, inc=LD, inc_k=16)
        ld2 = p.op("sync", lambda e, bi=bi, sl=sl: e.dma_start(
            s0t[bi][:], T["s0"][:, :, sl]), inc=LD, inc_k=16)
        ld3 = p.op("sync", lambda e, bi=bi, sl=sl: e.dma_start(
            act[bi][:], T["acc"][:, :, sl]), inc=LD, inc_k=16)
        wsi = [(LD, ld3)]
        if prev_n2 is not None:
            wsi.append((VE, prev_n2))
        si_t = p.op("scalar", lambda e, bi=bi: e.activation(
            gt.rearrange("p a b -> p (a b)"),
            act[bi].rearrange("p a b -> p (a b)"), AF.Silu),
            waits=wsi, inc=SC)
        v1 = p.op("vector", lambda e, bi=bi: e.tensor_tensor(
            gt[:], s0t[bi][:], gt[:], ALU.mult),
            waits=[(SC, si_t), (LD, ld2)], inc=VE)
        wv2 = [(VE, v1), (LD, ld1)]
        if it >= 2:
            wv2.append((ST, done_st[it - 2]))
            wv2.append((MM, ln_done[it - 2]))
        v2 = p.op("vector", lambda e, bi=bi: e.tensor_tensor(
            x1t[bi][:], xt[bi][:], gt[:], ALU.add), waits=wv2, inc=VE)
        done_v[it] = v2
        done_st[it] = p.op("sync", lambda e, bi=bi, sl=sl: e.dma_start(
            T["x1"][:, :, sl], x1t[bi][:]), waits=[(VE, v2)], inc=ST, inc_k=16)
        wsq = [(VE, v2)]
        if prev_n2 is not None:
            wsq.append((VE, prev_n2))
        sq_t = p.op("vector", lambda e, bi=bi: e.tensor_tensor(
            gt[:], x1t[bi][:], x1t[bi][:], ALU.mult), waits=wsq, inc=VE)
        for c in range(DC):
            w1 = [(VE, v2), (VE, ones_ready)]
            if c == 0 and prev_t1 is not None:
                w1.append((VE, prev_t1))
            p.op("tensor", lambda e, bi=bi, c=c: e.matmul(
                ps_sum[:], ones8[:], x1t[bi][:, c, :],
                start=(c == 0), stop=(c == DC - 1)),
                waits=w1 if c == 0 else [])
        for c in range(DC):
            w1 = [(VE, sq_t)]
            if c == 0 and prev_t2 is not None:
                w1.append((VE, prev_t2))
            mmt = p.op("tensor", lambda e, c=c: e.matmul(
                ps_sq[:], ones8[:], gt[:, c, :],
                start=(c == 0), stop=(c == DC - 1)),
                waits=w1 if c == 0 else [],
                inc=MM if c == DC - 1 else None)
        mm_stats = mmt

        def rc(call, waits=()):
            return p.op("vector", call, waits=waits, inc=VE)
        t1 = rc(lambda e: e.tensor_scalar_mul(srow[:, 0, :], ps_sum[0:1, :],
                                              1.0 / D), waits=[(MM, mm_stats)])
        t2 = rc(lambda e: e.tensor_scalar(srow[:, 1, :], ps_sq[0:1, :], 1.0 / D,
                                          None, ALU.mult), waits=[(VE, t1)])
        t3 = rc(lambda e: e.tensor_tensor(srow[:, 4, :], srow[:, 0, :],
                                          srow[:, 0, :], ALU.mult),
                waits=[(VE, t2)])
        t4 = rc(lambda e: e.tensor_tensor(srow[:, 1, :], srow[:, 1, :],
                                          srow[:, 4, :], ALU.subtract),
                waits=[(VE, t3)])
        t4b = rc(lambda e: e.tensor_scalar_add(srow[:, 1, :], srow[:, 1, :],
                                               1e-5), waits=[(VE, t4)])
        t5 = p.op("scalar", lambda e: e.activation(
            srow[:, 2, :], srow[:, 1, :], AF.Sqrt), waits=[(VE, t4b)], inc=SC)
        t6 = rc(lambda e: e.reciprocal(srow[:, 2, :], srow[:, 2, :]),
                waits=[(SC, t5)])
        t7 = rc(lambda e: e.tensor_scalar_mul(srow[:, 3, :], srow[:, 0, :],
                                              -1.0), waits=[(VE, t6)])
        wbc = [(VE, t7)]
        if prev_n2 is not None:
            wbc.append((VE, prev_n2))
        p.op("tensor", lambda e: e.matmul(
            ps_nm[:], ones_row[:], srow[:, 3, :], start=True, stop=True),
            waits=wbc)
        bc_t = p.op("tensor", lambda e: e.matmul(
            ps_rs[:], ones_row[:], srow[:, 2, :], start=True, stop=True),
            inc=MM)
        wst = [(MM, bc_t)]
        if it >= 2:
            wst.append((LG, ln_store[it - 2]))
        n1 = p.op("vector", lambda e, bi=bi: e.tensor_tensor(
            gt[:], x1t[bi][:], ps_nm[:, None, :].broadcast_to([128, DC, TC]),
            ALU.add), waits=wst, inc=VE)
        n2 = p.op("vector", lambda e, bi=bi: e.tensor_tensor(
            xnt[bi][:], gt[:], ps_rs[:, None, :].broadcast_to([128, DC, TC]),
            ALU.mult), waits=[(VE, n1)], inc=VE)
        ln_done[it] = MM.n
        prev_t1, prev_t2, prev_n2 = t1, t2, n2
        ln_store[it] = p.op("gpsimd", lambda e, bi=bi, sl=sl: e.dma_start(
            T["xn1"][:, :, sl], xnt[bi][:]), waits=[(VE, n2)], inc=LG, inc_k=16)
    p.wait("sync", [(ST, ST.n)])
    p.wait("gpsimd", [(LG, LG.n)])


def emit_attn1(p, T, AR, SM, PS, PSD):
    nc = p.nc
    LD = p.sem("LDS")
    LG = p.sem("LDG")
    ST = p.sem("STS")
    MM = p.sem("TEN")
    SC = p.sem("SCA")
    VE = p.sem("VEC")

    o = [0]

    def al(n, a=64):
        o[0] = (o[0] + a - 1) // a * a
        r = o[0]
        o[0] += n
        return r

    Call = AR.view(BF16, al(16 * 128 * 2), [128, 16, 128])
    veffM = AR.view(BF16, al(16 * 64 * 2), [128, 16, 64])
    Gb = AR.view(BF16, al(128 * 2), [128, 128])
    Eh = AR.view(BF16, al(128 * 2), [64, 128])
    ckvb = AR.view(BF16, al(S * 2), [128, S])
    cqb = AR.view(BF16, al(S * 2), [128, S])
    qlb = [AR.view(BF16, al(S * 2), [128, S]) for _ in range(2)]
    wvh = [AR.view(BF16, al(32 * 65 * 2), [128, 32, 65]) for _ in range(2)]
    expT = AR.view(BF16, al(4 * TC * 2), [128, 4, TC])
    msk = SM.view(BF16, 0, [128, 4, TC])
    nrow = [AR.view(BF16, al(TC * 2), [128, TC]) for _ in range(2)]
    bcb = [AR.view(F32, al(TC * 4), [64, TC]) for _ in range(2)]
    onesP = AR.view(BF16, al(256), [128, 128])
    shift = AR.view(F32, al(64), [128, 1])
    wdqT_b = AR.view(BF16, al(DC * 128 * 2), [128, DC, 128])
    wdkvT_b = AR.view(BF16, al(DC * 128 * 2), [128, DC, 128])
    wdq_b = AR.view(BF16, al(D * 2), [128, D])
    wuq_b = AR.view(BF16, al(DC * 128 * 2), [128, DC, 128])
    wuqr_b = AR.view(BF16, al(16 * 128 * 2), [64, 16 * 128])
    wuk_b = AR.view(BF16, al(DC * 128 * 2), [128, DC, 128])
    wuv_b = AR.view(BF16, al(DC * 128 * 2), [128, DC, 128])
    woT_b = AR.view(BF16, al(DC * D * 2), [128, DC, D])
    xn1_off = al(DC * S * 2)
    xn1b = AR.view(BF16, xn1_off, [128, DC, S])
    s1st = AR.view(BF16, xn1_off, [128, DC, S])
    assert o[0] <= AR.nbytes, o[0]

    ps_qk = [PS[0], PS[1]]
    ps_pv = [PS[2], PS[3]]
    ps_bc = [PS[4], PS[6]]
    ps_wv = PS[5]
    ps_g = PS[7]

    lwt = None
    for dst, srcn, rr in (
            (wdqT_b, "wdqT", "(c p) m -> p c m"),
            (wdkvT_b, "wdkvT", "(c p) m -> p c m"),
            (wdq_b, "wdq", None), (wuq_b, "wuq", "(c p) m -> p c m"),
            (wuqr_b, "wuqr", "d h m -> d (h m)"),
            (wuk_b, "wuk", "(c p) m -> p c m"),
            (wuv_b, "wuv", "(c p) m -> p c m"),
            (woT_b, "woT", "(c p) m -> p c m")):
        srcap = T[srcn][:] if rr is None else T[srcn].rearrange(rr, **({"p": 128} if "p" in rr else {}))
        lwt = p.op("gpsimd", lambda e, d=dst, s=srcap: e.dma_start(d[:], s),
                   inc=LG, inc_k=16)
    ldm = p.op("gpsimd", lambda e: e.dma_start(
        msk[:], T["masks"].rearrange("r p t -> p r t")), inc=LG, inc_k=16)
    p.op("vector", lambda e: e.memset(onesP[:], 0.0), inc=VE)
    p.op("vector", lambda e: e.memset(onesP[0:1, :], 1.0), inc=VE)
    p.op("vector", lambda e: e.memset(nrow[0][:], 0.0), inc=VE)
    p.op("vector", lambda e: e.memset(nrow[1][:], 0.0), inc=VE)
    p.op("vector", lambda e: e.memset(shift[:], -EXP_SHIFT), inc=VE)
    p.op("vector", lambda e: e.memset(wvh[0][:, :, 64:65], 1.0), inc=VE)
    p.op("vector", lambda e: e.memset(wvh[1][:, :, 64:65], 1.0), inc=VE)
    const_v = VE.n
    ldx = p.op("sync", lambda e: e.dma_start(xn1b[:], T["xn1"][:]), inc=LD, inc_k=16)

    qk_guard = {0: None, 1: None}
    for wsl, dstp in ((wdqT_b, cqb), (wdkvT_b, ckvb)):
        for n in range(NT):
            bank = n % 2
            psb = ps_qk[bank]
            w0 = [(LD, ldx), (LG, lwt)]
            if qk_guard[bank] is not None:
                w0.append(qk_guard[bank])
            for c in range(DC):
                mmt = p.op("tensor", lambda e, c=c, n=n, psb=psb, wsl=wsl: e.matmul(
                    psb[:], wsl[:, c, :], xn1b[:, c, n * TC:(n + 1) * TC],
                    start=(c == 0), stop=(c == DC - 1)),
                    waits=w0 if c == 0 else [], inc=MM if c == DC - 1 else None)
            ev = p.op("scalar", lambda e, n=n, psb=psb, dstp=dstp: e.copy(
                dstp[:, n * TC:(n + 1) * TC], psb[:]), waits=[(MM, mmt)], inc=SC)
            qk_guard[bank] = (SC, ev)
    cqckv_done = qk_guard[1]

    for c in range(DC):
        mmt = p.op("tensor", lambda e, c=c: e.matmul(
            ps_g[:, 0:128], wuq_b[:, c, :], wuk_b[:, c, :],
            start=(c == 0), stop=(c == DC - 1)),
            waits=[(LG, lwt)] if c == 0 else [], inc=MM if c == DC - 1 else None)
    g_ev = p.op("vector", lambda e: e.tensor_copy(Gb[:], ps_g[:, 0:128]),
                waits=[(MM, mmt)], inc=VE)

    evC_guard = {0: None, 1: None}
    mmC_prev = None
    for h in range(H):
        hb = h % 2
        wE = [(VE, g_ev)]
        if mmC_prev is not None:
            wE.append((MM, mmC_prev))
        mmE = p.op("tensor", lambda e, h=h, hb=hb: e.matmul(
            ps_pv[hb][0:64, 0:128], wdq_b[:, h * 64:(h + 1) * 64], Gb[:],
            start=True, stop=True), waits=wE, inc=MM)
        evE = p.op("vector", lambda e, hb=hb: e.tensor_copy(
            Eh[:], ps_pv[hb][0:64, 0:128]), waits=[(MM, mmE)], inc=VE)
        wC = [(VE, evE)]
        if evC_guard[hb] is not None:
            wC.append((VE, evC_guard[hb]))
        mmC_prev = p.op("tensor", lambda e, h=h, hb=hb: e.matmul(
            ps_bc[hb][:, 0:128],
            wuqr_b[:, h * 128:(h + 1) * 128], Eh[:],
            start=True, stop=True), waits=wC, inc=MM)
        evC_guard[hb] = p.op("vector", lambda e, h=h, hb=hb: e.tensor_copy(
            Call[:, h, :], ps_bc[hb][:, 0:128]), waits=[(MM, mmC_prev)], inc=VE)

    for n2 in range(2):
        psb = ps_qk[n2]
        w0 = [qk_guard[n2]] if qk_guard[n2] is not None else []
        for c in range(DC):
            mmt = p.op("tensor", lambda e, c=c, n2=n2, psb=psb: e.matmul(
                psb[:], wuv_b[:, c, :], woT_b[:, c, n2 * TC:(n2 + 1) * TC],
                start=(c == 0), stop=(c == DC - 1)),
                waits=w0 if c == 0 else [], inc=MM if c == DC - 1 else None)
        ve = p.op("vector", lambda e, n2=n2, psb=psb: e.tensor_copy(
            veffM[:, n2 * 8:(n2 + 1) * 8, :],
            psb[:].rearrange("p (a b) -> p a b", b=64)), waits=[(MM, mmt)], inc=VE)
        qk_guard[n2] = (VE, ve)
    veff_done = qk_guard[1][1]

    qk4 = [[PS[0], PS[1]], [PS[6], PS[7]]]
    qkD = [PSD[0], PSD[3]]
    ps_bc1 = PS[4]
    expf = expT.rearrange("p a b -> p (a b)")
    qkd_guard = {0: qk_guard[0] if qk_guard[0][1] > qk_guard[1][1] else qk_guard[1],
                 1: ("VE_max", 0)}
    _evc = [v for v in evC_guard.values() if v is not None]
    qkd_guard[1] = (VE, max([g_ev] + _evc))
    prev_pv = {}
    pv_bank_free = {0: None, 1: None}
    nrow_free = {0: None, 1: None}
    bcb_free = {0: None, 1: None}
    bc_bank_free = None
    ql_read_done = {}
    wv_read_done = {}
    ql_ready = {}
    wv_ready = {}
    wv_evac_last = None
    qk_pair_done = {0: None, 1: None}
    pend = [None]

    def flush_norm():
        qv2, base2, chunk2, qsl2, rc_cnt, do_store = pend[0]
        pend[0] = None
        wbc = [(VE, rc_cnt)]
        if bc_bank_free[0] is not None:
            wbc.append((VE, bc_bank_free[0]))
        bc_t = p.op("tensor", lambda e, qv2=qv2: e.matmul(
            ps_bc1[:, :], onesP[:, :], nrow[qv2][:],
            start=True, stop=True), waits=wbc, inc=MM)
        nrow_free[qv2] = bc_t
        wcp = [(MM, bc_t)]
        if bcb_free[qv2] is not None:
            wcp.append((VE, bcb_free[qv2]))
        cp_t = p.op("vector", lambda e, qv2=qv2: e.tensor_copy(
            bcb[qv2][:], ps_bc1[0:64, :]), waits=wcp, inc=VE)
        bc_bank_free[0] = cp_t
        rcp_t = p.op("vector", lambda e, qv2=qv2: e.reciprocal(
            bcb[qv2][:], bcb[qv2][:]), waits=[(VE, cp_t)], inc=VE)
        tt_t = p.op("vector",
                    lambda e, base2=base2, chunk2=chunk2, qsl2=qsl2, qv2=qv2:
                    e.tensor_tensor(s1st[base2:base2 + 64, chunk2, qsl2],
                                    ps_pv[qv2][0:64, :], bcb[qv2][:],
                                    ALU.mult),
                    waits=[(VE, rcp_t)], inc=VE)
        pv_bank_free[qv2] = tt_t
        bcb_free[qv2] = tt_t
        if do_store:
            p.op("sync", lambda e, chunk2=chunk2: e.dma_start(
                T["s1"][:, chunk2, :], s1st[:, chunk2, :]),
                waits=[(VE, tt_t)], inc=ST, inc_k=16)

    bc_bank_free = {0: None}
    for h in range(H):
        hb2 = h % 2
        base = (h % 2) * 64
        chunk = h // 2
        half_free = {0: None, 1: None}
        for n in range(NT):
            bank = n % 2
            psb = qk4[0][bank]
            w0 = []
            if n < 2 and qkd_guard[0] is not None:
                w0.append(qkd_guard[0])
            if half_free[bank] is not None:
                w0.append((VE, half_free[bank]))
            mmt = p.op("tensor", lambda e, h=h, n=n, psb=psb: e.matmul(
                psb[:], Call[:, h, :], cqb[:, n * TC:(n + 1) * TC],
                start=True, stop=True), waits=w0, inc=MM)
            wev = [(MM, mmt)]
            if h >= 2:
                wev.append((MM, ql_read_done[h - 2]))
            ev = p.op("vector", lambda e, n=n, psb=psb, hb2=hb2: e.tensor_copy(
                qlb[hb2][:, n * TC:(n + 1) * TC], psb[:]), waits=wev, inc=VE)
            half_free[bank] = ev
        ql_ready[h] = ev
        qkd_guard[0] = (VE, ev)
        for grp in range(4):
            w0 = []
            if wv_evac_last is not None:
                w0.append((VE, wv_evac_last))
            for k8 in range(8):
                ktc = grp * 8 + k8
                mmt = p.op("tensor", lambda e, h=h, ktc=ktc, k8=k8: e.matmul(
                    ps_wv[:, k8 * 64:(k8 + 1) * 64],
                    ckvb[:, ktc * 128:(ktc + 1) * 128], veffM[:, h, :],
                    start=True, stop=True),
                    waits=w0 if k8 == 0 else [], inc=MM if k8 == 7 else None)
            wev = [(MM, mmt)]
            if h >= 2:
                wev.append((MM, wv_read_done[h - 2]))
            wv_evac_last = p.op("vector", lambda e, grp=grp, hb2=hb2: e.tensor_copy(
                wvh[hb2][:, grp * 8:(grp + 1) * 8, 0:64],
                ps_wv[:].rearrange("p (a b) -> p a b", b=64)),
                waits=wev, inc=VE)
        wv_ready[h] = wv_evac_last

        def emit_qk_pair(j, qsl, extra):
            m = j % 2
            wqk = list(extra)
            if qkd_guard[m] is not None:
                wqk.append(qkd_guard[m])
            b0 = 2 * j
            p.op("tensor", lambda e, b0=b0, qsl=qsl, m=m, hb2=hb2:
                 e.matmul(qk4[m][0][:], ckvb[:, b0 * 128:(b0 + 1) * 128],
                          qlb[hb2][:, qsl], start=True, stop=True),
                 waits=wqk)
            qk_pair_done[m] = p.op(
                "tensor", lambda e, b0=b0, qsl=qsl, m=m, hb2=hb2:
                e.matmul(qk4[m][1][:], ckvb[:, (b0 + 1) * 128:(b0 + 2) * 128],
                         qlb[hb2][:, qsl], start=True, stop=True),
                inc=MM)

        for qc in range(NT):
            nkt = 4 * (qc + 1)
            npair = nkt // 2
            qv = qc % 2
            qsl = slice(qc * TC, (qc + 1) * TC)
            extra0 = []
            if qc == 0:
                extra0.append((VE, ql_ready[h]))
                if h == 0:
                    extra0.append((LG, ldm))
                    extra0.append((VE, const_v))
            emit_qk_pair(0, qsl, extra0)
            emit_qk_pair(1, qsl, [])
            for j in range(npair):
                m = j % 2
                s = (2 * j) % 4
                if j == 1 and pend[0] is not None:
                    flush_norm()
                wex = [(MM, qk_pair_done[m])]
                if prev_pv.get(s + 1) is not None:
                    wex.append((MM, prev_pv[s + 1]))
                ex_t = p.op("scalar", lambda e, s=s, m=m: e.activation(
                    expf[:, s * TC:(s + 2) * TC], qkD[m][:], AF.Exp,
                    bias=shift[:, 0:1], scale=0.125), waits=wex, inc=SC)
                qkd_guard[m] = (SC, ex_t)
                mk = {}
                for t in range(2):
                    ktc = 2 * j + t
                    r = ktc - 4 * qc
                    if 0 <= r <= 3:
                        mk[t] = p.op("vector",
                                     lambda e, slot=s + t, r=r: e.tensor_tensor(
                                         expT[:, slot, :], expT[:, slot, :],
                                         msk[:, r, :], ALU.mult),
                                     waits=[(SC, ex_t)], inc=VE)
                pv_t = None
                for t in range(2):
                    ktc = 2 * j + t
                    wpv = []
                    if t == 0:
                        wpv.append((VE, mk[0]) if 0 in mk else (SC, ex_t))
                        if j == 0:
                            if qc == 0:
                                wpv.append((VE, wv_ready[h]))
                            if pv_bank_free[qv] is not None:
                                wpv.append((VE, pv_bank_free[qv]))
                    elif 1 in mk:
                        wpv.append((VE, mk[1]))
                    pv_t = p.op("tensor",
                                lambda e, slot=s + t, ktc=ktc, nkt=nkt, qv=qv,
                                hb2=hb2:
                                e.matmul(ps_pv[qv][0:65, :],
                                         wvh[hb2][:, ktc, 0:65],
                                         expT[:, slot, :],
                                         start=(ktc == 0), stop=(ktc == nkt - 1)),
                                waits=wpv, inc=MM if t == 1 else None)
                prev_pv[s] = prev_pv[s + 1] = pv_t
                if j + 2 < npair:
                    emit_qk_pair(j + 2, qsl, [])
            wrc = [(MM, pv_t)]
            if nrow_free[qv] is not None:
                wrc.append((MM, nrow_free[qv]))
            rc_t = p.op("vector", lambda e, qv=qv: e.tensor_copy(
                nrow[qv][0:1, :], ps_pv[qv][64:65, :]), waits=wrc, inc=VE)
            pend[0] = (qv, base, chunk, qsl, rc_t,
                       h % 2 == 1 and qc == NT - 1)
        ql_read_done[h] = MM.n
        wv_read_done[h] = MM.n
    if pend[0] is not None:
        flush_norm()
    p.wait("sync", [(ST, ST.n)])


def emit_ffn1(p, T, AR, SM, PS):
    nc = p.nc
    LD = p.sem("LDS")
    LG = p.sem("LDG")
    ST = p.sem("STS")
    MM = p.sem("TEN")
    SC = p.sem("SCA")
    VE = p.sem("VEC")
    TQ = 1024
    NTQ = S // TQ
    M1 = FF1 // 128

    o = [0]

    def al(n, a=64):
        o[0] = (o[0] + a - 1) // a * a
        r = o[0]
        o[0] += n
        return r

    hb = AR.view(BF16, al(M1 * TQ * 2), [128, M1, TQ])
    xnt = [AR.view(BF16, al(DC * TQ * 2), [128, DC, TQ]) for _ in range(2)]
    wgb = [AR.view(BF16, al(DC * 128 * 2), [128, DC, 128]) for _ in range(3)]
    wub = [AR.view(BF16, al(DC * 128 * 2), [128, DC, 128]) for _ in range(3)]
    wdb = [AR.view(BF16, al(M1 * 128 * 2), [128, M1, 128]) for _ in range(2)]
    sgb = AR.view(BF16, al(TC * 2), [128, TC])
    sub = AR.view(BF16, al(TC * 2), [128, TC])
    g1b = AR.view(F32, al(TC * 4), [128, TC])
    s1t = [AR.view(BF16, al(DC * TC * 2), [128, DC, TC]) for _ in range(2)]
    x1t = [AR.view(F32, al(DC * TC * 4), [128, DC, TC]) for _ in range(2)]
    ost = [AR.view(F32, al(DC * TC * 4), [128, DC, TC]) for _ in range(2)]
    assert o[0] <= AR.nbytes, o[0]

    wg_r = T["wgT"].rearrange("(c p) (m q) -> p c m q", p=128, q=128)
    wu_r = T["wuT"].rearrange("(c p) (m q) -> p c m q", p=128, q=128)
    wd_r = T["wdT"].rearrange("(k p) (d q) -> p k d q", p=128, q=128)

    ps_g = [PS[0], PS[1]]
    ps_u = [PS[2], PS[3]]
    ps_d = [PS[4], PS[5]]

    ld_xn = {}
    ld_wg = {}
    mm_h = {}
    mm_tq = {}
    h_done = {}
    ev_d = {}
    st_done = {}
    s1x1_ld = {}
    prev_sg = {0: None, 1: None}
    prev_su = {0: None, 1: None}
    prev_g1 = {0: None, 1: None}
    last_ht = None
    last_v2 = None

    for m in range(min(3, M1)):
        p.op("gpsimd", lambda e, m=m: e.dma_start(wgb[m % 3][:], wg_r[:, :, m, :]),
             inc=LG, inc_k=16)
        ld_wg[(0, m)] = p.op("gpsimd", lambda e, m=m: e.dma_start(
            wub[m % 3][:], wu_r[:, :, m, :]), inc=LG, inc_k=16)

    for tq in range(NTQ):
        bi = tq % 2
        sl = slice(tq * TQ, (tq + 1) * TQ)
        wxn = [(MM, mm_tq[tq - 2])] if tq >= 2 else []
        ld_xn[tq] = p.op("sync", lambda e, bi=bi, sl=sl: e.dma_start(
            xnt[bi][:], T["xn1"][:, :, sl]), waits=wxn, inc=LD, inc_k=16)
        for m in range(M1):
            wb = m % 3
            nt_tq, nt_m = (tq, m + 3) if m + 3 < M1 else (tq + 1, (m + 3) % M1)
            if (nt_tq, nt_m) not in ld_wg and nt_tq < NTQ:
                wpre = [(MM, mm_h[(tq, m)])] if (tq, m) in mm_h else []
            for n2 in range(2):
                wmm = [(LD, ld_xn[tq]),
                       (LG, ld_wg[(tq, m) if tq > 0 or m > 2 else (0, m)])]
                if prev_sg[n2] is not None:
                    wmm.append((SC, prev_sg[n2]))
                for c in range(DC):
                    gmm = p.op("tensor", lambda e, c=c, m=m, n2=n2, wb=wb, bi=bi:
                               e.matmul(ps_g[n2][:], wgb[wb][:, c, :],
                                        xnt[bi][:, c, n2 * TC:(n2 + 1) * TC],
                                        start=(c == 0), stop=(c == DC - 1)),
                               waits=wmm if c == 0 else [],
                               inc=MM if c == DC - 1 else None)
                wmu = []
                if prev_su[n2] is not None:
                    wmu.append((VE, prev_su[n2]))
                for c in range(DC):
                    umm = p.op("tensor", lambda e, c=c, m=m, n2=n2, wb=wb, bi=bi:
                               e.matmul(ps_u[n2][:], wub[wb][:, c, :],
                                        xnt[bi][:, c, n2 * TC:(n2 + 1) * TC],
                                        start=(c == 0), stop=(c == DC - 1)),
                               waits=wmu if c == 0 else [],
                               inc=MM if c == DC - 1 else None)
                wsg = [(MM, gmm)]
                if last_ht is not None:
                    wsg.append((VE, last_ht))
                sg_t = p.op("scalar", lambda e, n2=n2: e.activation(
                    sgb[:], ps_g[n2][:], AF.Silu, scale=1.0 / 127.0),
                    waits=wsg, inc=SC)
                prev_sg[n2] = sg_t
                wsu = [(MM, umm)]
                if last_ht is not None:
                    wsu.append((VE, last_ht))
                su_t = p.op("vector", lambda e, n2=n2: e.tensor_scalar_mul(
                    sub[:], ps_u[n2][:], 1.0 / 127.0), waits=wsu, inc=VE)
                prev_su[n2] = su_t
                wht = [(VE, su_t), (SC, sg_t)]
                if m == 0 and n2 == 0 and tq >= 1:
                    wht.append((MM, mm_tq[tq - 1]))
                ht = p.op("vector", lambda e, m=m, n2=n2: e.tensor_tensor(
                    hb[:, m, n2 * TC:(n2 + 1) * TC], sgb[:], sub[:], ALU.mult),
                    waits=wht, inc=VE)
                h_done[(tq, m, n2)] = ht
                last_ht = ht
            mm_h[(tq, m)] = MM.n
            if (nt_tq, nt_m) not in ld_wg and nt_tq < NTQ:
                p.op("gpsimd", lambda e, nt_tq=nt_tq, nt_m=nt_m, wb=wb: e.dma_start(
                    wgb[wb][:], wg_r[:, :, nt_m, :]),
                    waits=[(MM, mm_h[(tq, m)])], inc=LG, inc_k=16)
                ld_wg[(nt_tq, nt_m)] = p.op(
                    "gpsimd", lambda e, nt_tq=nt_tq, nt_m=nt_m, wb=wb: e.dma_start(
                        wub[wb][:], wu_r[:, :, nt_m, :]), inc=LG, inc_k=16)
        h_all = VE.n
        for t2 in range(2):
            tgi = tq * 2 + t2
            bj = tgi % 2
            sl2 = slice(tgi * TC, (tgi + 1) * TC)
            wld = []
            if tgi >= 2:
                wld.append((ST, st_done[tgi - 2]))
            s1x1_ld[tgi] = p.op("sync", lambda e, bj=bj, sl2=sl2: e.dma_start(
                s1t[bj][:], T["s1"][:, :, sl2]), waits=wld, inc=LD, inc_k=16)
            s1x1_ld[tgi] = p.op("sync", lambda e, bj=bj, sl2=sl2: e.dma_start(
                x1t[bj][:], T["x1"][:, :, sl2]), inc=LD, inc_k=16)
            for dc in range(DC):
                wb2 = dc % 2
                pu = (tgi, dc - 2) if dc >= 2 else (tgi - 1, DC - 2 + dc)
                wpre = [(MM, ev_d[pu][0])] if pu in ev_d else []
                ldw = p.op("gpsimd", lambda e, dc=dc, wb2=wb2: e.dma_start(
                    wdb[wb2][:], wd_r[:, :, dc, :]), waits=wpre, inc=LG, inc_k=16)
                wmm = [(LG, ldw), (VE, h_all)]
                if prev_g1[wb2] is not None:
                    wmm.append((SC, prev_g1[wb2]))
                for k in range(M1):
                    dmm = p.op("tensor", lambda e, k=k, dc=dc, wb2=wb2, t2=t2:
                               e.matmul(ps_d[wb2][:], wdb[wb2][:, k, :],
                                        hb[:, k, t2 * TC:(t2 + 1) * TC],
                                        start=(k == 0), stop=(k == M1 - 1)),
                               waits=wmm if k == 0 else [],
                               inc=MM if k == M1 - 1 else None)
                wg1 = [(MM, dmm)]
                if last_v2 is not None:
                    wg1.append((VE, last_v2))
                g1_t = p.op("scalar", lambda e, wb2=wb2: e.activation(
                    g1b[:], ps_d[wb2][:], AF.Silu, scale=1.0 / 127.0),
                    waits=wg1, inc=SC)
                prev_g1[wb2] = g1_t
                v1 = p.op("vector", lambda e, dc=dc, bj=bj: e.tensor_tensor(
                    g1b[:], s1t[bj][:, dc, :], g1b[:], ALU.mult),
                    waits=[(SC, g1_t), (LD, s1x1_ld[tgi])], inc=VE)
                wv2 = [(VE, v1)]
                if tgi >= 2 and dc == 0:
                    wv2.append((ST, st_done[tgi - 2]))
                v2 = p.op("vector", lambda e, dc=dc, bj=bj: e.tensor_tensor(
                    ost[bj][:, dc, :], x1t[bj][:, dc, :], g1b[:], ALU.add),
                    waits=wv2, inc=VE)
                last_v2 = v2
                ev_d[(tgi, dc)] = (MM.n, v2)
            st_done[tgi] = p.op("sync", lambda e, bj=bj, sl2=sl2: e.dma_start(
                T["out"][:, :, sl2], ost[bj][:]),
                waits=[(VE, ev_d[(tgi, DC - 1)][1])], inc=ST, inc_k=16)
        mm_tq[tq] = MM.n
    p.wait("sync", [(ST, ST.n)])


def build_program(phase_names, want_outputs=()):
    nc = bass.Bass("TRN2", target_bir_lowering=False)
    phase_names = [ph for ph in PHASE_ORDER if ph in phase_names]
    reads = set()
    writes = set()
    for ph in phase_names:
        r, w = PHASES[ph]
        reads |= set(r)
        writes |= set(w)

    T = {}
    T["xT"] = nc.dram_tensor("xT", [128, DC, S], F32, kind="ExternalInput")
    if "ffn1" in phase_names:
        T["out"] = nc.dram_tensor("out", [128, DC, S], F32, kind="ExternalOutput")
    for name, (shape, dt_) in INTER.items():
        if name in writes:
            kind = "ExternalOutput" if (name in want_outputs) else "Internal"
            T[name] = nc.dram_tensor(name, shape, dt_, kind=kind)
        elif name in reads:
            T[name] = nc.dram_tensor(name, shape, dt_, kind="ExternalInput")
    for name, (shape, dt_) in WEIGHT_SPECS.items():
        T[name] = nc.dram_tensor(name, shape, dt_, kind="ExternalInput")

    AR = Arena(nc, "AR", 199 * 1024 + 512)
    SM = Arena(nc, "SM", 8 * 1024)
    PSD = [nc.alloc_psum_tensor(f"psd{i}", [128, 2 * TC], F32) for i in range(4)]
    PS = [PSD[i // 2][:, (i % 2) * TC:(i % 2 + 1) * TC] for i in range(8)]
    p = Prog(nc)

    first = True
    for ph in phase_names:
        if not first:
            p.barrier()
        first = False
        if ph == "ln0":
            emit_ln(p, T, AR, PS, "xT", "xn0", f32_dst="xn0f")

        elif ph == "attn0":
            emit_attn0(p, T, AR, SM, PS, PSD)
        elif ph == "router":
            emit_router(p, T, AR, SM, PS)
        elif ph == "moe":
            emit_moe(p, T, AR, SM, PS)
        elif ph == "combln":
            emit_comb_ln(p, T, AR, PS)
        elif ph == "attn1":
            emit_attn1(p, T, AR, SM, PS, PSD)
        elif ph == "ffn1":
            emit_ffn1(p, T, AR, SM, PS)
        else:
            raise ValueError(ph)
    p.realize()
    return nc


def _in_map_for_core(inp, w, consts, b, needed):
    m = {}
    full = dict(w)
    full.update(consts)
    for name in needed:
        if name == "xT":
            m["xT"] = _host_x(inp, b)
        elif name in full:
            m[name] = full[name]
    return m


def kernel(**inputs):
    w = _host_weights(inputs)
    consts = _host_constants()
    nc = build_program(PHASE_ORDER)
    needed = ["xT"] + list(WEIGHT_SPECS.keys())
    in_maps = [_in_map_for_core(inputs, w, consts, b, needed) for b in range(NCORE)]
    res = run_bass_kernel_spmd(nc, in_maps, core_ids=list(range(NCORE)))
    out = np.empty((NCORE, S, D), dtype=np.float32)
    for b in range(NCORE):
        o = res.results[b]["out"]
        out[b] = o.transpose(1, 0, 2).reshape(D, S).T
    return out


# version 16
# speedup vs baseline: 1.4396x; 1.0202x over previous
import math
import numpy as np
import ml_dtypes
import concourse.bass as bass
import concourse.mybir as mybir
from concourse.bass_utils import run_bass_kernel_spmd

F32 = mybir.dt.float32
F32R = mybir.dt.float32r
BF16 = mybir.dt.bfloat16
I32 = mybir.dt.int32
U8 = mybir.dt.uint8
AF = mybir.ActivationFunctionType
AX = mybir.AxisListType
ALU = mybir.AluOpType

NCORE = 8
S = 4096
D = 1024
DC = D // 128
H = 16
HD = 64
TC = 512
NT = S // TC
MOE_I = 4096
NEXP = 9
HH = 2048
FF1 = 3072
KVL = 128
EXP_SHIFT = 3.0

ENGINES = ("sync", "scalar", "vector", "tensor", "gpsimd")


def _tile_vec(v, p=128):
    n = v.shape[0]
    return np.ascontiguousarray(v.reshape(n // p, p).T)


def _host_constants():
    half = HD // 2
    inv_freq = (1.0 / (10000.0 ** (np.arange(half, dtype=np.float32) / half)))
    t = np.arange(S, dtype=np.float32)
    freqs = t[:, None] * inv_freq[None, :]
    cosT = np.cos(freqs).T.astype(np.float32)
    sinT = np.sin(freqs).T.astype(np.float32)
    CC4 = np.ascontiguousarray(np.concatenate([cosT] * 4, axis=0))
    SS4 = np.ascontiguousarray(
        np.concatenate([-sinT, sinT, -sinT, sinT], axis=0))
    i = np.arange(128)[:, None]
    j = np.arange(TC)[None, :]
    masks = np.stack([(i <= j - 128 * r).astype(np.float32) for r in range(4)])
    ident32 = np.eye(128, dtype=np.float32)
    ident16 = np.eye(128, dtype=ml_dtypes.bfloat16)
    return {
        "cc4": CC4, "ss4": SS4, "masks": np.ascontiguousarray(masks),
        "ident32": ident32, "ident16": ident16,
    }


def _host_weights(inp):
    f = np.float32
    w = {}
    w["wqkvT"] = np.ascontiguousarray(inp["L0_attn_w_qkv"].T)
    w["woutT"] = np.ascontiguousarray(inp["L0_attn_w_out"].T)
    w["wr1"] = np.ascontiguousarray(inp["L0_ffn_Wr1"])
    w["br1t"] = _tile_vec(inp["L0_ffn_br1"])
    w["wr2"] = np.ascontiguousarray(inp["L0_ffn_Wr2"])
    w["br2"] = np.ascontiguousarray(inp["L0_ffn_br2"].reshape(8, 1))
    w["ws1"] = np.ascontiguousarray(inp["L0_ffn_Ws1"])
    w["bs1t"] = _tile_vec(inp["L0_ffn_bs1"])
    w["ws2"] = np.ascontiguousarray(inp["L0_ffn_Ws2"])
    w["bs2t"] = _tile_vec(inp["L0_ffn_bs2"])
    w["we1"] = np.ascontiguousarray(inp["L0_ffn_We1"])
    w["be1t"] = np.ascontiguousarray(
        np.stack([_tile_vec(inp["L0_ffn_be1"][e]) for e in range(8)]))
    w["we2"] = np.ascontiguousarray(inp["L0_ffn_We2"])
    w["be2t"] = np.ascontiguousarray(
        np.stack([_tile_vec(inp["L0_ffn_be2"][e]) for e in range(8)]))
    w["wdqT"] = np.ascontiguousarray(inp["L1_attn_W_dq"].T)
    w["wdq"] = np.ascontiguousarray(inp["L1_attn_W_dq"])
    w["wuq"] = np.ascontiguousarray(inp["L1_attn_W_uq"])
    w["wuqr"] = np.ascontiguousarray(
        inp["L1_attn_W_uq"].reshape(16, 64, 128).transpose(1, 0, 2))
    w["wdkvT"] = np.ascontiguousarray(inp["L1_attn_W_dkv"].T)
    w["wuk"] = np.ascontiguousarray(inp["L1_attn_W_uk"])
    w["wuv"] = np.ascontiguousarray(inp["L1_attn_W_uv"])
    w["woT"] = np.ascontiguousarray(inp["L1_attn_W_o"].T)
    w["wgT"] = np.ascontiguousarray(inp["L1_ffn_w_gate"].T)
    w["wuT"] = np.ascontiguousarray(inp["L1_ffn_w_up"].T)
    w["wdT"] = np.ascontiguousarray(inp["L1_ffn_w_down"].T)
    return w


def _host_x(inp, b):
    xb = inp["L0_attn_x"][b]
    return np.ascontiguousarray(xb.T.reshape(DC, 128, S).transpose(1, 0, 2))


INTER = {
    "xn0": ([128, DC, S], BF16),
    "xn0f": ([128, DC, S], F32),
    "ctx0": ([128, DC, S], BF16),
    "s0": ([128, DC, S], BF16),
    "w9": ([NEXP, S], F32),
    "acc": ([128, DC, S], F32),
    "x1": ([128, DC, S], F32),
    "xn1": ([128, DC, S], BF16),
    "s1": ([128, DC, S], BF16),
}

WEIGHT_SPECS = {
    "wqkvT": ([D, 3 * D], F32), "woutT": ([D, D], F32),
    "wr1": ([D, 256], F32), "br1t": ([128, 2], F32),
    "wr2": ([256, 8], F32), "br2": ([8, 1], F32),
    "ws1": ([D, MOE_I], F32), "bs1t": ([128, 32], F32),
    "ws2": ([MOE_I, D], F32), "bs2t": ([128, 8], F32),
    "we1": ([8, D, MOE_I], F32), "be1t": ([8, 128, 32], F32),
    "we2": ([8, MOE_I, D], F32), "be2t": ([8, 128, 8], F32),
    "wdqT": ([D, KVL], F32), "wdq": ([KVL, D], F32),
    "wuq": ([D, KVL], F32), "wuqr": ([64, 16, KVL], F32),
    "wdkvT": ([D, KVL], F32),
    "wuk": ([D, KVL], F32), "wuv": ([D, KVL], F32), "woT": ([D, D], F32),
    "wgT": ([D, FF1], I32), "wuT": ([D, FF1], I32), "wdT": ([FF1, D], I32),
    "cc4": ([128, S], F32), "ss4": ([128, S], F32),
    "masks": ([4, 128, TC], F32),
    "ident32": ([128, 128], F32), "ident16": ([128, 128], BF16),
}

PHASES = {
    "ln0": ([], ["xn0", "xn0f"]),
    "attn0": (["xn0"], ["s0"]),
    "router": (["xn0f"], ["w9"]),
    "moe": (["xn0", "w9"], ["acc"]),
    "combln": (["s0", "acc"], ["x1", "xn1"]),
    "attn1": (["xn1"], ["s1"]),
    "ffn1": (["xn1", "x1", "s1"], []),
}
PHASE_ORDER = ["ln0", "attn0", "router", "moe", "combln", "attn1", "ffn1"]


class CSem:
    def __init__(self, nc, name):
        self.h = nc.alloc_semaphore(name)
        self.n = 0


class Prog:
    def __init__(self, nc):
        self.nc = nc
        self.q = {e: [] for e in ENGINES}
        self.sems = {}

    def sem(self, name):
        if name not in self.sems:
            self.sems[name] = CSem(self.nc, name)
        return self.sems[name]

    def emit(self, eng, fn):
        self.q[eng].append(fn)

    def op(self, eng, call, waits=(), inc=None, inc_k=1):
        tgt = None
        if inc is not None:
            inc.n += inc_k
            tgt = inc.n
        wl = [(cs.h, v) for (cs, v) in waits]
        if inc is not None:
            ih, iv = inc.h, inc_k

            def fn(e, call=call, wl=wl, ih=ih, iv=iv):
                for h, v in wl:
                    e.wait_ge(h, v)
                call(e).then_inc(ih, iv)
        else:
            def fn(e, call=call, wl=wl):
                for h, v in wl:
                    e.wait_ge(h, v)
                call(e)
        self.emit(eng, fn)
        return tgt

    def wait(self, eng, waits):
        wl = [(cs.h, v) for (cs, v) in waits]

        def fn(e, wl=wl):
            for h, v in wl:
                e.wait_ge(h, v)
        self.emit(eng, fn)

    def barrier(self):
        bar = self.sem("BAR")
        bar.n += len(ENGINES)
        tgt = bar.n
        for eng in ENGINES:
            def fn(e, h=bar.h, t=tgt):
                e.drain().then_inc(h, 1)
                e.wait_ge(h, t)
            self.emit(eng, fn)

    def realize(self):
        with self.nc.Block() as blk:
            blk.sync(lambda e: [f(e) for f in self.q["sync"]])
            blk.scalar(lambda e: [f(e) for f in self.q["scalar"]])
            blk.vector(lambda e: [f(e) for f in self.q["vector"]])
            blk.tensor(lambda e: [f(e) for f in self.q["tensor"]])
            blk.gpsimd(lambda e: [f(e) for f in self.q["gpsimd"]])


class Arena:
    def __init__(self, nc, name, nbytes):
        self.nc = nc
        self.name = name
        self.t = nc.alloc_sbuf_tensor(name, [128, nbytes], U8)
        self.nbytes = nbytes
        self.addr = nc.lookup_mloc(self.t).addr
        self._n_alias = 0

    def alias(self, dtype, off, shape):
        esz = {F32: 4, F32R: 4, BF16: 2, I32: 4, U8: 1}[dtype]
        cols = 1
        for s_ in shape[1:]:
            cols *= s_
        assert off + cols * esz <= self.nbytes
        self._n_alias += 1
        t = self.nc.alloc_sbuf_tensor_at(
            f"{self.name}_al{self._n_alias}", [shape[0], cols], dtype,
            offset=self.addr + off)
        ap = t[:, :]
        if len(shape) == 3:
            ap = ap.rearrange("p (a b) -> p a b", a=shape[1], b=shape[2])
        elif len(shape) == 4:
            ap = ap.rearrange("p (a b c) -> p a b c",
                              a=shape[1], b=shape[2], c=shape[3])
        return ap

    def view(self, dtype, off, shape):
        esz = {F32: 4, F32R: 4, BF16: 2, I32: 4, U8: 1}[dtype]
        assert off % 64 == 0, off
        cols = 1
        for s_ in shape[1:]:
            cols *= s_
        assert off + cols * esz <= self.nbytes, (off, cols * esz, self.nbytes)
        ap = self.t.bitcast(dtype)[0:shape[0], off // esz: off // esz + cols]
        if len(shape) == 2:
            return ap
        if len(shape) == 3:
            return ap.rearrange("p (a b) -> p a b", a=shape[1], b=shape[2])
        if len(shape) == 4:
            return ap.rearrange("p (a b c) -> p a b c",
                                a=shape[1], b=shape[2], c=shape[3])
        raise ValueError(shape)


def emit_ln(p, T, AR, PS, src_name, dst_name, f32_dst=None):
    nc = p.nc
    LD = p.sem("LDS")
    ST = p.sem("STS")
    LG = p.sem("LDG")
    MM = p.sem("TEN")
    SC = p.sem("SCA")
    VE = p.sem("VEC")
    src, dst = T[src_name], T[dst_name]
    dstf = T[f32_dst] if f32_dst else None

    xt = [AR.view(F32, 0, [128, DC, TC]), AR.view(F32, 16384, [128, DC, TC])]
    x2 = AR.view(F32, 32768, [128, DC, TC])
    xo = [AR.view(F32, 49152, [128, DC, TC]), AR.view(F32, 81920, [128, DC, TC])]
    srow = AR.view(F32, 114688, [1, 6, TC])
    ones8 = AR.view(F32, 127232, [128, 8])
    ones_row = AR.view(F32, 127296, [1, 128])

    ps_sum = PS[0][0:8, :]
    ps_sq = PS[1][0:8, :]
    ps_nm = PS[2]
    ps_rs = PS[3]

    p.op("vector", lambda e: e.memset(ones8[:], 1.0), inc=VE)
    p.op("vector", lambda e: e.memset(ones_row[:], 1.0), inc=VE)
    wrm = AR.view(BF16, 127808, [128, 128])
    wm_t = p.op("vector", lambda e: e.memset(wrm[:], 0.0), inc=VE)
    for wi_ in range(40):
        p.op("tensor", lambda e: e.matmul(
            PS[4][:, 0:128], wrm[:], wrm[:], start=True, stop=True),
            waits=[(VE, wm_t)] if wi_ == 0 else [])
    ones_ready = VE.n

    st_base = ST.n
    ln_done = {}
    ln_store_lg = {}
    ln_store_st = {}
    prev_t1 = prev_t2 = prev_n2 = None
    for it in range(NT):
        buf = xt[it % 2]
        w = []
        if it >= 2:
            w.append((VE, ln_done[it - 2]))
        ld_t = p.op("sync", lambda e, b=buf, i=it: e.dma_start(
            b[:], src[:, :, i * TC:(i + 1) * TC]), waits=w, inc=LD, inc_k=16)
        wsq = [(LD, ld_t)]
        if prev_n2 is not None:
            wsq.append((VE, prev_n2))
        sq_t = p.op("vector", lambda e, b=buf: e.tensor_tensor(
            x2[:], b[:], b[:], ALU.mult), waits=wsq, inc=VE)
        for c in range(DC):
            w0 = [(LD, ld_t), (VE, ones_ready)]
            if c == 0 and prev_t1 is not None:
                w0.append((VE, prev_t1))
            p.op("tensor", lambda e, b=buf, c=c: e.matmul(
                ps_sum[:], ones8[:], b[:, c, :], start=(c == 0), stop=(c == DC - 1)),
                waits=w0 if c == 0 else [])
        for c in range(DC):
            last = c == DC - 1
            w0 = [(VE, sq_t)]
            if c == 0 and prev_t2 is not None:
                w0.append((VE, prev_t2))
            t = p.op("tensor", lambda e, c=c: e.matmul(
                ps_sq[:], ones8[:], x2[:, c, :], start=(c == 0), stop=(c == DC - 1)),
                waits=w0 if c == 0 else [],
                inc=MM if last else None)
        mm_stats = MM.n
        def rc(call, waits=()):
            return p.op("vector", call, waits=waits, inc=VE)
        t1 = rc(lambda e: e.tensor_scalar_mul(srow[:, 0, :], ps_sum[0:1, :], 1.0 / D),
                waits=[(MM, mm_stats)])
        t2 = rc(lambda e: e.tensor_scalar(srow[:, 1, :], ps_sq[0:1, :], 1.0 / D, None,
                                          ALU.mult), waits=[(VE, t1)])
        t3 = rc(lambda e: e.tensor_tensor(srow[:, 4, :], srow[:, 0, :], srow[:, 0, :],
                                          ALU.mult), waits=[(VE, t2)])
        t4 = rc(lambda e: e.tensor_tensor(srow[:, 1, :], srow[:, 1, :], srow[:, 4, :],
                                          ALU.subtract), waits=[(VE, t3)])
        t4b = rc(lambda e: e.tensor_scalar_add(srow[:, 1, :], srow[:, 1, :], 1e-5),
                 waits=[(VE, t4)])
        t5 = p.op("scalar", lambda e: e.activation(
            srow[:, 2, :], srow[:, 1, :], AF.Sqrt), waits=[(VE, t4b)], inc=SC)
        t6 = rc(lambda e: e.reciprocal(srow[:, 2, :], srow[:, 2, :]), waits=[(SC, t5)])
        t7 = rc(lambda e: e.tensor_scalar_mul(srow[:, 3, :], srow[:, 0, :], -1.0),
                waits=[(VE, t6)])
        wbc = [(VE, t7)]
        if prev_n2 is not None:
            wbc.append((VE, prev_n2))
        p.op("tensor", lambda e: e.matmul(
            ps_nm[:], ones_row[:], srow[:, 3, :], start=True, stop=True),
            waits=wbc)
        bc_t = p.op("tensor", lambda e: e.matmul(
            ps_rs[:], ones_row[:], srow[:, 2, :], start=True, stop=True),
            inc=MM)
        ob = xo[it % 2]
        wst = []
        if it >= 2:
            wst.append((LG, ln_store_lg[it - 2]))
            if dstf is not None:
                wst.append((ST, ln_store_st[it - 2]))
        n1 = p.op("vector", lambda e, b=buf, ob=ob: e.tensor_tensor(
            x2[:], b[:], ps_nm[:, None, :].broadcast_to([128, DC, TC]), ALU.add),
            waits=[(MM, bc_t)] + wst, inc=VE)
        n2 = p.op("vector", lambda e, ob=ob: e.tensor_tensor(
            ob[:], x2[:], ps_rs[:, None, :].broadcast_to([128, DC, TC]), ALU.mult),
            waits=[(VE, n1)], inc=VE)
        ln_done[it] = n2
        prev_t1, prev_t2, prev_n2 = t1, t2, n2
        ln_store_lg[it] = p.op("gpsimd", lambda e, ob=ob, i=it: e.dma_start(
            dst[:, :, i * TC:(i + 1) * TC], ob[:]), waits=[(VE, n2)], inc=LG, inc_k=16)
        if dstf is not None:
            ln_store_st[it] = p.op("sync", lambda e, ob=ob, i=it: e.dma_start(
                dstf[:, :, i * TC:(i + 1) * TC], ob[:]), waits=[(VE, n2)],
                inc=ST, inc_k=16)
    p.wait("sync", [(ST, ST.n)])
    p.wait("gpsimd", [(LG, LG.n)])


def emit_attn0(p, T, AR, SM, PS, PSD):
    nc = p.nc
    LD = p.sem("LDS")
    LG = p.sem("LDG")
    ST = p.sem("STS")
    MM = p.sem("TEN")
    SC = p.sem("SCA")
    VE = p.sem("VEC")
    PL = p.sem("PLC")

    o = [0]

    def al(n, a=64):
        o[0] = (o[0] + a - 1) // a * a
        r = o[0]
        o[0] += n
        return r

    xn = AR.view(BF16, al(DC * S * 2), [128, DC, S])
    ctxa = AR.view(BF16, al(DC * S * 2), [128, DC, S])
    scratch0 = o[0]
    cc4b = AR.view(BF16, al(S * 2), [128, S])
    ss4b = AR.view(BF16, al(S * 2), [128, S])
    qpad = [AR.view(BF16, al(S * 2), [128, S]) for _ in range(2)]
    kpb = AR.view(BF16, al(S * 2), [128, S])
    vt = [AR.view(BF16, al(32 * 65 * 2), [128, 32, 65]) for _ in range(2)]
    wqb = [AR.view(BF16, al(DC * 128 * 2), [128, DC, 128]) for _ in range(2)]
    wkb = [AR.view(BF16, al(DC * 128 * 2), [128, DC, 128]) for _ in range(2)]
    wvb = [AR.view(BF16, al(DC * 128 * 2), [128, DC, 128]) for _ in range(2)]
    qsw2 = [AR.view(BF16, al(TC * 2), [128, TC]) for _ in range(2)]
    expT = AR.view(BF16, al(4 * TC * 2), [128, 4, TC])
    nrow = [SM.view(BF16, 4096 + i * TC * 2, [128, TC]) for i in range(2)]
    bcb = [AR.view(F32, al(TC * 4), [64, TC]),
           SM.view(F32, 6144, [64, TC])]
    onesP = AR.view(BF16, al(256), [128, 128])
    shift = AR.view(F32, al(64), [128, 1])
    msk = SM.view(BF16, 0, [128, 4, TC])

    assert o[0] <= AR.nbytes, o[0]
    o[0] = scratch0
    wo = AR.view(BF16, al(DC * DC * 128 * 2), [128, DC, DC, 128])
    s0st = [AR.view(BF16, al(DC * TC * 2), [128, DC, TC]) for _ in range(2)]
    assert o[0] <= AR.nbytes, o[0]

    qk4 = [[PS[0], PS[1]], [PS[6], PS[7]]]
    qkD = [PSD[0], PSD[3]]
    ps_pv = [PS[2], PS[3]]
    ps_bc1 = PS[4]
    ps_v = PS[5]
    ps_p = PS[0]
    expf = expT.rearrange("p a b -> p (a b)")

    p.op("vector", lambda e: e.memset(shift[:], -EXP_SHIFT), inc=VE)
    p.op("vector", lambda e: e.memset(onesP[:], 0.0), inc=VE)
    p.op("vector", lambda e: e.memset(onesP[0:1, :], 1.0), inc=VE)
    p.op("vector", lambda e: e.memset(nrow[0][:], 0.0), inc=VE)
    p.op("vector", lambda e: e.memset(nrow[1][:], 0.0), inc=VE)
    p.op("vector", lambda e: e.memset(qpad[0][64:128, :], 0.0), inc=VE)
    p.op("vector", lambda e: e.memset(qpad[1][0:64, :], 0.0), inc=VE)
    p.op("vector", lambda e: e.memset(vt[0][:, :, 64:65], 1.0), inc=VE)
    p.op("vector", lambda e: e.memset(vt[1][:, :, 64:65], 1.0), inc=VE)
    const_v = VE.n
    ld0 = p.op("sync", lambda e: e.dma_start(xn[:], T["xn0"][:]), inc=LD, inc_k=16)
    for wi_ in range(60):
        p.op("tensor", lambda e: e.matmul(
            PS[1][:, 0:128], onesP[:], onesP[:], start=True, stop=True),
            waits=[(VE, const_v)] if wi_ == 0 else [])
    p.op("gpsimd", lambda e: e.dma_start(cc4b[:], T["cc4"][:]), inc=LG, inc_k=16)
    ld1 = p.op("gpsimd", lambda e: e.dma_start(ss4b[:], T["ss4"][:]),
               inc=LG, inc_k=16)
    ldm = p.op("gpsimd", lambda e: e.dma_start(
        msk[:], T["masks"].rearrange("r p t -> p r t")), inc=LG, inc_k=16)

    wqkvT = T["wqkvT"].rearrange("(c p) n -> p c n", p=128)

    wready = {}
    for dst, col0 in ((wqb[0], 0), (wkb[0], D), (wvb[0], 2 * D)):
        wready[0] = p.op("gpsimd", lambda e, d=dst, c=col0: e.dma_start(
            d[:], wqkvT[:, :, c:c + 128]), inc=LG, inc_k=16)

    prev_pv = {}
    pv_bank_free = {0: None, 1: None}
    nrow_free = {0: None, 1: None}
    bcb_free = {0: None, 1: None}
    bc_bank_free = {0: None}
    qkd_guard = {0: None, 1: None}
    qk_pair_done = {0: None, 1: None}
    pend = [None]
    attn_end = {}
    proj_end = {}
    vev_prev = None
    ps_p_free = {0: None}
    tt2_prev = {0: None, 1: None}

    def flush_norm():
        qv2, base2, g2, qsl2, rc_cnt, _ = pend[0]
        pend[0] = None
        wbc = [(VE, rc_cnt)]
        if bc_bank_free[0] is not None:
            wbc.append((VE, bc_bank_free[0]))
        bc_t = p.op("tensor", lambda e, qv2=qv2: e.matmul(
            ps_bc1[:, :], onesP[:, :], nrow[qv2][:],
            start=True, stop=True), waits=wbc, inc=MM)
        nrow_free[qv2] = bc_t
        wcp = [(MM, bc_t)]
        if bcb_free[qv2] is not None:
            wcp.append((VE, bcb_free[qv2]))
        cp_t = p.op("vector", lambda e, qv2=qv2: e.tensor_copy(
            bcb[qv2][:], ps_bc1[0:64, :]), waits=wcp, inc=VE)
        bc_bank_free[0] = cp_t
        rcp_t = p.op("vector", lambda e, qv2=qv2: e.reciprocal(
            bcb[qv2][:], bcb[qv2][:]), waits=[(VE, cp_t)], inc=VE)
        tt_t = p.op("vector",
                    lambda e, base2=base2, g2=g2, qsl2=qsl2, qv2=qv2:
                    e.tensor_tensor(ctxa[base2:base2 + 64, g2, qsl2],
                                    ps_pv[qv2][0:64, :], bcb[qv2][:],
                                    ALU.mult),
                    waits=[(VE, rcp_t)], inc=VE)
        pv_bank_free[qv2] = tt_t
        bcb_free[qv2] = tt_t
        return tt_t

    for g in range(H // 2):
        wb = g % 2
        for wi, (wsl, dstp) in enumerate(((wqb[wb], None), (wkb[wb], kpb))):
            for ch in range(NT):
                cp = ch % 2
                sl = slice(ch * TC, (ch + 1) * TC)
                w0 = [(LD, ld0), (LG, wready[g])]
                if wi == 0 and ch == 0 and qkd_guard[0] is not None:
                    w0.append(qkd_guard[0])
                if ps_p_free[0] is not None:
                    w0.append(ps_p_free[0])
                for c in range(DC):
                    mmt = p.op("tensor",
                               lambda e, c=c, sl=sl, wsl=wsl: e.matmul(
                                   ps_p[:], wsl[:, c, :], xn[:, c, sl],
                                   start=(c == 0), stop=(c == DC - 1)),
                               waits=w0 if c == 0 else [],
                               inc=MM if c == DC - 1 else None)
                    w0 = []
                wev = [(MM, mmt), (LG, ld1)]
                if tt2_prev[cp] is not None:
                    wev.append((VE, tt2_prev[cp]))
                for q_ in range(4):
                    src = (q_ ^ 1) * 32
                    p.op("vector",
                         lambda e, q_=q_, src=src, cp=cp, sl=sl:
                         e.tensor_tensor(
                             qsw2[cp][q_ * 32:(q_ + 1) * 32, :],
                             ps_p[src:src + 32, :],
                             ss4b[q_ * 32:(q_ + 1) * 32, sl], ALU.mult),
                         waits=wev if q_ == 0 else [], inc=VE)
                    wev = []
                wtt = []
                if wi == 0 and ch == 0 and g >= 1:
                    wtt.append((MM, attn_end[g - 1]))
                if wi == 0:
                    for hh2 in range(2):
                        b2 = hh2 * 64
                        t1 = p.op("vector",
                                  lambda e, sl=sl, b2=b2, hh2=hh2:
                                  e.tensor_tensor(
                                      qpad[hh2][b2:b2 + 64, sl],
                                      ps_p[b2:b2 + 64, :],
                                      cc4b[b2:b2 + 64, sl], ALU.mult),
                                  waits=wtt, inc=VE)
                        wtt = []
                        t2 = p.op("vector",
                                  lambda e, cp=cp, sl=sl, b2=b2, hh2=hh2:
                                  e.tensor_tensor(
                                      qpad[hh2][b2:b2 + 64, sl],
                                      qpad[hh2][b2:b2 + 64, sl],
                                      qsw2[cp][b2:b2 + 64, :], ALU.add),
                                  waits=[(VE, t1)], inc=VE)
                    ps_p_free[0] = (VE, t1)
                else:
                    t1 = p.op("vector", lambda e, sl=sl, dstp=dstp:
                              e.tensor_tensor(dstp[:, sl], ps_p[:],
                                              cc4b[:, sl], ALU.mult),
                              waits=wtt, inc=VE)
                    ps_p_free[0] = (VE, t1)
                    t2 = p.op("vector", lambda e, cp=cp, sl=sl, dstp=dstp:
                              e.tensor_tensor(dstp[:, sl], dstp[:, sl],
                                              qsw2[cp][:], ALU.add),
                              waits=[(VE, t1)], inc=VE)
                tt2_prev[cp] = t2
        qkd_guard[0] = ps_p_free[0]
        for tq in range(8):
            w0 = [(LD, ld0), (LG, wready[g])]
            if vev_prev is not None:
                w0.append((VE, vev_prev))
            if tq == 0 and qkd_guard[1] is not None:
                w0.append(qkd_guard[1])
            for sub in range(4):
                for c in range(DC):
                    mmt = p.op("tensor",
                               lambda e, c=c, sub=sub, tq=tq, wb=wb: e.matmul(
                                   ps_v[:, sub * 128:(sub + 1) * 128],
                                   xn[:, c, tq * 512 + sub * 128:
                                      tq * 512 + sub * 128 + 128],
                                   wvb[wb][:, c, :],
                                   start=(c == 0), stop=(c == DC - 1)),
                               waits=w0 if (sub == 0 and c == 0) else [],
                               inc=MM if (sub == 3 and c == DC - 1) else None)
                w0 = []
            pv4 = ps_v.rearrange("p (s n) -> p s n", s=4)
            p.op("vector", lambda e, tq=tq: e.tensor_copy(
                vt[0][:, tq * 4:(tq + 1) * 4, 0:64], pv4[:, :, 0:64]),
                waits=[(MM, mmt)], inc=VE)
            vev_prev = p.op("vector", lambda e, tq=tq: e.tensor_copy(
                vt[1][:, tq * 4:(tq + 1) * 4, 0:64], pv4[:, :, 64:128]),
                inc=VE)
        vproj_done = vev_prev
        proj_end[g] = MM.n
        if g + 1 < H // 2:
            wg = (g + 1) % 2
            wld_guard = [(MM, proj_end[g - 1])] if g >= 1 else []
            for dst, col0 in ((wqb[wg], (g + 1) * 128),
                              (wkb[wg], D + (g + 1) * 128),
                              (wvb[wg], 2 * D + (g + 1) * 128)):
                wready[g + 1] = p.op("gpsimd", lambda e, d=dst, c=col0:
                                     e.dma_start(d[:], wqkvT[:, :, c:c + 128]),
                                     waits=wld_guard, inc=LG, inc_k=16)
                wld_guard = []

        for hh in range(2):
            base = hh * 64
            myvt = vt[hh]

            def emit_qk_pair(j, qsl, extra):
                m = j % 2
                wqk = list(extra)
                if qkd_guard[m] is not None:
                    wqk.append(qkd_guard[m])
                b0 = 2 * j
                p.op("tensor", lambda e, b0=b0, qsl=qsl, m=m, hh=hh:
                     e.matmul(qk4[m][0][:],
                              kpb[:, b0 * 128:(b0 + 1) * 128],
                              qpad[hh][:, qsl], start=True, stop=True),
                     waits=wqk)
                qk_pair_done[m] = p.op(
                    "tensor", lambda e, b0=b0, qsl=qsl, m=m, hh=hh:
                    e.matmul(qk4[m][1][:],
                             kpb[:, (b0 + 1) * 128:(b0 + 2) * 128],
                             qpad[hh][:, qsl], start=True, stop=True),
                    inc=MM)

            for qc in range(NT):
                nkt = 4 * (qc + 1)
                npair = nkt // 2
                qv = qc % 2
                qsl = slice(qc * TC, (qc + 1) * TC)
                extra0 = []
                if qc == 0 and hh == 0:
                    extra0.append((VE, tt2_prev[1]))
                    if g == 0:
                        extra0.append((LG, ldm))
                        extra0.append((VE, const_v))
                emit_qk_pair(0, qsl, extra0)
                emit_qk_pair(1, qsl, [])
                for j in range(npair):
                    m = j % 2
                    s = (2 * j) % 4
                    if j == 1 and pend[0] is not None:
                        flush_norm()
                    wex = [(MM, qk_pair_done[m])]
                    if prev_pv.get(s + 1) is not None:
                        wex.append((MM, prev_pv[s + 1]))
                    ex_t = p.op("scalar", lambda e, s=s, m=m: e.activation(
                        expf[:, s * TC:(s + 2) * TC], qkD[m][:], AF.Exp,
                        bias=shift[:, 0:1], scale=0.125), waits=wex, inc=SC)
                    qkd_guard[m] = (SC, ex_t)
                    mk = {}
                    for t in range(2):
                        ktc = 2 * j + t
                        r = ktc - 4 * qc
                        if 0 <= r <= 3:
                            mk[t] = p.op("vector",
                                         lambda e, slot=s + t, r=r:
                                         e.tensor_tensor(
                                             expT[:, slot, :], expT[:, slot, :],
                                             msk[:, r, :], ALU.mult),
                                         waits=[(SC, ex_t)], inc=VE)
                    pv_t = None
                    for t in range(2):
                        ktc = 2 * j + t
                        wpv = []
                        if t == 0:
                            wpv.append((VE, mk[0]) if 0 in mk else (SC, ex_t))
                            if j == 0:
                                if qc == 0 and hh == 0:
                                    wpv.append((VE, vproj_done))
                                if pv_bank_free[qv] is not None:
                                    wpv.append((VE, pv_bank_free[qv]))
                        elif 1 in mk:
                            wpv.append((VE, mk[1]))
                        pv_t = p.op("tensor",
                                    lambda e, slot=s + t, ktc=ktc, nkt=nkt,
                                    qv=qv, myvt=myvt:
                                    e.matmul(ps_pv[qv][0:65, :],
                                             myvt[:, ktc, 0:65],
                                             expT[:, slot, :],
                                             start=(ktc == 0),
                                             stop=(ktc == nkt - 1)),
                                    waits=wpv, inc=MM if t == 1 else None)
                    prev_pv[s] = prev_pv[s + 1] = pv_t
                    if j + 2 < npair:
                        emit_qk_pair(j + 2, qsl, [])
                wrc = [(MM, pv_t)]
                if nrow_free[qv] is not None:
                    wrc.append((MM, nrow_free[qv]))
                rc_t = p.op("vector", lambda e, qv=qv: e.tensor_copy(
                    nrow[qv][0:1, :], ps_pv[qv][64:65, :]), waits=wrc, inc=VE)
                pend[0] = (qv, base, g, qsl, rc_t, False)
        attn_end[g] = MM.n
    final_tt = flush_norm()

    lw = p.op("gpsimd", lambda e: e.dma_start(
        wo[:], T["woutT"].rearrange("(fc p) (dc m) -> p fc dc m", p=128, m=128)),
        waits=[(MM, attn_end[H // 2 - 1])], inc=LG, inc_k=16)
    ev_done = {}
    st_done = {}
    for it in range(NT):
        bi = it % 2
        for dc in range(DC):
            bank = dc % 2
            wmm = []
            if it == 0 and dc == 0:
                wmm = [(LG, lw), (VE, final_tt)]
            prev = it * DC + dc - 2
            if prev >= 0:
                wmm.append((SC, ev_done[prev]))
            for fc in range(DC):
                mmt = p.op("tensor",
                           lambda e, fc=fc, dc=dc, it=it, bank=bank: e.matmul(
                               qk4[0][bank][:], wo[:, fc, dc, :],
                               ctxa[:, fc, it * TC:(it + 1) * TC],
                               start=(fc == 0), stop=(fc == DC - 1)),
                           waits=wmm if fc == 0 else [],
                           inc=MM if fc == DC - 1 else None)
                wmm = []
            wev = [(MM, mmt)]
            if it >= 2 and dc == 0:
                wev.append((ST, st_done[it - 2]))
            ev_done[it * DC + dc] = p.op(
                "scalar", lambda e, dc=dc, bi=bi, bank=bank: e.copy(
                    s0st[bi][:, dc, :], qk4[0][bank][:]), waits=wev, inc=SC)
        st_done[it] = p.op("sync", lambda e, it=it, bi=bi: e.dma_start(
            T["s0"][:, :, it * TC:(it + 1) * TC], s0st[bi][:]),
            waits=[(SC, ev_done[it * DC + DC - 1])], inc=ST, inc_k=16)
    p.wait("sync", [(ST, ST.n)])


def emit_router(p, T, AR, SM, PS):
    nc = p.nc
    LD = p.sem("LDS")
    LG = p.sem("LDG")
    ST = p.sem("STS")
    MM = p.sem("TEN")
    SC = p.sem("SCA")
    VE = p.sem("VEC")

    o = [0]

    def al(n, a=64):
        o[0] = (o[0] + a - 1) // a * a
        r = o[0]
        o[0] += n
        return r

    xnt = [AR.alias(F32R, al(DC * TC * 4), [128, DC, TC]) for _ in range(2)]
    r1 = AR.alias(F32R, al(2 * S * 4), [128, 2, S])
    wr1b = AR.alias(F32R, al(DC * 256 * 4), [128, DC, 256])
    wr2b = AR.alias(F32R, al(2 * 8 * 4), [128, 2, 8])
    br1b = AR.view(F32, al(2 * 4), [128, 2])
    br2b = AR.view(F32, al(4), [8, 1])
    lgrow = AR.view(F32, al(S * 4), [8, S])
    idf = AR.view(F32, al(128 * 4), [128, 128])
    gTM = AR.view(F32, al(32 * 8 * 4), [128, 32, 8])
    e8 = AR.view(F32, al(32 * 8 * 4), [128, 32, 8])
    m1 = AR.view(F32, al(32 * 8 * 4), [128, 32, 8])
    m2 = AR.view(F32, al(32 * 8 * 4), [128, 32, 8])
    w9tm = AR.view(F32, al(32 * 9 * 4), [128, 32, 9])
    cols = AR.view(F32, al(32 * 8 * 4), [128, 8, 32])
    w9sb = AR.view(F32, al(S * 4), [9, S])
    assert o[0] <= AR.nbytes

    ps_r = [PS[0], PS[1]]
    ps_l = PS[2][0:8, :]
    ps_t = PS[3]
    ps_t9 = PS[4][0:9, :]

    lw = p.op("gpsimd", lambda e: e.dma_start(
        wr1b[:], T["wr1"].rearrange("(c p) m -> p c m", p=128)), inc=LG, inc_k=16)
    lw = p.op("gpsimd", lambda e: e.dma_start(
        wr2b[:], T["wr2"].rearrange("(c p) m -> p c m", p=128)), inc=LG, inc_k=16)
    p.op("sync", lambda e: e.dma_start(br1b[:], T["br1t"][:]), inc=LD, inc_k=16)
    p.op("sync", lambda e: e.dma_start(br2b[:], T["br2"][:]), inc=LD, inc_k=16)
    ld_c = p.op("sync", lambda e: e.dma_start(idf[:], T["ident32"][:]),
                inc=LD, inc_k=16)

    ev_prev = {0: None, 1: None}
    ev_prev_tile = {}
    lg_prev = None
    for it in range(NT):
        bi = it % 2
        w0 = [(LG, lw), (LD, ld_c)]
        if it >= 2:
            w0.append((SC, ev_prev_tile[it - 2]))
        ld_t = p.op("gpsimd", lambda e, it=it, bi=bi: e.dma_start(
            xnt[bi][:], T["xn0f"][:, :, it * TC:(it + 1) * TC]),
            waits=w0, inc=LG, inc_k=16)
        for m_ in range(2):
            wmm = [(LD, ld_t)]
            if ev_prev[m_] is not None:
                wmm.append((SC, ev_prev[m_]))
            wmm = [(LG, ld_t)] + wmm[1:]
            for c in range(DC):
                mmt = p.op("tensor", lambda e, c=c, m_=m_, bi=bi: e.matmul(
                    ps_r[m_][:], wr1b[:, c, m_ * 128:(m_ + 1) * 128],
                    xnt[bi][:, c, :], start=(c == 0), stop=(c == DC - 1)),
                    waits=wmm if c == 0 else [], inc=MM if c == DC - 1 else None)
            ev_prev[m_] = p.op("scalar", lambda e, m_=m_, it=it: e.activation(
                r1[:, m_, it * TC:(it + 1) * TC], ps_r[m_][:], AF.Relu,
                bias=br1b[:, m_:m_ + 1]), waits=[(MM, mmt)], inc=SC)
        ev_prev_tile[it] = ev_prev[1]
        wl = [(SC, ev_prev[1])]
        if lg_prev is not None:
            wl.append((SC, lg_prev))
        for c2 in range(2):
            mmt = p.op("tensor", lambda e, c2=c2, it=it: e.matmul(
                ps_l[:], wr2b[:, c2, :], r1[:, c2, it * TC:(it + 1) * TC],
                start=(c2 == 0), stop=(c2 == 1)),
                waits=wl if c2 == 0 else [], inc=MM if c2 == 1 else None)
        lg_prev = p.op("scalar", lambda e, it=it: e.activation(
            lgrow[:, it * TC:(it + 1) * TC], ps_l[:], AF.Identity,
            bias=br2b[:, 0:1]), waits=[(MM, mmt)], inc=SC)

    tp_prev = None
    for grp in range(2):
        wtp = [(SC, lg_prev)]
        if tp_prev is not None:
            wtp.append((VE, tp_prev))
        for i in range(16):
            tt = grp * 16 + i
            mmt = p.op("tensor", lambda e, i=i, tt=tt: e.transpose(
                ps_t[:, i * 8:(i + 1) * 8], lgrow[:, tt * 128:(tt + 1) * 128],
                idf[0:8, 0:8]), waits=wtp if i == 0 else [],
                inc=MM if i == 15 else None)
        pt3 = ps_t.rearrange("p (i v) -> p i v", v=8)
        tp_prev = p.op("vector", lambda e, grp=grp: e.tensor_copy(
            gTM[:, grp * 16:(grp + 1) * 16, :], pt3[:, 0:16, :]),
            waits=[(MM, mmt)], inc=VE)

    def vop(call, waits=()):
        return p.op("vector", call, waits=waits, inc=VE)
    mx = cols[:, 0, :]
    e1c = cols[:, 1, :]
    e2c = cols[:, 2, :]
    Sc_ = cols[:, 3, :]
    rS = cols[:, 4, :]
    w1 = cols[:, 5, :]
    w2 = cols[:, 6, :]
    tc_ = cols[:, 7, :]
    t0 = vop(lambda e: e.reduce_max(mx, gTM[:], axis=AX.X), waits=[(VE, tp_prev)])
    t1 = vop(lambda e: e.tensor_tensor(
        e8[:], gTM[:], mx[:, :, None].broadcast_to([128, 32, 8]), ALU.subtract),
        waits=[(VE, t0)])
    t2 = p.op("scalar", lambda e: e.activation(
        e8.rearrange("p a b -> p (a b)"), e8.rearrange("p a b -> p (a b)"),
        AF.Exp), waits=[(VE, t1)], inc=SC)
    t3 = vop(lambda e: e.reduce_sum(Sc_, e8[:], axis=AX.X), waits=[(SC, t2)])
    t4 = vop(lambda e: e.tensor_tensor(
        m1[:], gTM[:], mx[:, :, None].broadcast_to([128, 32, 8]), ALU.is_ge),
        waits=[(SC, t2)])
    t5 = vop(lambda e: e.reduce_max(e1c, e8[:], axis=AX.X), waits=[(VE, t4)])
    t6 = vop(lambda e: e.tensor_scalar(m2[:], m1[:], -1.0, 1.0, ALU.mult, ALU.add),
             waits=[(VE, t4)])
    t7 = vop(lambda e: e.tensor_tensor(gTM[:], e8[:], m2[:], ALU.mult),
             waits=[(VE, t6)])
    t8 = vop(lambda e: e.reduce_max(e2c, gTM[:], axis=AX.X), waits=[(VE, t7)])
    t9 = vop(lambda e: e.tensor_tensor(
        m2[:], gTM[:], e2c[:, :, None].broadcast_to([128, 32, 8]), ALU.is_ge),
        waits=[(VE, t8)])
    t10 = vop(lambda e: e.tensor_tensor(tc_, e1c, e2c, ALU.subtract),
              waits=[(VE, t9), (VE, t5), (VE, t3)])
    t11 = vop(lambda e: e.reciprocal(rS, Sc_), waits=[(VE, t10)])
    t12 = vop(lambda e: e.tensor_tensor(tc_, tc_, rS, ALU.mult), waits=[(VE, t11)])
    t13 = vop(lambda e: e.tensor_scalar_mul(tc_, tc_, 0.5), waits=[(VE, t12)])
    t14 = p.op("scalar", lambda e: e.activation(w1, tc_, AF.Sigmoid),
               waits=[(VE, t13)], inc=SC)
    t15 = vop(lambda e: e.tensor_scalar(w2, w1, -1.0, 1.0, ALU.mult, ALU.add),
              waits=[(SC, t14)])
    t16 = vop(lambda e: e.memset(w9tm[:, :, 0:1], 1.0))
    t17 = vop(lambda e: e.tensor_tensor(
        m1[:], m1[:], w1[:, :, None].broadcast_to([128, 32, 8]), ALU.mult),
        waits=[(VE, t15)])
    t18 = vop(lambda e: e.tensor_tensor(
        m2[:], m2[:], w2[:, :, None].broadcast_to([128, 32, 8]), ALU.mult),
        waits=[(VE, t17)])
    t19 = vop(lambda e: e.tensor_tensor(w9tm[:, :, 1:9], m1[:], m2[:], ALU.add),
              waits=[(VE, t18)])

    tp_prev = None
    for grp in range(8):
        wtp = [(VE, t19)]
        if tp_prev is not None:
            wtp.append((SC, tp_prev))
        for i in range(4):
            tt = grp * 4 + i
            mmt = p.op("tensor", lambda e, i=i, tt=tt: e.transpose(
                ps_t9[:, i * 128:(i + 1) * 128],
                w9tm.rearrange("p a b -> p (a b)")[:, tt * 9:(tt + 1) * 9],
                idf[:, :]), waits=wtp if i == 0 else [],
                inc=MM if i == 3 else None)
        pt4 = ps_t9.rearrange("p (i v) -> p i v", v=128)
        tp_prev = p.op("scalar", lambda e, grp=grp: e.copy(
            w9sb.rearrange("p (i v) -> p i v", v=128)[:, grp * 4:(grp + 1) * 4, :],
            pt4[:, 0:4, :]), waits=[(MM, mmt)], inc=SC)
    p.op("sync", lambda e: e.dma_start(T["w9"][:], w9sb[:]),
         waits=[(SC, tp_prev)], inc=ST, inc_k=16)
    p.wait("sync", [(ST, ST.n)])


def emit_moe(p, T, AR, SM, PS):
    nc = p.nc
    LD = p.sem("LDS")
    LG = p.sem("LDG")
    MM = p.sem("TEN")
    SC = p.sem("SCA")
    VE = p.sem("VEC")

    o = [0]

    def al(n, a=64):
        o[0] = (o[0] + a - 1) // a * a
        r = o[0]
        o[0] += n
        return r

    we1 = [AR.view(BF16, al(DC * 16 * 128 * 2), [128, DC, 16, 128]) for _ in range(2)]
    we2 = [AR.view(BF16, al(16 * DC * 128 * 2), [128, 16, DC, 128]) for _ in range(2)]
    ab = AR.view(BF16, al(16 * TC * 2), [128, 16, TC])
    xnt = [AR.view(BF16, al(DC * TC * 2), [128, DC, TC]) for _ in range(2)]
    accst = [AR.view(F32, al(DC * TC * 4), [128, DC, TC]) for _ in range(2)]
    assert o[0] <= AR.nbytes, o[0]
    wbc = [SM.view(F32, i * 2048, [128, TC]) for i in range(2)]
    be1b = SM.view(F32, 4096, [128, 32])
    be2b = SM.view(F32, 4096 + 128, [128, 8])

    ps_a = [PS[0], PS[1]]
    ps_h = [PS[2], PS[3]]

    we1_r = T["we1"].rearrange("e (c p) (m q) -> e p c m q", p=128, q=128)
    we2_r = T["we2"].rearrange("e (k p) (d q) -> e p k d q", p=128, q=128)
    ws1_r = T["ws1"].rearrange("(c p) (m q) -> p c m q", p=128, q=128)
    ws2_r = T["ws2"].rearrange("(k p) (d q) -> p k d q", p=128, q=128)

    def w_srcs(ei, hh):
        if ei == 0:
            return (ws1_r[:, :, hh * 16:(hh + 1) * 16, :],
                    ws2_r[:, hh * 16:(hh + 1) * 16, :, :])
        return (we1_r[ei - 1, :, :, hh * 16:(hh + 1) * 16, :],
                we2_r[ei - 1, :, hh * 16:(hh + 1) * 16, :, :])

    steps = [(ei, hh) for ei in range(NEXP) for hh in range(2)]
    ld_w = {}
    mm_step = {}
    mm_tile = {}
    ac_t = {}
    sil_last = {}
    stt_last_h0 = {}
    prev_sil = {}
    prev_stt = {}
    prev_stt_tile_map = {}

    for si in range(2):
        ei, hh = steps[si]
        s1, s2 = w_srcs(ei, hh)
        p.op("gpsimd", lambda e, s1=s1, si=si: e.dma_start(we1[si % 2][:], s1),
             inc=LG, inc_k=16)
        ld_w[si] = p.op("gpsimd", lambda e, s2=s2, si=si: e.dma_start(
            we2[si % 2][:], s2), inc=LG, inc_k=16)

    bias_ld = {}
    p.op("sync", lambda e: e.dma_start(be1b[:], T["bs1t"][:]), inc=LD, inc_k=16)
    bias_ld[0] = p.op("sync", lambda e: e.dma_start(be2b[:], T["bs2t"][:]),
                      inc=LD, inc_k=16)

    for si, (ei, hh) in enumerate(steps):
        wb = si % 2
        if hh == 0 and ei > 0:
            wbias = [(SC, sil_last[si - 1]), (VE, stt_last_h0[ei - 1])]
            p.op("sync", lambda e, ei=ei: e.dma_start(
                be1b[:], T["be1t"][ei - 1]), waits=wbias, inc=LD, inc_k=16)
            bias_ld[ei] = p.op("sync", lambda e, ei=ei: e.dma_start(
                be2b[:], T["be2t"][ei - 1]), inc=LD, inc_k=16)
        sec_prev = None
        for tg in range(NT):
            tb = tg % 2
            sl = slice(tg * TC, (tg + 1) * TC)
            idx = si * NT + tg
            wxn = []
            pu = (si, tg - 2) if tg >= 2 else (si - 1, NT - 2 + tg)
            if pu in mm_tile:
                wxn.append((MM, mm_tile[pu]))
            ld_x = p.op("sync", lambda e, tb=tb, sl=sl: e.dma_start(
                xnt[tb][:], T["xn0"][:, :, sl]), waits=wxn, inc=LD, inc_k=16)
            wwb = []
            if pu in prev_stt_tile_map:
                wwb.append((VE, prev_stt_tile_map[pu]))
            ld_wb = p.op("gpsimd", lambda e, tb=tb, ei=ei, sl=sl: e.dma_start(
                wbc[tb][:], T["w9"][ei:ei + 1, sl].partition_broadcast(128)),
                waits=wwb, inc=LG, inc_k=16)
            for m_ in range(16):
                bank = m_ % 2
                wmm = [(LD, ld_x), (LG, ld_w[si])] if m_ < 2 else []
                if prev_sil.get(bank) is not None:
                    wmm.append((SC, prev_sil[bank]))
                for c in range(DC):
                    mmt = p.op("tensor",
                               lambda e, c=c, m_=m_, wb=wb, tb=tb, bank=bank:
                               e.matmul(ps_a[bank][:], we1[wb][:, c, m_, :],
                                        xnt[tb][:, c, :],
                                        start=(c == 0), stop=(c == DC - 1)),
                               waits=wmm if c == 0 else [],
                               inc=MM if c == DC - 1 else None)
                wsl = [(MM, mmt)]
                if m_ == 0:
                    wsl.append((LD, bias_ld[ei]))
                    if sec_prev is not None:
                        wsl.append((MM, sec_prev))
                prev_sil[bank] = p.op(
                    "scalar", lambda e, m_=m_, bank=bank, hh=hh: e.activation(
                        ab[:, m_, :], ps_a[bank][:], AF.Silu,
                        bias=be1b[:, hh * 16 + m_:hh * 16 + m_ + 1]),
                    waits=wsl, inc=SC)
            sil_all = SC.n
            stt_t = None
            for d in range(DC):
                bank = d % 2
                wmm = [(SC, sil_all)] if d == 0 else []
                if prev_stt.get(bank) is not None:
                    wmm.append((VE, prev_stt[bank]))
                for k in range(16):
                    mmt = p.op("tensor",
                               lambda e, k=k, d=d, wb=wb, bank=bank:
                               e.matmul(ps_h[bank][:], we2[wb][:, k, d, :],
                                        ab[:, k, :],
                                        start=(k == 0), stop=(k == 15)),
                               waits=wmm if k == 0 else [],
                               inc=MM if k == 15 else None)
                wev = [(MM, mmt), (LG, ld_wb)]
                if d == 0:
                    if (idx - 2) >= 0 and (idx - 2) in ac_t:
                        wev.append((LG, ac_t[idx - 2]))
                if hh == 0:
                    stt_t = p.op(
                        "vector",
                        lambda e, d=d, tb=tb, bank=bank, idx=idx:
                        e.scalar_tensor_tensor(
                            accst[idx % 2][:, d, :], ps_h[bank][:],
                            be2b[:, d:d + 1], wbc[tb][:], ALU.add, ALU.mult),
                        waits=wev, inc=VE)
                else:
                    stt_t = p.op(
                        "vector",
                        lambda e, d=d, tb=tb, bank=bank, idx=idx: e.tensor_tensor(
                            accst[idx % 2][:, d, :], ps_h[bank][:], wbc[tb][:],
                            ALU.mult), waits=wev, inc=VE)
                prev_stt[bank] = stt_t
            prev_stt_tile_map[(si, tg)] = stt_t
            op_ = ALU.bypass if si == 0 else ALU.add
            ac_t[idx] = p.op("gpsimd", lambda e, sl=sl, op_=op_, idx=idx:
                             e.dma_start(T["acc"][:, :, sl], accst[idx % 2][:],
                                         accum_op=op_),
                             waits=[(VE, stt_t)], inc=LG, inc_k=16)
            mm_tile[(si, tg)] = MM.n
            sec_prev = MM.n
        mm_step[si] = MM.n
        sil_last[si] = SC.n
        if hh == 0:
            stt_last_h0[ei] = VE.n
        if si + 2 < len(steps):
            nei, nhh = steps[si + 2]
            s1, s2 = w_srcs(nei, nhh)
            p.op("gpsimd", lambda e, s1=s1, wb=wb: e.dma_start(we1[wb][:], s1),
                 waits=[(MM, mm_step[si])], inc=LG, inc_k=16)
            ld_w[si + 2] = p.op("gpsimd", lambda e, s2=s2, wb=wb: e.dma_start(
                we2[wb][:], s2), inc=LG, inc_k=16)
    p.wait("gpsimd", [(LG, LG.n)])


def emit_comb_ln(p, T, AR, PS):
    LD = p.sem("LDS")
    LG = p.sem("LDG")
    ST = p.sem("STS")
    MM = p.sem("TEN")
    SC = p.sem("SCA")
    VE = p.sem("VEC")

    o = [0]

    def al(n, a=64):
        o[0] = (o[0] + a - 1) // a * a
        r = o[0]
        o[0] += n
        return r

    xt = [AR.view(F32, al(DC * TC * 4), [128, DC, TC]) for _ in range(2)]
    s0t = [AR.view(BF16, al(DC * TC * 2), [128, DC, TC]) for _ in range(2)]
    act = [AR.view(F32, al(DC * TC * 4), [128, DC, TC]) for _ in range(2)]
    gt = AR.view(F32, al(DC * TC * 4), [128, DC, TC])
    x1t = [AR.view(F32, al(DC * TC * 4), [128, DC, TC]) for _ in range(2)]
    xnt = [AR.view(BF16, al(DC * TC * 2), [128, DC, TC]) for _ in range(2)]
    srow = AR.view(F32, al(6 * TC * 4), [1, 6, TC])
    ones8 = AR.view(F32, al(128 * 8 * 4 // 128 * 128), [128, 8])
    ones_row = AR.view(F32, al(512), [1, 128])
    assert o[0] <= AR.nbytes

    ps_sum = PS[0][0:8, :]
    ps_sq = PS[1][0:8, :]
    ps_nm = PS[2]
    ps_rs = PS[3]

    p.op("vector", lambda e: e.memset(ones8[:], 1.0), inc=VE)
    p.op("vector", lambda e: e.memset(ones_row[:], 1.0), inc=VE)
    ones_ready = VE.n

    done_v = {}
    done_st = {}
    ln_done = {}
    ln_store = {}
    prev_t1 = prev_t2 = prev_n2 = None
    for it in range(NT):
        bi = it % 2
        sl = slice(it * TC, (it + 1) * TC)
        w0 = [(VE, done_v[it - 2])] if it >= 2 else []
        ld1 = p.op("sync", lambda e, bi=bi, sl=sl: e.dma_start(
            xt[bi][:], T["xT"][:, :, sl]), waits=w0, inc=LD, inc_k=16)
        ld2 = p.op("sync", lambda e, bi=bi, sl=sl: e.dma_start(
            s0t[bi][:], T["s0"][:, :, sl]), inc=LD, inc_k=16)
        ld3 = p.op("sync", lambda e, bi=bi, sl=sl: e.dma_start(
            act[bi][:], T["acc"][:, :, sl]), inc=LD, inc_k=16)
        wsi = [(LD, ld3)]
        if prev_n2 is not None:
            wsi.append((VE, prev_n2))
        si_t = p.op("scalar", lambda e, bi=bi: e.activation(
            gt.rearrange("p a b -> p (a b)"),
            act[bi].rearrange("p a b -> p (a b)"), AF.Silu),
            waits=wsi, inc=SC)
        v1 = p.op("vector", lambda e, bi=bi: e.tensor_tensor(
            gt[:], s0t[bi][:], gt[:], ALU.mult),
            waits=[(SC, si_t), (LD, ld2)], inc=VE)
        wv2 = [(VE, v1), (LD, ld1)]
        if it >= 2:
            wv2.append((ST, done_st[it - 2]))
            wv2.append((MM, ln_done[it - 2]))
        v2 = p.op("vector", lambda e, bi=bi: e.tensor_tensor(
            x1t[bi][:], xt[bi][:], gt[:], ALU.add), waits=wv2, inc=VE)
        done_v[it] = v2
        done_st[it] = p.op("sync", lambda e, bi=bi, sl=sl: e.dma_start(
            T["x1"][:, :, sl], x1t[bi][:]), waits=[(VE, v2)], inc=ST, inc_k=16)
        wsq = [(VE, v2)]
        if prev_n2 is not None:
            wsq.append((VE, prev_n2))
        sq_t = p.op("vector", lambda e, bi=bi: e.tensor_tensor(
            gt[:], x1t[bi][:], x1t[bi][:], ALU.mult), waits=wsq, inc=VE)
        for c in range(DC):
            w1 = [(VE, v2), (VE, ones_ready)]
            if c == 0 and prev_t1 is not None:
                w1.append((VE, prev_t1))
            p.op("tensor", lambda e, bi=bi, c=c: e.matmul(
                ps_sum[:], ones8[:], x1t[bi][:, c, :],
                start=(c == 0), stop=(c == DC - 1)),
                waits=w1 if c == 0 else [])
        for c in range(DC):
            w1 = [(VE, sq_t)]
            if c == 0 and prev_t2 is not None:
                w1.append((VE, prev_t2))
            mmt = p.op("tensor", lambda e, c=c: e.matmul(
                ps_sq[:], ones8[:], gt[:, c, :],
                start=(c == 0), stop=(c == DC - 1)),
                waits=w1 if c == 0 else [],
                inc=MM if c == DC - 1 else None)
        mm_stats = mmt

        def rc(call, waits=()):
            return p.op("vector", call, waits=waits, inc=VE)
        t1 = rc(lambda e: e.tensor_scalar_mul(srow[:, 0, :], ps_sum[0:1, :],
                                              1.0 / D), waits=[(MM, mm_stats)])
        t2 = rc(lambda e: e.tensor_scalar(srow[:, 1, :], ps_sq[0:1, :], 1.0 / D,
                                          None, ALU.mult), waits=[(VE, t1)])
        t3 = rc(lambda e: e.tensor_tensor(srow[:, 4, :], srow[:, 0, :],
                                          srow[:, 0, :], ALU.mult),
                waits=[(VE, t2)])
        t4 = rc(lambda e: e.tensor_tensor(srow[:, 1, :], srow[:, 1, :],
                                          srow[:, 4, :], ALU.subtract),
                waits=[(VE, t3)])
        t4b = rc(lambda e: e.tensor_scalar_add(srow[:, 1, :], srow[:, 1, :],
                                               1e-5), waits=[(VE, t4)])
        t5 = p.op("scalar", lambda e: e.activation(
            srow[:, 2, :], srow[:, 1, :], AF.Sqrt), waits=[(VE, t4b)], inc=SC)
        t6 = rc(lambda e: e.reciprocal(srow[:, 2, :], srow[:, 2, :]),
                waits=[(SC, t5)])
        t7 = rc(lambda e: e.tensor_scalar_mul(srow[:, 3, :], srow[:, 0, :],
                                              -1.0), waits=[(VE, t6)])
        wbc = [(VE, t7)]
        if prev_n2 is not None:
            wbc.append((VE, prev_n2))
        p.op("tensor", lambda e: e.matmul(
            ps_nm[:], ones_row[:], srow[:, 3, :], start=True, stop=True),
            waits=wbc)
        bc_t = p.op("tensor", lambda e: e.matmul(
            ps_rs[:], ones_row[:], srow[:, 2, :], start=True, stop=True),
            inc=MM)
        wst = [(MM, bc_t)]
        if it >= 2:
            wst.append((LG, ln_store[it - 2]))
        n1 = p.op("vector", lambda e, bi=bi: e.tensor_tensor(
            gt[:], x1t[bi][:], ps_nm[:, None, :].broadcast_to([128, DC, TC]),
            ALU.add), waits=wst, inc=VE)
        n2 = p.op("vector", lambda e, bi=bi: e.tensor_tensor(
            xnt[bi][:], gt[:], ps_rs[:, None, :].broadcast_to([128, DC, TC]),
            ALU.mult), waits=[(VE, n1)], inc=VE)
        ln_done[it] = MM.n
        prev_t1, prev_t2, prev_n2 = t1, t2, n2
        ln_store[it] = p.op("gpsimd", lambda e, bi=bi, sl=sl: e.dma_start(
            T["xn1"][:, :, sl], xnt[bi][:]), waits=[(VE, n2)], inc=LG, inc_k=16)
    p.wait("sync", [(ST, ST.n)])
    p.wait("gpsimd", [(LG, LG.n)])


def emit_attn1(p, T, AR, SM, PS, PSD):
    nc = p.nc
    LD = p.sem("LDS")
    LG = p.sem("LDG")
    ST = p.sem("STS")
    MM = p.sem("TEN")
    SC = p.sem("SCA")
    VE = p.sem("VEC")

    o = [0]

    def al(n, a=64):
        o[0] = (o[0] + a - 1) // a * a
        r = o[0]
        o[0] += n
        return r

    Call = AR.view(BF16, al(16 * 128 * 2), [128, 16, 128])
    veffM = AR.view(BF16, al(16 * 64 * 2), [128, 16, 64])
    Gb = AR.view(BF16, al(128 * 2), [128, 128])
    Eh = AR.view(BF16, al(128 * 2), [64, 128])
    ckvb = AR.view(BF16, al(S * 2), [128, S])
    cqb = AR.view(BF16, al(S * 2), [128, S])
    qlb = [AR.view(BF16, al(S * 2), [128, S]) for _ in range(2)]
    wvh = [AR.view(BF16, al(32 * 65 * 2), [128, 32, 65]) for _ in range(2)]
    expT = AR.view(BF16, al(4 * TC * 2), [128, 4, TC])
    msk = SM.view(BF16, 0, [128, 4, TC])
    nrow = [AR.view(BF16, al(TC * 2), [128, TC]) for _ in range(2)]
    bcb = [AR.view(F32, al(TC * 4), [64, TC]) for _ in range(2)]
    onesP = AR.view(BF16, al(256), [128, 128])
    shift = AR.view(F32, al(64), [128, 1])
    wdqT_b = AR.view(BF16, al(DC * 128 * 2), [128, DC, 128])
    wdkvT_b = AR.view(BF16, al(DC * 128 * 2), [128, DC, 128])
    wdq_b = AR.view(BF16, al(D * 2), [128, D])
    wuq_b = AR.view(BF16, al(DC * 128 * 2), [128, DC, 128])
    wuqr_b = AR.view(BF16, al(16 * 128 * 2), [64, 16 * 128])
    wuk_b = AR.view(BF16, al(DC * 128 * 2), [128, DC, 128])
    wuv_b = AR.view(BF16, al(DC * 128 * 2), [128, DC, 128])
    woT_b = AR.view(BF16, al(DC * D * 2), [128, DC, D])
    xn1_off = al(DC * S * 2)
    xn1b = AR.view(BF16, xn1_off, [128, DC, S])
    s1st = AR.view(BF16, xn1_off, [128, DC, S])
    assert o[0] <= AR.nbytes, o[0]

    ps_qk = [PS[0], PS[1]]
    ps_pv = [PS[2], PS[3]]
    ps_bc = [PS[4], PS[6]]
    ps_wv = PS[5]
    ps_g = PS[7]

    lwt = None
    for dst, srcn, rr in (
            (wdqT_b, "wdqT", "(c p) m -> p c m"),
            (wdkvT_b, "wdkvT", "(c p) m -> p c m"),
            (wdq_b, "wdq", None), (wuq_b, "wuq", "(c p) m -> p c m"),
            (wuqr_b, "wuqr", "d h m -> d (h m)"),
            (wuk_b, "wuk", "(c p) m -> p c m"),
            (wuv_b, "wuv", "(c p) m -> p c m"),
            (woT_b, "woT", "(c p) m -> p c m")):
        srcap = T[srcn][:] if rr is None else T[srcn].rearrange(rr, **({"p": 128} if "p" in rr else {}))
        lwt = p.op("gpsimd", lambda e, d=dst, s=srcap: e.dma_start(d[:], s),
                   inc=LG, inc_k=16)
    ldm = p.op("gpsimd", lambda e: e.dma_start(
        msk[:], T["masks"].rearrange("r p t -> p r t")), inc=LG, inc_k=16)
    p.op("vector", lambda e: e.memset(onesP[:], 0.0), inc=VE)
    p.op("vector", lambda e: e.memset(onesP[0:1, :], 1.0), inc=VE)
    p.op("vector", lambda e: e.memset(nrow[0][:], 0.0), inc=VE)
    p.op("vector", lambda e: e.memset(nrow[1][:], 0.0), inc=VE)
    p.op("vector", lambda e: e.memset(shift[:], -EXP_SHIFT), inc=VE)
    p.op("vector", lambda e: e.memset(wvh[0][:, :, 64:65], 1.0), inc=VE)
    p.op("vector", lambda e: e.memset(wvh[1][:, :, 64:65], 1.0), inc=VE)
    const_v = VE.n
    ldx = p.op("sync", lambda e: e.dma_start(xn1b[:], T["xn1"][:]), inc=LD, inc_k=16)

    qk_guard = {0: None, 1: None}
    for wsl, dstp in ((wdqT_b, cqb), (wdkvT_b, ckvb)):
        for n in range(NT):
            bank = n % 2
            psb = ps_qk[bank]
            w0 = [(LD, ldx), (LG, lwt)]
            if qk_guard[bank] is not None:
                w0.append(qk_guard[bank])
            for c in range(DC):
                mmt = p.op("tensor", lambda e, c=c, n=n, psb=psb, wsl=wsl: e.matmul(
                    psb[:], wsl[:, c, :], xn1b[:, c, n * TC:(n + 1) * TC],
                    start=(c == 0), stop=(c == DC - 1)),
                    waits=w0 if c == 0 else [], inc=MM if c == DC - 1 else None)
            ev = p.op("scalar", lambda e, n=n, psb=psb, dstp=dstp: e.copy(
                dstp[:, n * TC:(n + 1) * TC], psb[:]), waits=[(MM, mmt)], inc=SC)
            qk_guard[bank] = (SC, ev)
    cqckv_done = qk_guard[1]

    for c in range(DC):
        mmt = p.op("tensor", lambda e, c=c: e.matmul(
            ps_g[:, 0:128], wuq_b[:, c, :], wuk_b[:, c, :],
            start=(c == 0), stop=(c == DC - 1)),
            waits=[(LG, lwt)] if c == 0 else [], inc=MM if c == DC - 1 else None)
    g_ev = p.op("vector", lambda e: e.tensor_copy(Gb[:], ps_g[:, 0:128]),
                waits=[(MM, mmt)], inc=VE)

    evC_guard = {0: None, 1: None}
    mmC_prev = None
    for h in range(H):
        hb = h % 2
        wE = [(VE, g_ev)]
        if mmC_prev is not None:
            wE.append((MM, mmC_prev))
        mmE = p.op("tensor", lambda e, h=h, hb=hb: e.matmul(
            ps_pv[hb][0:64, 0:128], wdq_b[:, h * 64:(h + 1) * 64], Gb[:],
            start=True, stop=True), waits=wE, inc=MM)
        evE = p.op("vector", lambda e, hb=hb: e.tensor_copy(
            Eh[:], ps_pv[hb][0:64, 0:128]), waits=[(MM, mmE)], inc=VE)
        wC = [(VE, evE)]
        if evC_guard[hb] is not None:
            wC.append((VE, evC_guard[hb]))
        mmC_prev = p.op("tensor", lambda e, h=h, hb=hb: e.matmul(
            ps_bc[hb][:, 0:128],
            wuqr_b[:, h * 128:(h + 1) * 128], Eh[:],
            start=True, stop=True), waits=wC, inc=MM)
        evC_guard[hb] = p.op("vector", lambda e, h=h, hb=hb: e.tensor_copy(
            Call[:, h, :], ps_bc[hb][:, 0:128]), waits=[(MM, mmC_prev)], inc=VE)

    for n2 in range(2):
        psb = ps_qk[n2]
        w0 = [qk_guard[n2]] if qk_guard[n2] is not None else []
        for c in range(DC):
            mmt = p.op("tensor", lambda e, c=c, n2=n2, psb=psb: e.matmul(
                psb[:], wuv_b[:, c, :], woT_b[:, c, n2 * TC:(n2 + 1) * TC],
                start=(c == 0), stop=(c == DC - 1)),
                waits=w0 if c == 0 else [], inc=MM if c == DC - 1 else None)
        ve = p.op("vector", lambda e, n2=n2, psb=psb: e.tensor_copy(
            veffM[:, n2 * 8:(n2 + 1) * 8, :],
            psb[:].rearrange("p (a b) -> p a b", b=64)), waits=[(MM, mmt)], inc=VE)
        qk_guard[n2] = (VE, ve)
    veff_done = qk_guard[1][1]

    qk4 = [[PS[0], PS[1]], [PS[6], PS[7]]]
    qkD = [PSD[0], PSD[3]]
    ps_bc1 = PS[4]
    expf = expT.rearrange("p a b -> p (a b)")
    qkd_guard = {0: qk_guard[0] if qk_guard[0][1] > qk_guard[1][1] else qk_guard[1],
                 1: ("VE_max", 0)}
    _evc = [v for v in evC_guard.values() if v is not None]
    qkd_guard[1] = (VE, max([g_ev] + _evc))
    prev_pv = {}
    pv_bank_free = {0: None, 1: None}
    nrow_free = {0: None, 1: None}
    bcb_free = {0: None, 1: None}
    bc_bank_free = None
    ql_read_done = {}
    wv_read_done = {}
    ql_ready = {}
    wv_ready = {}
    wv_evac_last = None
    qk_pair_done = {0: None, 1: None}
    pend = [None]

    def flush_norm():
        qv2, base2, chunk2, qsl2, rc_cnt, do_store = pend[0]
        pend[0] = None
        wbc = [(VE, rc_cnt)]
        if bc_bank_free[0] is not None:
            wbc.append((VE, bc_bank_free[0]))
        bc_t = p.op("tensor", lambda e, qv2=qv2: e.matmul(
            ps_bc1[:, :], onesP[:, :], nrow[qv2][:],
            start=True, stop=True), waits=wbc, inc=MM)
        nrow_free[qv2] = bc_t
        wcp = [(MM, bc_t)]
        if bcb_free[qv2] is not None:
            wcp.append((VE, bcb_free[qv2]))
        cp_t = p.op("vector", lambda e, qv2=qv2: e.tensor_copy(
            bcb[qv2][:], ps_bc1[0:64, :]), waits=wcp, inc=VE)
        bc_bank_free[0] = cp_t
        rcp_t = p.op("vector", lambda e, qv2=qv2: e.reciprocal(
            bcb[qv2][:], bcb[qv2][:]), waits=[(VE, cp_t)], inc=VE)
        tt_t = p.op("vector",
                    lambda e, base2=base2, chunk2=chunk2, qsl2=qsl2, qv2=qv2:
                    e.tensor_tensor(s1st[base2:base2 + 64, chunk2, qsl2],
                                    ps_pv[qv2][0:64, :], bcb[qv2][:],
                                    ALU.mult),
                    waits=[(VE, rcp_t)], inc=VE)
        pv_bank_free[qv2] = tt_t
        bcb_free[qv2] = tt_t
        if do_store:
            p.op("sync", lambda e, chunk2=chunk2: e.dma_start(
                T["s1"][:, chunk2, :], s1st[:, chunk2, :]),
                waits=[(VE, tt_t)], inc=ST, inc_k=16)

    bc_bank_free = {0: None}
    for h in range(H):
        hb2 = h % 2
        base = (h % 2) * 64
        chunk = h // 2
        half_free = {0: None, 1: None}
        for n in range(NT):
            bank = n % 2
            psb = qk4[0][bank]
            w0 = []
            if n < 2 and qkd_guard[0] is not None:
                w0.append(qkd_guard[0])
            if half_free[bank] is not None:
                w0.append((VE, half_free[bank]))
            mmt = p.op("tensor", lambda e, h=h, n=n, psb=psb: e.matmul(
                psb[:], Call[:, h, :], cqb[:, n * TC:(n + 1) * TC],
                start=True, stop=True), waits=w0, inc=MM)
            wev = [(MM, mmt)]
            if h >= 2:
                wev.append((MM, ql_read_done[h - 2]))
            ev = p.op("vector", lambda e, n=n, psb=psb, hb2=hb2: e.tensor_copy(
                qlb[hb2][:, n * TC:(n + 1) * TC], psb[:]), waits=wev, inc=VE)
            half_free[bank] = ev
        ql_ready[h] = ev
        qkd_guard[0] = (VE, ev)
        for grp in range(4):
            w0 = []
            if wv_evac_last is not None:
                w0.append((VE, wv_evac_last))
            for k8 in range(8):
                ktc = grp * 8 + k8
                mmt = p.op("tensor", lambda e, h=h, ktc=ktc, k8=k8: e.matmul(
                    ps_wv[:, k8 * 64:(k8 + 1) * 64],
                    ckvb[:, ktc * 128:(ktc + 1) * 128], veffM[:, h, :],
                    start=True, stop=True),
                    waits=w0 if k8 == 0 else [], inc=MM if k8 == 7 else None)
            wev = [(MM, mmt)]
            if h >= 2:
                wev.append((MM, wv_read_done[h - 2]))
            wv_evac_last = p.op("vector", lambda e, grp=grp, hb2=hb2: e.tensor_copy(
                wvh[hb2][:, grp * 8:(grp + 1) * 8, 0:64],
                ps_wv[:].rearrange("p (a b) -> p a b", b=64)),
                waits=wev, inc=VE)
        wv_ready[h] = wv_evac_last

        def emit_qk_pair(j, qsl, extra):
            m = j % 2
            wqk = list(extra)
            if qkd_guard[m] is not None:
                wqk.append(qkd_guard[m])
            b0 = 2 * j
            p.op("tensor", lambda e, b0=b0, qsl=qsl, m=m, hb2=hb2:
                 e.matmul(qk4[m][0][:], ckvb[:, b0 * 128:(b0 + 1) * 128],
                          qlb[hb2][:, qsl], start=True, stop=True),
                 waits=wqk)
            qk_pair_done[m] = p.op(
                "tensor", lambda e, b0=b0, qsl=qsl, m=m, hb2=hb2:
                e.matmul(qk4[m][1][:], ckvb[:, (b0 + 1) * 128:(b0 + 2) * 128],
                         qlb[hb2][:, qsl], start=True, stop=True),
                inc=MM)

        for qc in range(NT):
            nkt = 4 * (qc + 1)
            npair = nkt // 2
            qv = qc % 2
            qsl = slice(qc * TC, (qc + 1) * TC)
            extra0 = []
            if qc == 0:
                extra0.append((VE, ql_ready[h]))
                if h == 0:
                    extra0.append((LG, ldm))
                    extra0.append((VE, const_v))
            emit_qk_pair(0, qsl, extra0)
            emit_qk_pair(1, qsl, [])
            for j in range(npair):
                m = j % 2
                s = (2 * j) % 4
                if j == 1 and pend[0] is not None:
                    flush_norm()
                wex = [(MM, qk_pair_done[m])]
                if prev_pv.get(s + 1) is not None:
                    wex.append((MM, prev_pv[s + 1]))
                ex_t = p.op("scalar", lambda e, s=s, m=m: e.activation(
                    expf[:, s * TC:(s + 2) * TC], qkD[m][:], AF.Exp,
                    bias=shift[:, 0:1], scale=0.125), waits=wex, inc=SC)
                qkd_guard[m] = (SC, ex_t)
                mk = {}
                for t in range(2):
                    ktc = 2 * j + t
                    r = ktc - 4 * qc
                    if 0 <= r <= 3:
                        mk[t] = p.op("vector",
                                     lambda e, slot=s + t, r=r: e.tensor_tensor(
                                         expT[:, slot, :], expT[:, slot, :],
                                         msk[:, r, :], ALU.mult),
                                     waits=[(SC, ex_t)], inc=VE)
                pv_t = None
                for t in range(2):
                    ktc = 2 * j + t
                    wpv = []
                    if t == 0:
                        wpv.append((VE, mk[0]) if 0 in mk else (SC, ex_t))
                        if j == 0:
                            if qc == 0:
                                wpv.append((VE, wv_ready[h]))
                            if pv_bank_free[qv] is not None:
                                wpv.append((VE, pv_bank_free[qv]))
                    elif 1 in mk:
                        wpv.append((VE, mk[1]))
                    pv_t = p.op("tensor",
                                lambda e, slot=s + t, ktc=ktc, nkt=nkt, qv=qv,
                                hb2=hb2:
                                e.matmul(ps_pv[qv][0:65, :],
                                         wvh[hb2][:, ktc, 0:65],
                                         expT[:, slot, :],
                                         start=(ktc == 0), stop=(ktc == nkt - 1)),
                                waits=wpv, inc=MM if t == 1 else None)
                prev_pv[s] = prev_pv[s + 1] = pv_t
                if j + 2 < npair:
                    emit_qk_pair(j + 2, qsl, [])
            wrc = [(MM, pv_t)]
            if nrow_free[qv] is not None:
                wrc.append((MM, nrow_free[qv]))
            rc_t = p.op("vector", lambda e, qv=qv: e.tensor_copy(
                nrow[qv][0:1, :], ps_pv[qv][64:65, :]), waits=wrc, inc=VE)
            pend[0] = (qv, base, chunk, qsl, rc_t,
                       h % 2 == 1 and qc == NT - 1)
        ql_read_done[h] = MM.n
        wv_read_done[h] = MM.n
    if pend[0] is not None:
        flush_norm()
    p.wait("sync", [(ST, ST.n)])


def emit_ffn1(p, T, AR, SM, PS):
    nc = p.nc
    LD = p.sem("LDS")
    LG = p.sem("LDG")
    ST = p.sem("STS")
    MM = p.sem("TEN")
    SC = p.sem("SCA")
    VE = p.sem("VEC")
    TQ = 1024
    NTQ = S // TQ
    M1 = FF1 // 128

    o = [0]

    def al(n, a=64):
        o[0] = (o[0] + a - 1) // a * a
        r = o[0]
        o[0] += n
        return r

    hb = AR.view(BF16, al(M1 * TQ * 2), [128, M1, TQ])
    xnt = [AR.view(BF16, al(DC * TQ * 2), [128, DC, TQ]) for _ in range(2)]
    wgb = [AR.view(BF16, al(DC * 128 * 2), [128, DC, 128]) for _ in range(3)]
    wub = [AR.view(BF16, al(DC * 128 * 2), [128, DC, 128]) for _ in range(3)]
    wdb = [AR.view(BF16, al(M1 * 128 * 2), [128, M1, 128]) for _ in range(3)]
    sgb = AR.view(BF16, al(TC * 2), [128, TC])
    sub = AR.view(BF16, al(TC * 2), [128, TC])
    g1b = AR.view(F32, al(TC * 4), [128, TC])
    s1t = [AR.view(BF16, al(DC * TC * 2), [128, DC, TC]) for _ in range(2)]
    x1t = [AR.view(F32, al(DC * TC * 4), [128, DC, TC]) for _ in range(2)]
    ost = [AR.view(F32, al(DC * TC * 4), [128, DC, TC]) for _ in range(2)]
    assert o[0] <= AR.nbytes, o[0]

    wg_r = T["wgT"].rearrange("(c p) (m q) -> p c m q", p=128, q=128)
    wu_r = T["wuT"].rearrange("(c p) (m q) -> p c m q", p=128, q=128)
    wd_r = T["wdT"].rearrange("(k p) (d q) -> p k d q", p=128, q=128)

    ps_g = [PS[0], PS[1]]
    ps_u = [PS[2], PS[3]]
    ps_d = [PS[4], PS[5]]

    ld_xn = {}
    ld_wg = {}
    mm_h = {}
    mm_tq = {}
    h_done = {}
    ev_d = {}
    st_done = {}
    s1x1_ld = {}
    prev_sg = {0: None, 1: None}
    prev_su = {0: None, 1: None}
    prev_g1 = {0: None, 1: None}
    last_ht = None
    last_v2 = None

    for m in range(min(3, M1)):
        p.op("gpsimd", lambda e, m=m: e.dma_start(wgb[m % 3][:], wg_r[:, :, m, :]),
             inc=LG, inc_k=16)
        ld_wg[(0, m)] = p.op("gpsimd", lambda e, m=m: e.dma_start(
            wub[m % 3][:], wu_r[:, :, m, :]), inc=LG, inc_k=16)

    for tq in range(NTQ):
        bi = tq % 2
        sl = slice(tq * TQ, (tq + 1) * TQ)
        wxn = [(MM, mm_tq[tq - 2])] if tq >= 2 else []
        ld_xn[tq] = p.op("sync", lambda e, bi=bi, sl=sl: e.dma_start(
            xnt[bi][:], T["xn1"][:, :, sl]), waits=wxn, inc=LD, inc_k=16)
        for m in range(M1):
            wb = m % 3
            nt_tq, nt_m = (tq, m + 3) if m + 3 < M1 else (tq + 1, (m + 3) % M1)
            if (nt_tq, nt_m) not in ld_wg and nt_tq < NTQ:
                wpre = [(MM, mm_h[(tq, m)])] if (tq, m) in mm_h else []
            for n2 in range(2):
                wmm = [(LD, ld_xn[tq]),
                       (LG, ld_wg[(tq, m) if tq > 0 or m > 2 else (0, m)])]
                if prev_sg[n2] is not None:
                    wmm.append((SC, prev_sg[n2]))
                for c in range(DC):
                    gmm = p.op("tensor", lambda e, c=c, m=m, n2=n2, wb=wb, bi=bi:
                               e.matmul(ps_g[n2][:], wgb[wb][:, c, :],
                                        xnt[bi][:, c, n2 * TC:(n2 + 1) * TC],
                                        start=(c == 0), stop=(c == DC - 1)),
                               waits=wmm if c == 0 else [],
                               inc=MM if c == DC - 1 else None)
                wmu = []
                if prev_su[n2] is not None:
                    wmu.append((VE, prev_su[n2]))
                for c in range(DC):
                    umm = p.op("tensor", lambda e, c=c, m=m, n2=n2, wb=wb, bi=bi:
                               e.matmul(ps_u[n2][:], wub[wb][:, c, :],
                                        xnt[bi][:, c, n2 * TC:(n2 + 1) * TC],
                                        start=(c == 0), stop=(c == DC - 1)),
                               waits=wmu if c == 0 else [],
                               inc=MM if c == DC - 1 else None)
                wsg = [(MM, gmm)]
                if last_ht is not None:
                    wsg.append((VE, last_ht))
                sg_t = p.op("scalar", lambda e, n2=n2: e.activation(
                    sgb[:], ps_g[n2][:], AF.Silu, scale=1.0 / 127.0),
                    waits=wsg, inc=SC)
                prev_sg[n2] = sg_t
                wsu = [(MM, umm)]
                if last_ht is not None:
                    wsu.append((VE, last_ht))
                su_t = p.op("vector", lambda e, n2=n2: e.tensor_scalar_mul(
                    sub[:], ps_u[n2][:], 1.0 / 127.0), waits=wsu, inc=VE)
                prev_su[n2] = su_t
                wht = [(VE, su_t), (SC, sg_t)]
                if m == 0 and n2 == 0 and tq >= 1:
                    wht.append((MM, mm_tq[tq - 1]))
                ht = p.op("vector", lambda e, m=m, n2=n2: e.tensor_tensor(
                    hb[:, m, n2 * TC:(n2 + 1) * TC], sgb[:], sub[:], ALU.mult),
                    waits=wht, inc=VE)
                h_done[(tq, m, n2)] = ht
                last_ht = ht
            mm_h[(tq, m)] = MM.n
            if (nt_tq, nt_m) not in ld_wg and nt_tq < NTQ:
                p.op("gpsimd", lambda e, nt_tq=nt_tq, nt_m=nt_m, wb=wb: e.dma_start(
                    wgb[wb][:], wg_r[:, :, nt_m, :]),
                    waits=[(MM, mm_h[(tq, m)])], inc=LG, inc_k=16)
                ld_wg[(nt_tq, nt_m)] = p.op(
                    "gpsimd", lambda e, nt_tq=nt_tq, nt_m=nt_m, wb=wb: e.dma_start(
                        wub[wb][:], wu_r[:, :, nt_m, :]), inc=LG, inc_k=16)
        h_all = VE.n
        for t2 in range(2):
            tgi = tq * 2 + t2
            bj = tgi % 2
            sl2 = slice(tgi * TC, (tgi + 1) * TC)
            wld = []
            if tgi >= 2:
                wld.append((ST, st_done[tgi - 2]))
            s1x1_ld[tgi] = p.op("sync", lambda e, bj=bj, sl2=sl2: e.dma_start(
                s1t[bj][:], T["s1"][:, :, sl2]), waits=wld, inc=LD, inc_k=16)
            s1x1_ld[tgi] = p.op("sync", lambda e, bj=bj, sl2=sl2: e.dma_start(
                x1t[bj][:], T["x1"][:, :, sl2]), inc=LD, inc_k=16)
            for dc in range(DC):
                gi = tgi * DC + dc
                wb2 = gi % 3
                bk2 = dc % 2
                pu = ((gi - 3) // DC, (gi - 3) % DC) if gi >= 3 else None
                wpre = [(MM, ev_d[pu][0])] if pu is not None and pu in ev_d else []
                ldw = p.op("gpsimd", lambda e, dc=dc, wb2=wb2: e.dma_start(
                    wdb[wb2][:], wd_r[:, :, dc, :]), waits=wpre, inc=LG, inc_k=16)
                wmm = [(LG, ldw), (VE, h_all)]
                if prev_g1[bk2] is not None:
                    wmm.append((SC, prev_g1[bk2]))
                for k in range(M1):
                    dmm = p.op("tensor",
                               lambda e, k=k, dc=dc, wb2=wb2, bk2=bk2, t2=t2:
                               e.matmul(ps_d[bk2][:], wdb[wb2][:, k, :],
                                        hb[:, k, t2 * TC:(t2 + 1) * TC],
                                        start=(k == 0), stop=(k == M1 - 1)),
                               waits=wmm if k == 0 else [],
                               inc=MM if k == M1 - 1 else None)
                wg1 = [(MM, dmm)]
                if last_v2 is not None:
                    wg1.append((VE, last_v2))
                g1_t = p.op("scalar", lambda e, bk2=bk2: e.activation(
                    g1b[:], ps_d[bk2][:], AF.Silu, scale=1.0 / 127.0),
                    waits=wg1, inc=SC)
                prev_g1[bk2] = g1_t
                v1 = p.op("vector", lambda e, dc=dc, bj=bj: e.tensor_tensor(
                    g1b[:], s1t[bj][:, dc, :], g1b[:], ALU.mult),
                    waits=[(SC, g1_t), (LD, s1x1_ld[tgi])], inc=VE)
                wv2 = [(VE, v1)]
                if tgi >= 2 and dc == 0:
                    wv2.append((ST, st_done[tgi - 2]))
                v2 = p.op("vector", lambda e, dc=dc, bj=bj: e.tensor_tensor(
                    ost[bj][:, dc, :], x1t[bj][:, dc, :], g1b[:], ALU.add),
                    waits=wv2, inc=VE)
                last_v2 = v2
                ev_d[(tgi, dc)] = (MM.n, v2)
            st_done[tgi] = p.op("sync", lambda e, bj=bj, sl2=sl2: e.dma_start(
                T["out"][:, :, sl2], ost[bj][:]),
                waits=[(VE, ev_d[(tgi, DC - 1)][1])], inc=ST, inc_k=16)
        mm_tq[tq] = MM.n
    p.wait("sync", [(ST, ST.n)])


def build_program(phase_names, want_outputs=()):
    nc = bass.Bass("TRN2", target_bir_lowering=False)
    phase_names = [ph for ph in PHASE_ORDER if ph in phase_names]
    reads = set()
    writes = set()
    for ph in phase_names:
        r, w = PHASES[ph]
        reads |= set(r)
        writes |= set(w)

    T = {}
    T["xT"] = nc.dram_tensor("xT", [128, DC, S], F32, kind="ExternalInput")
    if "ffn1" in phase_names:
        T["out"] = nc.dram_tensor("out", [128, DC, S], F32, kind="ExternalOutput")
    for name, (shape, dt_) in INTER.items():
        if name in writes:
            kind = "ExternalOutput" if (name in want_outputs) else "Internal"
            T[name] = nc.dram_tensor(name, shape, dt_, kind=kind)
        elif name in reads:
            T[name] = nc.dram_tensor(name, shape, dt_, kind="ExternalInput")
    for name, (shape, dt_) in WEIGHT_SPECS.items():
        T[name] = nc.dram_tensor(name, shape, dt_, kind="ExternalInput")

    AR = Arena(nc, "AR", 199 * 1024 + 512)
    SM = Arena(nc, "SM", 8 * 1024)
    PSD = [nc.alloc_psum_tensor(f"psd{i}", [128, 2 * TC], F32) for i in range(4)]
    PS = [PSD[i // 2][:, (i % 2) * TC:(i % 2 + 1) * TC] for i in range(8)]
    p = Prog(nc)

    first = True
    for ph in phase_names:
        if not first:
            p.barrier()
        first = False
        if ph == "ln0":
            emit_ln(p, T, AR, PS, "xT", "xn0", f32_dst="xn0f")

        elif ph == "attn0":
            emit_attn0(p, T, AR, SM, PS, PSD)
        elif ph == "router":
            emit_router(p, T, AR, SM, PS)
        elif ph == "moe":
            emit_moe(p, T, AR, SM, PS)
        elif ph == "combln":
            emit_comb_ln(p, T, AR, PS)
        elif ph == "attn1":
            emit_attn1(p, T, AR, SM, PS, PSD)
        elif ph == "ffn1":
            emit_ffn1(p, T, AR, SM, PS)
        else:
            raise ValueError(ph)
    p.realize()
    return nc


def _in_map_for_core(inp, w, consts, b, needed):
    m = {}
    full = dict(w)
    full.update(consts)
    for name in needed:
        if name == "xT":
            m["xT"] = _host_x(inp, b)
        elif name in full:
            m[name] = full[name]
    return m


def kernel(**inputs):
    w = _host_weights(inputs)
    consts = _host_constants()
    nc = build_program(PHASE_ORDER)
    needed = ["xT"] + list(WEIGHT_SPECS.keys())
    in_maps = [_in_map_for_core(inputs, w, consts, b, needed) for b in range(NCORE)]
    res = run_bass_kernel_spmd(nc, in_maps, core_ids=list(range(NCORE)))
    out = np.empty((NCORE, S, D), dtype=np.float32)
    for b in range(NCORE):
        o = res.results[b]["out"]
        out[b] = o.transpose(1, 0, 2).reshape(D, S).T
    return out
